# version 18
# baseline (speedup 1.0000x reference)
import numpy as np
import concourse.bass as bass
import concourse.mybir as mybir
from concourse.bass_utils import run_bass_kernel_spmd

F32 = mybir.dt.float32
BF16 = mybir.dt.bfloat16
AX = mybir.AxisListType
AF = mybir.ActivationFunctionType
ALU = mybir.AluOpType

Bsz, Lseq, D = 4, 2048, 2048
NCORE = 8
H, P, Gg, S = 32, 64, 8, 32
NPC = 37
EPS = 1e-5
TM = 1024
MTT = 256
PTT = 128
FE = 1408
PAGE = 64
ET, EV, ES = "tensor", "vector", "scalar"


class _Buf:
    __slots__ = ("writer", "readers", "excl")
    def __init__(self, excl=False):
        self.writer, self.readers, self.excl = None, [], excl


class _Op:
    __slots__ = ("eng", "build", "deps", "needed", "sem", "step", "val")


class Prog:
    def __init__(self):
        self.ops = []

    @staticmethod
    def buf(excl=False):
        return _Buf(excl)

    def emit(self, eng, build, r=(), w=(), dma=None, ndma=1):
        op = _Op()
        op.eng, op.build, op.needed = eng, build, False
        op.sem = dma
        op.step = 16 * ndma if dma else 1
        deps = set()
        for b in r:
            if b.excl:
                if b.writer is not None:
                    deps.add(b.writer)
                deps.update(b.readers)
                b.readers, b.writer = [], op
            else:
                if b.writer is not None:
                    deps.add(b.writer)
                b.readers.append(op)
        for b in w:
            if b.writer is not None:
                deps.add(b.writer)
            deps.update(b.readers)
            b.writer, b.readers = op, []
        deps.discard(op)
        op.deps = deps
        self.ops.append(op)
        return op

    def finalize(self):
        for op in self.ops:
            for d in op.deps:
                d.needed = True
        counts = {}
        for op in self.ops:
            key = op.sem if op.sem else ("e", op.eng)
            if op.needed or op.sem:
                counts[key] = counts.get(key, 0) + op.step
                op.val = counts[key]
            else:
                op.val = None
        return counts

    def stream(self, e, eng, sems, marks):
        wm = marks.setdefault(eng, {})
        for op in self.ops:
            if op.eng != eng:
                continue
            for d in sorted(op.deps, key=lambda o: (str(o.sem), o.val or 0)):
                if d.val is None:
                    continue
                key = d.sem if d.sem else ("e", d.eng)
                if wm.get(key, 0) >= d.val:
                    continue
                e.wait_ge(sems[key], d.val)
                wm[key] = d.val
            if op.sem:
                op.build(e, sems[op.sem])
            else:
                instr = op.build(e)
                if op.needed:
                    instr.then_inc(sems[("e", eng)], 1)


def bcast(col_ap, n):
    return bass.AP(col_ap.tensor, col_ap.offset, [list(col_ap.ap[0]), [0, n]])


def build_program(nc):
    pg = Prog()

    di = lambda n, s: nc.dram_tensor(n, s, F32, kind="ExternalInput").ap()
    XM = di("XM", [D, TM])
    XP = di("XP", [D, TM])
    MSK = di("MSK", [128, 1])
    WINB = di("WINB", [NPC, 128, D])
    F1B = di("F1B", [8, 11, 128, D])
    F2B = di("F2B", [8, 11, 128, D])
    PJB = di("PJB", [8, 16, 128, FE])
    SF1B = di("SF1B", [44, 128, D])
    SF2B = di("SF2B", [44, 128, D])
    SPB = di("SPB", [16, 128, 4 * FE])
    WOB = di("WOB", [16, 128, D])
    CONST_D = {}
    for nm, shp in [("GWC", [128, 128]), ("SGC", [128, 16]), ("BINC", [128, NPC]),
                    ("MUPC", [128, NPC]), ("CWC", [128, 80]), ("CBC", [128, 20]),
                    ("NWC", [128, 16]), ("BOC", [128, 16]), ("ALC", [32, 1]),
                    ("DTBC", [32, 1]), ("DROW", [1, 32]), ("IDN", [128, 128]),
                    ("MASKC", [128, 128]), ("CS8", [128, 64]), ("CS2", [128, 4]),
                    ("ONES1", [1, 512]), ("Z32", [32, 256])]:
        CONST_D[nm] = di(nm, shp)
    XNH = nc.dram_tensor("XNH", [D, TM], F32, kind="Internal").ap()
    XNH_H = nc.dram_tensor("XNH_H", [D, TM], BF16, kind="Internal").ap()
    XNH_L = nc.dram_tensor("XNH_L", [D, TM], BF16, kind="Internal").ap()
    OUT = nc.dram_tensor("OUT", [D, TM], F32, kind="ExternalOutput").ap()

    sb = lambda n, s: nc.alloc_sbuf_tensor(n, s, F32).ap()
    ACC = sb("ACC", [128, 16 * 1024])
    acc_b = [pg.buf() for _ in range(16)]
    WSLOT = sb("WSLOT", [128, 2 * 2048])
    ws_b = [pg.buf() for _ in range(4)]
    SALL = sb("SALL", [32, H * 64])
    s_b = [pg.buf() for _ in range(H)]
    TAIL = sb("TAIL", [128, 64])
    tail_b = [pg.buf() for _ in range(20)]
    C = {}
    for nm, ap in CONST_D.items():
        C[nm] = sb(nm + "_s", list(ap.tensor.shape))
    MSKC = sb("MSK_s", [128, 1])
    cN = pg.buf()
    NEGA = sb("NEGA", [32, 1]); nega_b = pg.buf()
    DALL = sb("DALL", [128, 32]); dall_b = pg.buf()

    ARENA_W = 29056
    AR = sb("AR", [128, ARENA_W])
    ar_pages = {}

    class Region:
        __slots__ = ("off", "width")
        def __init__(self, off, width):
            self.off, self.width = off, width
        def a(self, f0, f1, p0=0, p1=128):
            return AR[p0:p1, self.off + f0:self.off + f1]
        def col(self, f, p0=0, p1=128):
            return AR[p0:p1, self.off + f:self.off + f + 1]
        def b(self, f0=0, f1=None):
            if f1 is None:
                f1 = self.width
            lo, hi = (self.off + f0) // PAGE, (self.off + f1 - 1) // PAGE
            out = []
            for i in range(lo, hi + 1):
                if i not in ar_pages:
                    ar_pages[i] = pg.buf()
                out.append(ar_pages[i])
            return out

    ar_off = [0]
    def alloc(width):
        w = ((width + PAGE - 1) // PAGE) * PAGE
        r = Region(ar_off[0], w)
        ar_off[0] += w
        assert ar_off[0] <= ARENA_W, f"arena overflow {ar_off[0]}"
        return r
    mark = lambda: ar_off[0]
    def reset(m):
        ar_off[0] = m

    PS = [nc.alloc_psum_tensor(f"PS{i}", [128, 512], F32).ap() for i in range(8)]
    ps_b = [pg.buf(excl=True) for _ in range(8)]
    mmrot = [0]
    mmbanks = [[0, 1]]
    def mmbank():
        lst = mmbanks[0]
        b = lst[mmrot[0] % len(lst)]
        mmrot[0] += 1
        return b
    trrot = [0]
    def trbank():
        trrot[0] ^= 1
        return 3 + trrot[0]
    BK_G = 5
    BK_S = [5, 2]
    BK_LY = [6, 7]

    def ld_consts(e, sem):
        for nm in CONST_D:
            e.dma_start(out=C[nm], in_=CONST_D[nm]).then_inc(sem, 16)
        e.dma_start(out=MSKC, in_=MSK).then_inc(sem, 16)
    pg.emit("sync", ld_consts, w=[cN], dma="c", ndma=len(CONST_D) + 1)

    pg.emit(ES, lambda e: e.activation(NEGA, C["ALC"], AF.Exp), r=[cN], w=[nega_b])
    pg.emit(EV, lambda e: e.tensor_scalar_mul(NEGA, NEGA, -1.0), r=[nega_b], w=[nega_b])
    bk0 = mmbank()
    pg.emit(ET, lambda e, bk0=bk0: e.matmul(PS[bk0][:, 0:32], C["ONES1"][0:1, 0:128],
                                            C["DROW"], start=True, stop=True),
            r=[cN], w=[ps_b[bk0]])
    pg.emit(EV, lambda e, bk0=bk0: e.tensor_copy(DALL, PS[bk0][:, 0:32]),
            r=[ps_b[bk0]], w=[dall_b])
    pg.emit(EV, lambda e: e.memset(SALL, 0.0), w=s_b)
    pg.emit(EV, lambda e: e.memset(TAIL, 0.0), w=tail_b)

    wcount = [0]
    def wload(src_ap, width):
        f = wcount[0] % 2
        wcount[0] += 1
        pg.emit("sync", lambda e, sem, src=src_ap, f=f, width=width:
                e.dma_start(out=WSLOT[:, f * 2048:f * 2048 + width], in_=src)
                .then_inc(sem, 16), w=[ws_b[2 * f], ws_b[2 * f + 1]], dma=f"w{f}", ndma=1)
        return f

    def ln_phase(T, src_ap, src_bufs, sink, nhalf):
        statx = alloc(T); stats = alloc(T)
        mrow = alloc(T); rrow = alloc(T)
        tr1 = alloc(T); tr2 = alloc(T)
        mrep = alloc(T); rrep = alloc(T)
        sqs = alloc(2 * T)
        NW = T // nhalf
        for ci in range(16):
            sl = (ci % 2) * T
            pg.emit(ES, lambda e, ci=ci, sl=sl:
                    e.activation(sqs.a(sl, sl + T), src_ap(ci), AF.Square),
                    r=[src_bufs[ci]] if not isinstance(src_bufs[ci], list) else src_bufs[ci],
                    w=sqs.b(sl, sl + T))
            for n in range(nhalf):
                rb = [src_bufs[ci]] if not isinstance(src_bufs[ci], list) else list(src_bufs[ci])
                for acc_r, src_fn in ((statx, lambda ci, n: src_ap(ci)[:, n * NW:(n + 1) * NW]),
                                      (stats, lambda ci, n, sl=sl: sqs.a(sl, sl + T)[:, n * NW:(n + 1) * NW])):
                    bk = mmbank()
                    pg.emit(ET, lambda e, ci=ci, n=n, bk=bk, src_fn=src_fn:
                            e.matmul(PS[bk][0:1, 0:NW], C["CS2"][:, 0:1], src_fn(ci, n),
                                     start=True, stop=True),
                            r=rb + [cN] + sqs.b(sl, sl + T), w=[ps_b[bk]])
                    if ci == 0:
                        pg.emit(EV, lambda e, n=n, bk=bk, acc_r=acc_r:
                                e.tensor_copy(acc_r.a(n * NW, (n + 1) * NW, 0, 1),
                                              PS[bk][0:1, 0:NW]),
                                r=[ps_b[bk]], w=acc_r.b())
                    else:
                        pg.emit(EV, lambda e, n=n, bk=bk, acc_r=acc_r:
                                e.tensor_add(acc_r.a(n * NW, (n + 1) * NW, 0, 1),
                                             acc_r.a(n * NW, (n + 1) * NW, 0, 1),
                                             PS[bk][0:1, 0:NW]),
                                r=[ps_b[bk]] + acc_r.b(), w=acc_r.b())
        pg.emit(EV, lambda e: e.tensor_scalar_mul(mrow.a(0, T, 0, 1), statx.a(0, T, 0, 1), 1.0 / D),
                r=statx.b(), w=mrow.b())
        pg.emit(EV, lambda e: e.tensor_scalar_mul(tr1.a(0, T, 0, 1), stats.a(0, T, 0, 1), 1.0 / D),
                r=stats.b(), w=tr1.b())
        pg.emit(EV, lambda e: e.tensor_mul(tr2.a(0, T, 0, 1), mrow.a(0, T, 0, 1), mrow.a(0, T, 0, 1)),
                r=mrow.b(), w=tr2.b())
        pg.emit(EV, lambda e: e.scalar_tensor_tensor(tr1.a(0, T, 0, 1), tr2.a(0, T, 0, 1), -1.0,
                                                     tr1.a(0, T, 0, 1), ALU.mult, ALU.add),
                r=tr1.b() + tr2.b(), w=tr1.b())
        pg.emit(EV, lambda e: e.tensor_scalar_add(tr1.a(0, T, 0, 1), tr1.a(0, T, 0, 1), EPS),
                r=tr1.b(), w=tr1.b())
        pg.emit(EV, lambda e: e.reciprocal(tr1.a(0, T, 0, 1), tr1.a(0, T, 0, 1)),
                r=tr1.b(), w=tr1.b())
        pg.emit(ES, lambda e: e.activation(rrow.a(0, T, 0, 1), tr1.a(0, T, 0, 1), AF.Sqrt),
                r=tr1.b(), w=rrow.b())
        for n in range(nhalf):
            for row, rep in [(mrow, mrep), (rrow, rrep)]:
                bk = mmbank()
                pg.emit(ET, lambda e, n=n, bk=bk, row=row:
                        e.matmul(PS[bk][:, 0:NW], C["ONES1"][0:1, 0:128],
                                 row.a(n * NW, (n + 1) * NW, 0, 1), start=True, stop=True),
                        r=row.b() + [cN], w=[ps_b[bk]])
                pg.emit(EV, lambda e, n=n, bk=bk, rep=rep:
                        e.tensor_copy(rep.a(n * NW, (n + 1) * NW), PS[bk][:, 0:NW]),
                        r=[ps_b[bk]], w=rep.b())
        for ci in range(16):
            sink(ci, mrep, rrep)

    wbf = alloc(2 * 2048)
    wsb_b = [pg.buf() for _ in range(4)]

    def WHh(sl, kk, w=64):
        o = wbf.off + sl * 1024 + kk * 64
        return AR[:, o:o + w].bitcast(BF16)

    def WLh(sl, kk, w=64):
        o = wbf.off + sl * 1024 + 512 + kk * 64
        return AR[:, o:o + w].bitcast(BF16)

    def wload_h(src_half, widthh):
        sl = wcount[0] % 4
        wcount[0] += 1
        pg.emit("sync", lambda e, sem, src=src_half, sl=sl, widthh=widthh:
                e.dma_start(out=WSLOT[:, sl * 1024:sl * 1024 + widthh], in_=src)
                .then_inc(sem, 16), w=[ws_b[sl]], dma=f"h{sl}", ndma=1)
        wf = widthh // 2
        hi = AR[:, wbf.off + sl * 1024:wbf.off + sl * 1024 + wf].bitcast(BF16)
        lo = AR[:, wbf.off + sl * 1024 + 512:wbf.off + sl * 1024 + 512 + wf].bitcast(BF16)
        pg.emit("gpsimd", lambda e, sem, hi=hi, src=src_half:
                e.dma_start(out=hi, in_=src).then_inc(sem, 16),
                w=[wsb_b[sl]], dma=f"hh{sl}", ndma=1)
        pg.emit("gpsimd" if sl % 2 == 0 else EV,
                lambda e, lo=lo, sl=sl, widthh=widthh, hi=hi:
                e.tensor_sub(lo, WSLOT[:, sl * 1024:sl * 1024 + widthh], hi),
                r=[ws_b[sl], wsb_b[sl]], w=[wsb_b[sl]])
        return sl

    def wload_parts(src_full, Kc):
        ka = (Kc + 1) // 2
        return [(wload_h(src_full[:, 0:ka * 128], ka * 128), 0, ka),
                (wload_h(src_full[:, ka * 128:Kc * 128], (Kc - ka) * 128), ka, Kc - ka)]

    def emit_parts(parts, rh, rl, rhs_bufs, bk, N=512):
        nparts = len(parts)
        for pi, (sl, k0, nk) in enumerate(parts):
            lastp = pi == nparts - 1
            qs = [(0, nk)] if not callable(rhs_bufs) else \
                 [(q, min(q + 4, nk)) for q in range(0, nk, 4)]
            for q0, q1 in qs:
                rb = rhs_bufs(k0 + q0, k0 + q1) if callable(rhs_bufs) else rhs_bufs
                def sub(e, sl=sl, k0=k0, q0=q0, q1=q1, bk=bk, N=N,
                        first=(pi == 0 and q0 == 0), last_=(lastp and q1 == nk)):
                    last = None
                    for kk in range(q0, q1):
                        ki = k0 + kk
                        e.matmul(PS[bk][:, 0:N], WHh(sl, kk), rh(ki),
                                 start=(first and kk == q0), stop=False)
                        e.matmul(PS[bk][:, 0:N], WHh(sl, kk), rl(ki),
                                 start=False, stop=False)
                        last = e.matmul(PS[bk][:, 0:N], WLh(sl, kk), rh(ki),
                                        start=False, stop=(last_ and kk == q1 - 1))
                    return last
                pg.emit(ET, sub, r=[wsb_b[sl]] + rb, w=[ps_b[bk]])

    def ffn_group3(src_full, Kc, rh, rl, rhs_bufs, bk, N=512):
        emit_parts(wload_parts(src_full, Kc), rh, rl, rhs_bufs, bk, N)

    m0 = mark()
    for ci in range(16):
        pg.emit("sync", lambda e, sem, ci=ci:
                e.dma_start(out=ACC[:, ci * 1024:(ci + 1) * 1024],
                            in_=XM[ci * 128:(ci + 1) * 128, :]).then_inc(sem, 16),
                w=[acc_b[ci]], dma="x", ndma=1)

    xnh_b = [pg.buf() for _ in range(16)]
    lnm = mark()
    xnst = alloc(4 * 1024)
    xnsp = alloc(2 * 1024)

    def main_sink(ci, mrep, rrep):
        sl = (ci % 4) * 1024
        a = xnst.a(sl, sl + 1024)
        bufs = xnst.b(sl, sl + 1024)
        pg.emit(EV, lambda e, ci=ci, a=a:
                e.tensor_sub(a, ACC[:, ci * 1024:(ci + 1) * 1024], mrep.a(0, 1024)),
                r=[acc_b[ci]] + mrep.b(), w=bufs)
        pg.emit(EV, lambda e, a=a, rrep=rrep: e.tensor_mul(a, a, rrep.a(0, 1024)),
                r=bufs + rrep.b(), w=bufs)
        so = xnsp.off + (ci % 2) * 1024
        hi = AR[:, so:so + 512].bitcast(BF16)
        lo = AR[:, so + 512:so + 1024].bitcast(BF16)
        spb = xnsp.b((ci % 2) * 1024, (ci % 2) * 1024 + 1024)
        pg.emit(ES, lambda e, a=a, hi=hi: e.activation(hi, a, AF.Copy),
                r=bufs, w=spb)
        pg.emit(EV, lambda e, a=a, hi=hi, lo=lo: e.tensor_sub(lo, a, hi),
                r=bufs + spb, w=spb)
        def st(e, sem, ci=ci, hi=hi, lo=lo, a=a):
            e.dma_start(out=XNH[ci * 128:(ci + 1) * 128, :], in_=a).then_inc(sem, 16)
            e.dma_start(out=XNH_H[ci * 128:(ci + 1) * 128, :], in_=hi).then_inc(sem, 16)
            e.dma_start(out=XNH_L[ci * 128:(ci + 1) * 128, :], in_=lo).then_inc(sem, 16)
        pg.emit("gpsimd", st, r=spb + bufs, w=[xnh_b[ci]], dma=f"xo{ci % 2}", ndma=3)

    ln_phase(1024, lambda ci: ACC[:, ci * 1024:(ci + 1) * 1024], acc_b, main_sink, 2)
    reset(lnm)

    zA = alloc(21 * MTT)
    cv = alloc(20 * MTT)
    xnt = alloc(16 * MTT)
    xnt_c = lambda ki: AR[:, xnt.off + ki * 256:xnt.off + (ki + 1) * 256]
    xstm = alloc(2 * 2048)
    btm = alloc(2 * 256)
    dt6 = alloc(4 * MTT)
    vtm = alloc(2 * 64)
    lqr = alloc(64)
    bsc = alloc(64)
    aqb = alloc(64)
    gsb = alloc(128)
    cst = alloc(128)
    esb = alloc(4 * 128)
    abc = alloc(4 * 128)
    ytm = alloc(2 * 64)
    bes = alloc(2 * 64)
    ysq = alloc(2 * MTT)
    rsa = alloc(MTT)
    msc = mark()

    def proj_block(mi, T, rhs_ap, rhs_bufs, sink):
        sl = wload(WINB[mi], 2048)
        bk = mmbank()
        def grp(e, sl=sl, bk=bk, T=T):
            last = None
            for ki in range(16):
                last = e.matmul(PS[bk][:, 0:T],
                                WSLOT[:, sl * 2048 + ki * 128:sl * 2048 + (ki + 1) * 128],
                                rhs_ap(ki), start=(ki == 0), stop=(ki == 15))
            return last
        pg.emit(ET, grp, r=[ws_b[2 * sl], ws_b[2 * sl + 1]] + rhs_bufs, w=[ps_b[bk]])
        sink(mi, bk, T)

    def inproj(T, rhs_ap, rhs_bufs, mis, sink):
        for mi in mis:
            proj_block(mi, T, rhs_ap, rhs_bufs, sink)

    def sinkA(mi, bk, T):
        j = (mi - 16) if mi < 36 else 20
        pg.emit(EV, lambda e, j=j, mi=mi, bk=bk, T=T:
                e.tensor_scalar(zA.a(j * MTT, j * MTT + T), PS[bk][:, 0:T],
                                C["BINC"][:, mi:mi + 1], C["MUPC"][:, mi:mi + 1],
                                ALU.add, ALU.mult),
                r=[ps_b[bk], cN], w=zA.b(j * MTT, j * MTT + T))

    def sinkB(mi, bk, T):
        pg.emit(EV, lambda e, mi=mi, bk=bk, T=T:
                e.tensor_scalar(cv.a(mi * MTT, mi * MTT + T), PS[bk][:, 0:T],
                                C["BINC"][:, mi:mi + 1], C["MUPC"][:, mi:mi + 1],
                                ALU.add, ALU.mult),
                r=[ps_b[bk], cN], w=cv.b(mi * MTT, mi * MTT + T))

    def conv_chunk(j, T, mask_tail):
        za = zA.a(j * MTT, j * MTT + T)
        za_b = zA.b(j * MTT, j * MTT + T)
        cia = cvin.a(0, T + 3)
        cib = cvin.b(0, T + 64)
        cva = cv.a(j * MTT, j * MTT + T)
        cvb = cv.b(j * MTT, j * MTT + T)
        tl = TAIL[:, 3 * j:3 * j + 3]
        if mask_tail:
            pg.emit(EV, lambda e, tl=tl: e.tensor_scalar_mul(tl, tl, MSKC),
                    r=[tail_b[j], cN], w=[tail_b[j]])
        pg.emit(EV, lambda e, tl=tl, cia=cia: e.tensor_copy(cia[:, 0:3], tl),
                r=[tail_b[j]], w=cib)
        pg.emit(EV, lambda e, za=za, cia=cia, T=T: e.tensor_copy(cia[:, 3:3 + T], za),
                r=za_b, w=cib)
        pg.emit(EV, lambda e, za=za, tl=tl, T=T: e.tensor_copy(tl, za[:, T - 3:T]),
                r=za_b, w=[tail_b[j]])
        pg.emit(EV, lambda e, cia=cia, cva=cva, j=j, T=T:
                e.tensor_scalar_mul(cva, cia[:, 0:T], C["CWC"][:, 4 * j:4 * j + 1]),
                r=cib + [cN], w=cvb)
        for k in range(1, 4):
            pg.emit(EV, lambda e, cia=cia, cva=cva, j=j, k=k, T=T:
                    e.scalar_tensor_tensor(cva, cia[:, k:k + T],
                                           C["CWC"][:, 4 * j + k:4 * j + k + 1],
                                           cva, ALU.mult, ALU.add),
                    r=cib + [cN] + cvb, w=cvb)
        pg.emit(ES, lambda e, cva=cva, j=j:
                e.activation(cva, cva, AF.Silu, bias=C["CBC"][:, j:j + 1]),
                r=cvb + [cN], w=cvb)

    def tail_only(j, T):
        za = zA.a(j * MTT, j * MTT + T)
        za_b = zA.b(j * MTT, j * MTT + T)
        tl = TAIL[:, 3 * j:3 * j + 3]
        pg.emit(EV, lambda e, za=za, tl=tl, T=T: e.tensor_copy(tl, za[:, T - 3:T]),
                r=za_b, w=[tail_b[j]])

    def dt_pipeline(T, nchunk):
        xa = dt6.a(0, T, 0, 32)
        ta = dt6.a(MTT, MTT + T, 0, 32)
        da = dt6.a(2 * MTT, 2 * MTT + T, 0, 32)
        la = dt6.a(3 * MTT, 3 * MTT + T, 0, 32)
        xb = dt6.b(0, T); tb = dt6.b(MTT, MTT + T); db = dt6.b(2 * MTT, 2 * MTT + T)
        lb = dt6.b(3 * MTT, 3 * MTT + T)
        raw = zA.a(20 * MTT, 20 * MTT + T, 0, 32)
        rwb = zA.b(20 * MTT, 20 * MTT + T)
        pg.emit(EV, lambda e: e.tensor_scalar_add(xa, raw, C["DTBC"]), r=rwb + [cN], w=xb)
        pg.emit(ES, lambda e: e.activation(ta, raw, AF.Abs, bias=C["DTBC"]), r=rwb + [cN], w=tb)
        pg.emit(ES, lambda e: e.activation(ta, ta, AF.Exp, scale=-1.0), r=tb, w=tb)
        pg.emit(ES, lambda e: e.activation(ta, ta, AF.Ln, bias=1.0), r=tb, w=tb)
        pg.emit(EV, lambda e: e.tensor_scalar_max(da, xa, 0.0), r=xb, w=db)
        pg.emit(EV, lambda e: e.tensor_add(da, da, ta), r=db + tb, w=db)
        pg.emit(EV, lambda e: e.tensor_scalar_mul(xa, da, NEGA), r=db + [nega_b], w=xb)
        pg.emit(ES, lambda e: e.activation(ta, da, AF.Ln), r=db, w=tb)
        for c in range(nchunk):
            pg.emit(EV, lambda e, c=c: e.tensor_tensor_scan(
                la[:, c * 128:(c + 1) * 128], xa[:, c * 128:(c + 1) * 128],
                C["Z32"][:, 0:128], 0.0, ALU.add, ALU.add), r=xb + [cN], w=lb)
        pg.emit(EV, lambda e: e.tensor_sub(xa, la, ta), r=lb + tb, w=xb)
        return la, lb, xa, xb

    def transposes(nchunk, la, lb, va, vb):
        for c in range(nchunk):
            for fi in range(16):
                bk = trbank()
                f0 = fi * MTT + c * 128
                pg.emit(ET, lambda e, f0=f0, bk=bk:
                        e.transpose(PS[bk][:, 0:128], cv.a(f0, f0 + 128), C["IDN"]),
                        r=cv.b(f0, f0 + 128) + [cN], w=[ps_b[bk]])
                x0 = c * 2048 + fi * 128
                pg.emit(EV, lambda e, x0=x0, bk=bk:
                        e.tensor_copy(xstm.a(x0, x0 + 128), PS[bk][:, 0:128]),
                        r=[ps_b[bk]], w=xstm.b(x0, x0 + 128))
            for bi in range(2):
                bk = trbank()
                f0 = (16 + bi) * MTT + c * 128
                pg.emit(ET, lambda e, f0=f0, bk=bk:
                        e.transpose(PS[bk][:, 0:128], cv.a(f0, f0 + 128), C["IDN"]),
                        r=cv.b(f0, f0 + 128) + [cN], w=[ps_b[bk]])
                b0 = c * 256 + bi * 128
                pg.emit(EV, lambda e, b0=b0, bk=bk:
                        e.tensor_copy(btm.a(b0, b0 + 128), PS[bk][:, 0:128]),
                        r=[ps_b[bk]], w=btm.b(b0, b0 + 128))
            bk = trbank()
            pg.emit(ET, lambda e, c=c, bk=bk:
                    e.transpose(PS[bk][0:128, 0:32], va[:, c * 128:(c + 1) * 128],
                                C["IDN"][0:32, 0:32]),
                    r=vb + [cN], w=[ps_b[bk]])
            pg.emit(EV, lambda e, c=c, bk=bk:
                    e.tensor_copy(vtm.a(c * 64, c * 64 + 32), PS[bk][0:128, 0:32]),
                    r=[ps_b[bk]], w=vtm.b(c * 64, c * 64 + 64))

    def scan_chunk(c, la, lb, full, filler=None):
        lac = la[:, c * 128:(c + 1) * 128]
        pg.emit(ET, lambda e, lac=lac:
                e.transpose(PS[2][0:1, 0:32], lac[:, 127:128], C["IDN"][0:32, 0:32]),
                r=lb + [cN], w=[ps_b[2]])
        pg.emit(EV, lambda e: e.tensor_copy(lqr.a(0, 32, 0, 1), PS[2][0:1, 0:32]),
                r=[ps_b[2]], w=lqr.b())
        pg.emit(ET, lambda e:
                e.matmul(PS[2][0:32, 32:64], C["ONES1"][0:1, 0:32], lqr.a(0, 32, 0, 1),
                         start=True, stop=True), r=lqr.b() + [cN], w=[ps_b[2]])
        pg.emit(ES, lambda e:
                e.activation(aqb.a(0, 32, 0, 32), PS[2][0:32, 32:64], AF.Exp),
                r=[ps_b[2]], w=aqb.b())
        pg.emit(ET, lambda e:
                e.matmul(PS[2][:, 64:96], C["ONES1"][0:1, 0:128], lqr.a(0, 32, 0, 1),
                         start=True, stop=True), r=lqr.b() + [cN], w=[ps_b[2]])
        pg.emit(EV, lambda e, c=c:
                e.scalar_tensor_tensor(bsc.a(0, 32), vtm.a(c * 64, c * 64 + 32), -1.0,
                                       PS[2][:, 64:96], ALU.mult, ALU.add),
                r=vtm.b(c * 64, c * 64 + 64) + [ps_b[2]], w=bsc.b())
        pg.emit(ES, lambda e: e.activation(bsc.a(0, 32), bsc.a(0, 32), AF.Exp),
                r=bsc.b(), w=bsc.b())

        for g in range(Gg):
            if filler is not None:
                th = next(filler, None)
                if th is not None:
                    th()
            bch, boff = 16 + g // 4, 32 * (g % 4)
            bf0 = bch * MTT + c * 128
            b_fm = cv.a(bf0, bf0 + 128, boff, boff + 32)
            b_fm_b = cv.b(bf0, bf0 + 128)
            b_base = AR[0:128, btm.off + c * 256 + 32 * g:btm.off + c * 256 + 32 * g + 32]
            in0 = bass.AP(b_base.tensor, b_base.offset,
                          [list(b_base.ap[0]), [0, 4], [1, 32]])
            s_base = AR[0:128, bsc.off + 4 * g:bsc.off + 4 * g + 4]
            in1 = bass.AP(s_base.tensor, s_base.offset,
                          [list(s_base.ap[0]), [1, 4], [0, 32]])
            o_base = AR[0:128, bes.off:bes.off + 128]
            out3 = bass.AP(o_base.tensor, o_base.offset,
                           [list(o_base.ap[0]), [32, 4], [1, 32]])
            pg.emit(EV, lambda e, in0=in0, in1=in1, out3=out3:
                    e.tensor_mul(out3, in0, in1),
                    r=btm.b(c * 256, c * 256 + 256) + bsc.b(), w=bes.b())
            if full:
                cf0 = (18 + g // 4) * MTT + c * 128
                c_fm = cv.a(cf0, cf0 + 128, boff, boff + 32)
                c_fm_b = cv.b(cf0, cf0 + 128)
                pg.emit(ET, lambda e, b_fm=b_fm, c_fm=c_fm, boff=boff:
                        e.matmul(PS[2][:, 128:256], b_fm, c_fm, start=True, stop=True,
                                 tile_position=(boff, 0)),
                        r=b_fm_b + c_fm_b, w=[ps_b[2]])
                pg.emit(EV, lambda e: e.tensor_copy(gsb.a(0, 128), PS[2][:, 128:256]),
                        r=[ps_b[2]], w=gsb.b())
                pg.emit(ET, lambda e, c_fm=c_fm, boff=boff:
                        e.matmul(PS[2][0:32, 256:384],
                                 C["IDN"][boff:boff + 32, boff:boff + 32], c_fm,
                                 start=True, stop=True, tile_position=(boff, 0)),
                        r=c_fm_b + [cN], w=[ps_b[2]])
                pg.emit(EV, lambda e: e.tensor_copy(cst.a(0, 128, 0, 32),
                                                    PS[2][0:32, 256:384]),
                        r=[ps_b[2]], w=cst.b())
                for h in range(4 * g, 4 * g + 4):
                    par = h % 2
                    eo = 128 * (h % 4)
                    bly = BK_LY[par]
                    selh = bcast(C["IDN"][0:32, h:h + 1], 128)
                    pg.emit(ET, lambda e, selh=selh, lac=lac, bly=bly:
                            e.matmul(PS[bly][:, 0:128], selh, lac, start=True, stop=True),
                            r=lb + [cN], w=[ps_b[bly]])
                    vcol = vtm.col(c * 64 + h)
                    pg.emit(EV, lambda e, vcol=vcol, eo=eo, bly=bly:
                            e.scalar_tensor_tensor(esb.a(eo, eo + 128), PS[bly][:, 0:128],
                                                   vcol, C["MASKC"], ALU.subtract, ALU.add),
                            r=[ps_b[bly]] + vtm.b(c * 64, c * 64 + 64) + [cN],
                            w=esb.b(eo, eo + 128))
                    pg.emit(ES, lambda e, eo=eo, bly=bly:
                            e.activation(abc.a(eo, eo + 128, 0, 32), PS[bly][0:32, 0:128],
                                         AF.Exp),
                            r=[ps_b[bly]], w=abc.b(eo, eo + 128))
                    pg.emit(ES, lambda e, eo=eo:
                            e.activation(esb.a(eo, eo + 128), esb.a(eo, eo + 128), AF.Exp),
                            r=esb.b(eo, eo + 128), w=esb.b(eo, eo + 128))
                    pg.emit(EV, lambda e, eo=eo:
                            e.tensor_mul(esb.a(eo, eo + 128), esb.a(eo, eo + 128),
                                         gsb.a(0, 128)),
                            r=esb.b(eo, eo + 128) + gsb.b(), w=esb.b(eo, eo + 128))
                    pg.emit(EV, lambda e, eo=eo:
                            e.tensor_mul(abc.a(eo, eo + 128, 0, 32), cst.a(0, 128, 0, 32),
                                         abc.a(eo, eo + 128, 0, 32)),
                            r=cst.b() + abc.b(eo, eo + 128), w=abc.b(eo, eo + 128))
            for h in range(4 * g, 4 * g + 4):
                xs = xstm.a(c * 2048 + h * 64, c * 2048 + (h + 1) * 64)
                xs_b = xstm.b(c * 2048 + h * 64, c * 2048 + (h + 1) * 64)
                sl = SALL[:, h * 64:(h + 1) * 64]
                par = h % 2
                eo = 128 * (h % 4)
                if full:
                    bky = [3, 4][par]
                    def ygrp(e, xs=xs, sl=sl, eo=eo, bky=bky):
                        e.matmul(PS[bky][:, 0:64], esb.a(eo, eo + 128), xs,
                                 start=True, stop=False)
                        return e.matmul(PS[bky][:, 0:64], abc.a(eo, eo + 128, 0, 32), sl,
                                        start=False, stop=True)
                    pg.emit(ET, ygrp, r=esb.b(eo, eo + 128) + abc.b(eo, eo + 128) + xs_b
                            + [s_b[h]], w=[ps_b[bky]])
                    ysl = par * 64
                    ya = ytm.a(ysl, ysl + 64)
                    pg.emit(EV, lambda e, xs=xs, h=h, ya=ya, bky=bky:
                            e.scalar_tensor_tensor(ya, xs, DALL[:, h:h + 1],
                                                   PS[bky][:, 0:64], ALU.mult, ALU.add),
                            r=xs_b + [dall_b, ps_b[bky]], w=ytm.b(ysl, ysl + 64))
                    if par == 1:
                        pg.emit(ET, lambda e:
                                e.transpose(PS[2][:, 384:512], ytm.a(0, 128), C["IDN"]),
                                r=ytm.b(0, 128) + [cN], w=[ps_b[2]])
                        yf0 = (h // 2) * MTT + c * 128
                        pg.emit(EV, lambda e, yf0=yf0:
                                e.tensor_copy(zA.a(yf0, yf0 + 128), PS[2][:, 384:512]),
                                r=[ps_b[2]], w=zA.b(yf0, yf0 + 128))
                bo = 32 * (h % 4)
                pg.emit(ET, lambda e, xs=xs, bo=bo:
                        e.matmul(PS[5][0:32, 256:320], bes.a(bo, bo + 32), xs,
                                 start=True, stop=True),
                        r=bes.b(bo, bo + 64) + xs_b, w=[ps_b[5]])
                pg.emit(EV, lambda e, sl=sl, h=h:
                        e.scalar_tensor_tensor(sl, sl, aqb.col(h, 0, 32),
                                               PS[5][0:32, 256:320], ALU.mult, ALU.add),
                        r=[s_b[h], ps_b[5]] + aqb.b(), w=[s_b[h]])

    pln = alloc(7 * 256)
    cvin = Region(rsa.off, MTT + 64)
    sqst = alloc(256)

    def ln_stream(tok0):
        rg = lambda i: Region(pln.off + i * 256, 256)
        statx, stats, mrow, rrow, tr1, mrep, rrep = [rg(i) for i in range(7)]
        tr2 = rrow

        def chunk_dma(ci, stg, sl):
            pg.emit("sync", lambda e, sem, ci=ci, stg=stg:
                    e.dma_start(out=stg.a(0, 256),
                                in_=XP[ci * 128:(ci + 1) * 128, tok0:tok0 + 256])
                    .then_inc(sem, 16), w=stg.b(), dma=f"px{sl}", ndma=1)

        for ci in range(16):
            sl = ci % 2
            stg = Region(ysq.off + sl * 256, 256)
            sqg = Region(sqst.off, 256)
            chunk_dma(ci, stg, sl)
            pg.emit(ES, lambda e, stg=stg, sqg=sqg:
                    e.activation(sqg.a(0, 256), stg.a(0, 256), AF.Square),
                    r=stg.b(), w=sqg.b())
            for acc_r, src_r in ((statx, stg), (stats, sqg)):
                bk = mmbank()
                pg.emit(ET, lambda e, bk=bk, src_r=src_r:
                        e.matmul(PS[bk][0:1, 0:256], C["CS2"][:, 0:1], src_r.a(0, 256),
                                 start=True, stop=True),
                        r=src_r.b() + [cN], w=[ps_b[bk]])
                if ci == 0:
                    pg.emit(EV, lambda e, bk=bk, acc_r=acc_r:
                            e.tensor_copy(acc_r.a(0, 256, 0, 1), PS[bk][0:1, 0:256]),
                            r=[ps_b[bk]], w=acc_r.b())
                else:
                    pg.emit(EV, lambda e, bk=bk, acc_r=acc_r:
                            e.tensor_add(acc_r.a(0, 256, 0, 1), acc_r.a(0, 256, 0, 1),
                                         PS[bk][0:1, 0:256]),
                            r=[ps_b[bk]] + acc_r.b(), w=acc_r.b())
        pg.emit(EV, lambda e: e.tensor_scalar_mul(mrow.a(0, 256, 0, 1),
                                                  statx.a(0, 256, 0, 1), 1.0 / D),
                r=statx.b(), w=mrow.b())
        pg.emit(EV, lambda e: e.tensor_scalar_mul(tr1.a(0, 256, 0, 1),
                                                  stats.a(0, 256, 0, 1), 1.0 / D),
                r=stats.b(), w=tr1.b())
        pg.emit(EV, lambda e: e.tensor_mul(tr2.a(0, 256, 0, 1), mrow.a(0, 256, 0, 1),
                                           mrow.a(0, 256, 0, 1)),
                r=mrow.b(), w=tr2.b())
        pg.emit(EV, lambda e: e.scalar_tensor_tensor(tr1.a(0, 256, 0, 1),
                                                     tr2.a(0, 256, 0, 1), -1.0,
                                                     tr1.a(0, 256, 0, 1), ALU.mult, ALU.add),
                r=tr1.b() + tr2.b(), w=tr1.b())
        pg.emit(EV, lambda e: e.tensor_scalar_add(tr1.a(0, 256, 0, 1),
                                                  tr1.a(0, 256, 0, 1), EPS),
                r=tr1.b(), w=tr1.b())
        pg.emit(EV, lambda e: e.reciprocal(tr1.a(0, 256, 0, 1), tr1.a(0, 256, 0, 1)),
                r=tr1.b(), w=tr1.b())
        pg.emit(ES, lambda e: e.activation(rrow.a(0, 256, 0, 1), tr1.a(0, 256, 0, 1),
                                           AF.Sqrt),
                r=tr1.b(), w=rrow.b())
        for row, rep in ((mrow, mrep), (rrow, rrep)):
            bk = mmbank()
            pg.emit(ET, lambda e, bk=bk, row=row:
                    e.matmul(PS[bk][:, 0:256], C["ONES1"][0:1, 0:128],
                             row.a(0, 256, 0, 1), start=True, stop=True),
                    r=row.b() + [cN], w=[ps_b[bk]])
            pg.emit(EV, lambda e, bk=bk, rep=rep:
                    e.tensor_copy(rep.a(0, 256), PS[bk][:, 0:256]),
                    r=[ps_b[bk]], w=rep.b())
        for ci in range(16):
            sl = ci % 2
            stg = Region(ysq.off + sl * 256, 256)
            chunk_dma(ci, stg, sl)
            dst = xnt_c(ci)
            db = xnt.b(ci * 256, (ci + 1) * 256)
            pg.emit(EV, lambda e, stg=stg, dst=dst:
                    e.tensor_sub(dst, stg.a(0, 256), mrep.a(0, 256)),
                    r=stg.b() + mrep.b(), w=db)
            pg.emit(EV, lambda e, dst=dst: e.tensor_mul(dst, dst, rrep.a(0, 256)),
                    r=db + rrep.b(), w=db)

    PRE_MIS = list(range(16, 34)) + [36]
    ln_stream(0)
    for pt in range(4):
        mis = PRE_MIS if pt < 3 else (list(range(16, 36)) + [36])
        for mi in mis:
            proj_block(mi, 256, xnt_c, xnt.b(), sinkA)
            if 16 <= mi < 34:
                conv_chunk(mi - 16, 256, False)
            elif mi in (34, 35):
                tail_only(mi - 16, 256)
        if pt < 3:
            ln_stream((pt + 1) * 256)
        la, lb, va, vb = dt_pipeline(256, 2)
        transposes(2, la, lb, va, vb)
        for cch in range(2):
            scan_chunk(cch, la, lb, full=False)

    pg.emit(EV, lambda e: e.tensor_scalar_mul(SALL, SALL, MSKC[0:32, :]),
            r=s_b + [cN], w=s_b)

    def xt_loads_emit(mt):
        def xt_loads(e, sem, mt=mt):
            for ki in range(16):
                e.dma_start(out=xnt_c(ki),
                            in_=XNH[ki * 128:(ki + 1) * 128, mt * 256:(mt + 1) * 256]) \
                    .then_inc(sem, 16)
        pg.emit("sync", xt_loads, r=xnh_b, w=xnt.b(), dma="xt", ndma=16)

    xt_loads_emit(0)
    for mt in range(4):
        for mi in range(16, 37):
            proj_block(mi, 256, xnt_c, xnt.b(), sinkA)
            if mi < 36:
                conv_chunk(mi - 16, 256, mt == 0)
        la, lb, va, vb = dt_pipeline(256, 2)
        transposes(2, la, lb, va, vb)
        fillers = iter([
            (lambda mi=mi: proj_block(mi, 256, xnt_c, xnt.b(), sinkB))
            for mi in range(16)])
        for cch in range(2):
            scan_chunk(cch, la, lb, full=True, filler=fillers)
        for th in fillers:
            th()
        if mt < 3:
            xt_loads_emit(mt + 1)
        for ci in range(16):
            f0 = ci * MTT
            pg.emit(ES, lambda e, f0=f0: e.activation(cv.a(f0, f0 + 256), cv.a(f0, f0 + 256),
                                                      AF.Silu),
                    r=cv.b(f0, f0 + 256), w=cv.b(f0, f0 + 256))
            pg.emit(EV, lambda e, f0=f0:
                    e.tensor_mul(zA.a(f0, f0 + 256), zA.a(f0, f0 + 256), cv.a(f0, f0 + 256)),
                    r=zA.b(f0, f0 + 256) + cv.b(f0, f0 + 256), w=zA.b(f0, f0 + 256))
        for ci in range(16):
            f0 = ci * MTT
            sl = (ci % 2) * MTT
            pg.emit(ES, lambda e, f0=f0, sl=sl:
                    e.activation(ysq.a(sl, sl + 256), zA.a(f0, f0 + 256), AF.Square),
                    r=zA.b(f0, f0 + 256), w=ysq.b(sl, sl + 256))
            g = ci // 2
            bk = mmbank()
            pg.emit(ET, lambda e, sl=sl, g=g, bk=bk:
                    e.matmul(PS[bk][0:8, 0:256], C["CS8"][:, 8 * g:8 * g + 8],
                             ysq.a(sl, sl + 256), start=True, stop=True),
                    r=ysq.b(sl, sl + 256) + [cN], w=[ps_b[bk]])
            if ci == 0:
                pg.emit(EV, lambda e, bk=bk: e.tensor_copy(rsa.a(0, 256, 0, 8),
                                                           PS[bk][0:8, 0:256]),
                        r=[ps_b[bk]], w=rsa.b())
            else:
                pg.emit(EV, lambda e, bk=bk: e.tensor_add(rsa.a(0, 256, 0, 8),
                                                          rsa.a(0, 256, 0, 8),
                                                          PS[bk][0:8, 0:256]),
                        r=[ps_b[bk]] + rsa.b(), w=rsa.b())
        pg.emit(EV, lambda e: e.tensor_scalar_mul(rsa.a(0, 256, 0, 8), rsa.a(0, 256, 0, 8),
                                                  1.0 / 256.0), r=rsa.b(), w=rsa.b())
        pg.emit(EV, lambda e: e.tensor_scalar_add(rsa.a(0, 256, 0, 8), rsa.a(0, 256, 0, 8),
                                                  EPS), r=rsa.b(), w=rsa.b())
        pg.emit(EV, lambda e: e.reciprocal(rsa.a(0, 256, 0, 8), rsa.a(0, 256, 0, 8)),
                r=rsa.b(), w=rsa.b())
        pg.emit(ES, lambda e: e.activation(rsa.a(0, 256, 0, 8), rsa.a(0, 256, 0, 8), AF.Sqrt),
                r=rsa.b(), w=rsa.b())
        for g in range(8):
            bk = mmbank()
            selg = bcast(C["IDN"][0:8, g:g + 1], 128)
            pg.emit(ET, lambda e, bk=bk, selg=selg:
                    e.matmul(PS[bk][:, 0:256], selg, rsa.a(0, 256, 0, 8),
                             start=True, stop=True),
                    r=rsa.b() + [cN], w=[ps_b[bk]])
            for ci in (2 * g, 2 * g + 1):
                f0 = ci * MTT
                pg.emit(EV, lambda e, f0=f0, ci=ci, bk=bk:
                        e.scalar_tensor_tensor(zA.a(f0, f0 + 256), zA.a(f0, f0 + 256),
                                               C["NWC"][:, ci:ci + 1], PS[bk][:, 0:256],
                                               ALU.mult, ALU.mult),
                        r=zA.b(f0, f0 + 256) + [cN, ps_b[bk]], w=zA.b(f0, f0 + 256))
        for mo in range(16):
            sl = wload(WOB[mo], 2048)
            bk = mmbank()
            def wog(e, sl=sl, bk=bk):
                last = None
                for ki in range(16):
                    last = e.matmul(PS[bk][:, 0:256],
                                    WSLOT[:, sl * 2048 + ki * 128:sl * 2048 + (ki + 1) * 128],
                                    zA.a(ki * MTT, ki * MTT + 256),
                                    start=(ki == 0), stop=(ki == 15))
                return last
            pg.emit(ET, wog, r=[ws_b[2 * sl], ws_b[2 * sl + 1]] + zA.b(0, 16 * MTT), w=[ps_b[bk]])
            pg.emit(EV, lambda e, mo=mo, bk=bk, mt=mt:
                    e.scalar_tensor_tensor(ACC[:, mo * 1024 + mt * 256:mo * 1024 + (mt + 1) * 256],
                                           PS[bk][:, 0:256], C["BOC"][:, mo:mo + 1],
                                           ACC[:, mo * 1024 + mt * 256:mo * 1024 + (mt + 1) * 256],
                                           ALU.add, ALU.add),
                    r=[ps_b[bk], cN, acc_b[mo]], w=[acc_b[mo]])

    reset(m0)
    mmbanks[0] = [0, 1, 2, 3, 4, 5]
    ring = alloc(16 * 512)
    hbuf = alloc(11 * 1024)
    h1s = alloc(1024)
    wrep = alloc(1024)
    grep = alloc(1024)
    lfm = alloc(1024)
    wfm = alloc(1024)
    prb = alloc(64)
    ltm = Region(prb.off + 0, 8)
    p8a = Region(prb.off + 8, 8)
    cols = [Region(prb.off + 16 + i, 1) for i in range(6)]
    gwb = alloc(192)

    ring_h = lambda ki: AR[:, ring.off + ki * 256:ring.off + (ki + 1) * 256].bitcast(BF16)
    ring_l = lambda ki: AR[:, ring.off + 4096 + ki * 256:ring.off + 4096 + (ki + 1) * 256].bitcast(BF16)
    H_h = lambda fi, n: AR[:, hbuf.off + fi * 512 + n * 256:hbuf.off + fi * 512 + (n + 1) * 256].bitcast(BF16)
    H_l = lambda fi, n: AR[:, hbuf.off + 5632 + fi * 512 + n * 256:hbuf.off + 5632 + fi * 512 + (n + 1) * 256].bitcast(BF16)
    GWH = lambda ki: AR[:, gwb.off + ki * 4:gwb.off + ki * 4 + 4].bitcast(BF16)
    GWL = lambda ki: AR[:, gwb.off + 64 + ki * 4:gwb.off + 64 + ki * 4 + 4].bitcast(BF16)
    SGH = lambda ki: AR[:, gwb.off + 128:gwb.off + 136].bitcast(BF16)[:, ki:ki + 1]
    SGL = lambda ki: AR[:, gwb.off + 144:gwb.off + 152].bitcast(BF16)[:, ki:ki + 1]
    gw_hi = AR[:, gwb.off:gwb.off + 64].bitcast(BF16)
    gw_lo = AR[:, gwb.off + 64:gwb.off + 128].bitcast(BF16)
    sg_hi = AR[:, gwb.off + 128:gwb.off + 136].bitcast(BF16)
    sg_lo = AR[:, gwb.off + 144:gwb.off + 152].bitcast(BF16)
    pg.emit(ES, lambda e: e.activation(gw_hi, C["GWC"], AF.Copy), r=[cN], w=gwb.b())
    pg.emit(EV, lambda e: e.tensor_sub(gw_lo, C["GWC"], gw_hi), r=[cN] + gwb.b(), w=gwb.b())
    pg.emit(ES, lambda e: e.activation(sg_hi, C["SGC"], AF.Copy), r=[cN] + gwb.b(), w=gwb.b())
    pg.emit(EV, lambda e: e.tensor_sub(sg_lo, C["SGC"], sg_hi), r=[cN] + gwb.b(), w=gwb.b())

    ring_rb = lambda a, b: ring.b(a * 256, b * 256) + \
        ring.b(4096 + a * 256, 4096 + b * 256)

    def ring_fill(n):
        for k0 in range(0, 16, 4):
            def fill(e, sem, n=n, k0=k0):
                for ki in range(k0, k0 + 4):
                    e.dma_start(out=ring_h(ki),
                                in_=XNH_H[ki * 128:(ki + 1) * 128, n * 512:(n + 1) * 512]) \
                        .then_inc(sem, 16)
                    e.dma_start(out=ring_l(ki),
                                in_=XNH_L[ki * 128:(ki + 1) * 128, n * 512:(n + 1) * 512]) \
                        .then_inc(sem, 16)
            pg.emit("sync", fill, r=xnh_b, w=ring_rb(k0, k0 + 4), dma="xr", ndma=8)

    for n in range(2):
        ring_fill(n)
        bk = mmbank()
        def rlog(e, bk=bk):
            last = None
            for ki in range(16):
                e.matmul(PS[bk][0:8, 0:512], GWH(ki), ring_h(ki),
                         start=(ki == 0), stop=False)
                e.matmul(PS[bk][0:8, 0:512], GWH(ki), ring_l(ki), start=False, stop=False)
                last = e.matmul(PS[bk][0:8, 0:512], GWL(ki), ring_h(ki),
                                start=False, stop=(ki == 15))
            return last
        pg.emit(ET, rlog, r=ring.b() + gwb.b(), w=[ps_b[bk]])
        pg.emit(EV, lambda e, n=n, bk=bk:
                e.tensor_copy(lfm.a(n * 512, (n + 1) * 512, 0, 8), PS[bk][0:8, 0:512]),
                r=[ps_b[bk]], w=lfm.b(n * 512, (n + 1) * 512))
        bk = mmbank()
        def slog(e, bk=bk):
            last = None
            for ki in range(16):
                e.matmul(PS[bk][0:1, 0:512], SGH(ki), ring_h(ki),
                         start=(ki == 0), stop=False)
                e.matmul(PS[bk][0:1, 0:512], SGH(ki), ring_l(ki), start=False, stop=False)
                last = e.matmul(PS[bk][0:1, 0:512], SGL(ki), ring_h(ki),
                                start=False, stop=(ki == 15))
            return last
        pg.emit(ET, slog, r=ring.b() + gwb.b(), w=[ps_b[bk]])
        pg.emit(EV, lambda e, n=n, bk=bk:
                e.tensor_copy(wfm.a(n * 512, (n + 1) * 512, 0, 1), PS[bk][0:1, 0:512]),
                r=[ps_b[bk]], w=wfm.b(n * 512, (n + 1) * 512))
    pg.emit(ES, lambda e: e.activation(wfm.a(0, 1024, 0, 1), wfm.a(0, 1024, 0, 1),
                                       AF.Sigmoid),
            r=wfm.b(), w=wfm.b())
    for n in range(2):
        bk = mmbank()
        pg.emit(ET, lambda e, n=n, bk=bk:
                e.matmul(PS[bk][:, 0:512], C["ONES1"][0:1, 0:128],
                         wfm.a(n * 512, (n + 1) * 512, 0, 1), start=True, stop=True),
                r=wfm.b() + [cN], w=[ps_b[bk]])
        pg.emit(EV, lambda e, n=n, bk=bk:
                e.tensor_copy(grep.a(n * 512, (n + 1) * 512), PS[bk][:, 0:512]),
                r=[ps_b[bk]], w=grep.b())
    for tc in range(8):
        bk = trbank()
        pg.emit(ET, lambda e, tc=tc, bk=bk:
                e.transpose(PS[bk][0:128, 0:8], lfm.a(tc * 128, (tc + 1) * 128, 0, 8),
                            C["IDN"][0:8, 0:8]),
                r=lfm.b(tc * 128, (tc + 1) * 128) + [cN], w=[ps_b[bk]])
        pg.emit(EV, lambda e, bk=bk: e.tensor_copy(ltm.a(0, 8), PS[bk][0:128, 0:8]),
                r=[ps_b[bk]], w=ltm.b())
        c1, c2, c3, c4, c5, c6 = [cc.a(0, 1) for cc in cols]
        cb = [cc.b() for cc in cols]
        pg.emit(EV, lambda e, c1=c1: e.reduce_max(c1, ltm.a(0, 8), axis=AX.X),
                r=ltm.b(), w=cb[0])
        pg.emit(EV, lambda e, c1=c1, c2=c2: e.tensor_scalar_mul(c2, c1, -1.0),
                r=cb[0], w=cb[1])
        pg.emit(ES, lambda e, c2=c2: e.activation(ltm.a(0, 8), ltm.a(0, 8), AF.Exp, bias=c2),
                r=ltm.b() + cb[1], w=ltm.b())
        pg.emit(EV, lambda e, c3=c3: e.reduce_sum(c3, ltm.a(0, 8), axis=AX.X),
                r=ltm.b(), w=cb[2])
        pg.emit(EV, lambda e, c3=c3: e.reciprocal(c3, c3), r=cb[2], w=cb[2])
        pg.emit(EV, lambda e, c3=c3: e.tensor_scalar_mul(ltm.a(0, 8), ltm.a(0, 8), c3),
                r=ltm.b() + cb[2], w=ltm.b())
        pg.emit(EV, lambda e, c1=c1: e.reduce_max(c1, ltm.a(0, 8), axis=AX.X),
                r=ltm.b(), w=cb[0])
        pg.emit(EV, lambda e, c1=c1: e.tensor_scalar(p8a.a(0, 8), ltm.a(0, 8), c1, None,
                                                     ALU.is_equal),
                r=ltm.b() + cb[0], w=p8a.b())
        pg.emit(EV, lambda e, c4=c4: e.reduce_sum(c4, p8a.a(0, 8), axis=AX.X),
                r=p8a.b(), w=cb[3])
        pg.emit(EV, lambda e: e.scalar_tensor_tensor(p8a.a(0, 8), p8a.a(0, 8), -1e30,
                                                     ltm.a(0, 8), ALU.mult, ALU.add),
                r=p8a.b() + ltm.b(), w=p8a.b())
        pg.emit(EV, lambda e, c5=c5: e.reduce_max(c5, p8a.a(0, 8), axis=AX.X),
                r=p8a.b(), w=cb[4])
        pg.emit(EV, lambda e, c4=c4: e.tensor_scalar(c4, c4, 1.5, None, ALU.is_ge),
                r=cb[3], w=cb[3])
        pg.emit(EV, lambda e, c1=c1, c5=c5, c6=c6: e.tensor_sub(c6, c1, c5),
                r=cb[0] + cb[4], w=cb[5])
        pg.emit(EV, lambda e, c4=c4, c5=c5, c6=c6:
                e.scalar_tensor_tensor(c5, c4, c6, c5, ALU.mult, ALU.add),
                r=cb[3] + cb[5] + cb[4], w=cb[4])
        pg.emit(EV, lambda e, c5=c5: e.tensor_scalar(p8a.a(0, 8), ltm.a(0, 8), c5, None,
                                                     ALU.is_ge),
                r=ltm.b() + cb[4], w=p8a.b())
        pg.emit(EV, lambda e: e.tensor_mul(p8a.a(0, 8), p8a.a(0, 8), ltm.a(0, 8)),
                r=p8a.b() + ltm.b(), w=p8a.b())
        bk = trbank()
        pg.emit(ET, lambda e, bk=bk: e.transpose(PS[bk][0:8, 0:128], p8a.a(0, 8), C["IDN"]),
                r=p8a.b() + [cN], w=[ps_b[bk]])
        pg.emit(EV, lambda e, tc=tc, bk=bk:
                e.tensor_copy(lfm.a(tc * 128, (tc + 1) * 128, 0, 8), PS[bk][0:8, 0:128]),
                r=[ps_b[bk]], w=lfm.b(tc * 128, (tc + 1) * 128))
    ring_fill(0)


    def ffn_pass(fc1_blk, fc2_blk, proj_blk, prep_region, nfi, tag,
                 fill0_done=False, next_fill0=None):
        for n in range(2):
            if n == 1 or not fill0_done:
                ring_fill(n)
            for fi in range(nfi):
                bk1 = mmbank()
                ffn_group3(fc1_blk(fi), 16, ring_h, ring_l, ring_rb, bk1)
                pg.emit(ES, lambda e, bk1=bk1: e.activation(h1s.a(0, 512), PS[bk1], AF.Silu),
                        r=[ps_b[bk1]], w=h1s.b(0, 512))
                bk2 = mmbank()
                ffn_group3(fc2_blk(fi), 16, ring_h, ring_l, ring_rb, bk2)
                pg.emit(EV, lambda e, bk2=bk2:
                        e.tensor_mul(h1s.a(0, 512), h1s.a(0, 512), PS[bk2]),
                        r=h1s.b(0, 512) + [ps_b[bk2]], w=h1s.b(0, 512))
                pg.emit(EV, lambda e, n=n:
                        e.tensor_mul(h1s.a(0, 512), h1s.a(0, 512),
                                     prep_region.a(n * 512, (n + 1) * 512)),
                        r=h1s.b(0, 512) + prep_region.b(), w=h1s.b(0, 512))
                hb = (hbuf.b(fi * 512 + n * 256, fi * 512 + (n + 1) * 256)
                      + hbuf.b(5632 + fi * 512 + n * 256, 5632 + fi * 512 + (n + 1) * 256))
                pg.emit(ES, lambda e, fi=fi, n=n:
                        e.activation(H_h(fi, n), h1s.a(0, 512), AF.Copy),
                        r=h1s.b(0, 512), w=hb)
                pg.emit(EV, lambda e, fi=fi, n=n:
                        e.tensor_sub(H_l(fi, n), h1s.a(0, 512), H_h(fi, n)),
                        r=h1s.b(0, 512) + hb, w=hb)
        if next_fill0 is not None:
            next_fill0()
        for ci in range(16):
            parts = wload_parts(proj_blk(ci), nfi)
            for n in range(2):
                bk = mmbank()
                emit_parts(parts, lambda fi, n=n: H_h(fi, n),
                           lambda fi, n=n: H_l(fi, n), hbuf.b(), bk)
                pg.emit(EV, lambda e, ci=ci, n=n, bk=bk:
                        e.tensor_add(ACC[:, ci * 1024 + n * 512:ci * 1024 + (n + 1) * 512],
                                     ACC[:, ci * 1024 + n * 512:ci * 1024 + (n + 1) * 512],
                                     PS[bk]),
                        r=[acc_b[ci], ps_b[bk]], w=[acc_b[ci]])

    for ex in range(8):
        for n in range(2):
            bk = mmbank()
            sele = bcast(C["IDN"][0:8, ex:ex + 1], 128)
            pg.emit(ET, lambda e, n=n, bk=bk, sele=sele:
                    e.matmul(PS[bk][:, 0:512], sele, lfm.a(n * 512, (n + 1) * 512, 0, 8),
                             start=True, stop=True),
                    r=lfm.b() + [cN], w=[ps_b[bk]])
            pg.emit(EV, lambda e, n=n, bk=bk:
                    e.tensor_copy(wrep.a(n * 512, (n + 1) * 512), PS[bk][:, 0:512]),
                    r=[ps_b[bk]], w=wrep.b())
        ffn_pass(lambda fi, ex=ex: F1B[ex, fi], lambda fi, ex=ex: F2B[ex, fi],
                 lambda ci, ex=ex: PJB[ex, ci], wrep, 11, f"e{ex}",
                 fill0_done=True, next_fill0=lambda: ring_fill(0))

    for p in range(4):
        ffn_pass(lambda fi, p=p: SF1B[11 * p + fi], lambda fi, p=p: SF2B[11 * p + fi],
                 lambda ci, p=p: SPB[ci][:, p * FE:(p + 1) * FE], grep, 11, f"s{p}",
                 fill0_done=True, next_fill0=(lambda: ring_fill(0)) if p < 3 else None)

    out_b = pg.buf()

    out_bs = [pg.buf() for _ in range(16)]
    for ci in range(16):
        pg.emit("gpsimd", lambda e, sem, ci=ci:
                e.dma_start(out=OUT[ci * 128:(ci + 1) * 128, :],
                            in_=ACC[:, ci * 1024:(ci + 1) * 1024]).then_inc(sem, 16),
                r=[acc_b[ci]], w=[out_bs[ci]], dma="o", ndma=1)
    pg.emit("gpsimd", lambda e: None, r=out_bs)

    counts = pg.finalize()
    semkeys = sorted({op.sem for op in pg.ops if op.sem}) + \
              [("e", ET), ("e", EV), ("e", ES), ("e", "gpsimd"), ("e", "sync")]
    import contextlib
    with contextlib.ExitStack() as st:
        sems = {}
        for i, k in enumerate(semkeys):
            nm = k if isinstance(k, str) else f"es_{k[1]}"
            sems[k] = st.enter_context(nc.semaphore(f"sm_{nm}_{i}"))
        blk = st.enter_context(nc.Block())
        marks = {}
        blk.sync(lambda e: pg.stream(e, "sync", sems, marks))
        blk.gpsimd(lambda e: pg.stream(e, "gpsimd", sems, marks))
        blk.tensor(lambda e: pg.stream(e, ET, sems, marks))
        blk.vector(lambda e: pg.stream(e, EV, sems, marks))
        blk.scalar(lambda e: pg.stream(e, ES, sems, marks))
    return nc



def _blockify(W):
    K, M = W.shape
    Kc, Mc = K // 128, M // 128
    return np.ascontiguousarray(
        W.reshape(Kc, 128, Mc, 128).transpose(2, 1, 0, 3).reshape(Mc, 128, K))


def kernel(**inputs):
    f = lambda k: np.asarray(inputs[k], dtype=np.float32)
    hs = f("L0_attn_hidden_states")
    W_in = f("L0_attn_W_in"); b_in = f("L0_attn_b_in")
    conv_w = f("L0_attn_conv_w"); conv_b = f("L0_attn_conv_b")
    A_log = f("L0_attn_A_log"); D_param = f("L0_attn_D_param")
    dt_bias = f("L0_attn_dt_bias"); norm_w = f("L0_attn_norm_w")
    W_out = f("L0_attn_W_out"); b_out = f("L0_attn_b_out")
    mup = f("L0_attn_mup_vector")
    gate_w = f("L0_ffn_gate_w")
    efc1 = f("L0_ffn_expert_fc1"); efc2 = f("L0_ffn_expert_fc2")
    eproj = f("L0_ffn_expert_proj")
    sfc1 = f("L0_ffn_shared_fc1"); sfc2 = f("L0_ffn_shared_fc2")
    sproj = f("L0_ffn_shared_proj"); sgw = f("L0_ffn_shared_gate_w")

    Wp = np.zeros((D, NPC * 128), dtype=np.float32)
    Wp[:, :4640] = W_in
    WINB = _blockify(Wp)
    F1B = np.stack([_blockify(np.ascontiguousarray(efc1[e].T)) for e in range(8)])
    F2B = np.stack([_blockify(np.ascontiguousarray(efc2[e].T)) for e in range(8)])
    PJB = np.stack([_blockify(np.ascontiguousarray(eproj[e].T)) for e in range(8)])
    SF1B = _blockify(np.ascontiguousarray(sfc1.T))
    SF2B = _blockify(np.ascontiguousarray(sfc2.T))
    SPB = _blockify(np.ascontiguousarray(sproj.T))
    WOB = _blockify(W_out)
    GWC = np.ascontiguousarray(
        gate_w.T.reshape(16, 128, 8).transpose(1, 0, 2).reshape(128, 128))
    SGC = np.ascontiguousarray(sgw.reshape(16, 128).T)
    vpad = lambda v, n: np.ascontiguousarray(
        np.pad(v, (0, n * 128 - v.shape[0])).reshape(n, 128).T)
    BINC = vpad(b_in, NPC); MUPC = vpad(mup, NPC)
    CWC = np.ascontiguousarray(conv_w.reshape(20, 128, 4).transpose(1, 0, 2).reshape(128, 80))
    CBC = vpad(conv_b, 20); NWC = vpad(norm_w, 16); BOC = vpad(b_out, 16)
    ALC = A_log.reshape(32, 1).copy()
    DTBC = dt_bias.reshape(32, 1).copy()
    DROW = D_param.reshape(1, 32).copy()
    IDN = np.eye(128, dtype=np.float32)
    jj, tt = np.meshgrid(np.arange(128), np.arange(128), indexing="ij")
    MASKC = np.where(jj <= tt, 0.0, -1e30).astype(np.float32)
    CS8 = np.zeros((128, 64), dtype=np.float32)
    for g in range(8):
        CS8[:, 8 * g + g] = 1.0
    CS2 = np.zeros((128, 4), dtype=np.float32)
    CS2[:, 0] = 1.0
    CS2[:, 3] = 1.0
    ONES1 = np.ones((1, 512), dtype=np.float32)
    Z32 = np.zeros((32, 256), dtype=np.float32)

    shared = dict(WINB=WINB, F1B=F1B, F2B=F2B, PJB=PJB, SF1B=SF1B, SF2B=SF2B,
                  SPB=SPB, WOB=WOB, GWC=GWC, SGC=SGC, BINC=BINC, MUPC=MUPC,
                  CWC=CWC, CBC=CBC, NWC=NWC, BOC=BOC, ALC=ALC, DTBC=DTBC,
                  DROW=DROW, IDN=IDN, MASKC=MASKC, CS8=CS8, CS2=CS2,
                  ONES1=ONES1, Z32=Z32)

    in_maps = []
    for c in range(NCORE):
        b, k = c // 2, c % 2
        m = dict(shared)
        m["XM"] = np.ascontiguousarray(hs[b, 1024 * k:1024 * (k + 1), :].T)
        m["XP"] = np.ascontiguousarray(hs[b, 0:1024, :].T)
        m["MSK"] = np.full((128, 1), float(k), dtype=np.float32)
        in_maps.append(m)

    nc = bass.Bass("TRN2", target_bir_lowering=False)
    build_program(nc)
    res = run_bass_kernel_spmd(nc, in_maps, core_ids=list(range(NCORE)))

    out = np.empty((Bsz, Lseq, D), dtype=np.float32)
    for c in range(NCORE):
        b, k = c // 2, c % 2
        out[b, 1024 * k:1024 * (k + 1), :] = np.asarray(res.results[c]["OUT"]).T
    return out


if __name__ == "__main__":
    import reference
    ins = reference.setup_inputs()
    ins = {k: np.asarray(v) for k, v in ins.items()}
    got = kernel(**ins)
    want = np.asarray(reference.reference(**ins))
    err = np.linalg.norm(got - want) / np.linalg.norm(want)
    print("rel_err:", err)


# version 19
# speedup vs baseline: 1.0319x; 1.0319x over previous
import numpy as np
import concourse.bass as bass
import concourse.mybir as mybir
from concourse.bass_utils import run_bass_kernel_spmd

F32 = mybir.dt.float32
BF16 = mybir.dt.bfloat16
AX = mybir.AxisListType
AF = mybir.ActivationFunctionType
ALU = mybir.AluOpType

Bsz, Lseq, D = 4, 2048, 2048
NCORE = 8
H, P, Gg, S = 32, 64, 8, 32
NPC = 37
EPS = 1e-5
TM = 1024
MTT = 256
PTT = 128
FE = 1408
PAGE = 64
ET, EV, ES = "tensor", "vector", "scalar"


class _Buf:
    __slots__ = ("writer", "readers", "excl")
    def __init__(self, excl=False):
        self.writer, self.readers, self.excl = None, [], excl


class _Op:
    __slots__ = ("eng", "build", "deps", "needed", "sem", "step", "val")


class Prog:
    def __init__(self):
        self.ops = []

    @staticmethod
    def buf(excl=False):
        return _Buf(excl)

    def emit(self, eng, build, r=(), w=(), dma=None, ndma=1):
        op = _Op()
        op.eng, op.build, op.needed = eng, build, False
        op.sem = dma
        op.step = 16 * ndma if dma else 1
        deps = set()
        for b in r:
            if b.excl:
                if b.writer is not None:
                    deps.add(b.writer)
                deps.update(b.readers)
                b.readers, b.writer = [], op
            else:
                if b.writer is not None:
                    deps.add(b.writer)
                b.readers.append(op)
        for b in w:
            if b.writer is not None:
                deps.add(b.writer)
            deps.update(b.readers)
            b.writer, b.readers = op, []
        deps.discard(op)
        op.deps = deps
        self.ops.append(op)
        return op

    def finalize(self):
        for op in self.ops:
            for d in op.deps:
                d.needed = True
        counts = {}
        for op in self.ops:
            key = op.sem if op.sem else ("e", op.eng)
            if op.needed or op.sem:
                counts[key] = counts.get(key, 0) + op.step
                op.val = counts[key]
            else:
                op.val = None
        return counts

    def stream(self, e, eng, sems, marks):
        wm = marks.setdefault(eng, {})
        for op in self.ops:
            if op.eng != eng:
                continue
            for d in sorted(op.deps, key=lambda o: (str(o.sem), o.val or 0)):
                if d.val is None:
                    continue
                key = d.sem if d.sem else ("e", d.eng)
                if wm.get(key, 0) >= d.val:
                    continue
                e.wait_ge(sems[key], d.val)
                wm[key] = d.val
            if op.sem:
                op.build(e, sems[op.sem])
            else:
                instr = op.build(e)
                if op.needed:
                    instr.then_inc(sems[("e", eng)], 1)


def bcast(col_ap, n):
    return bass.AP(col_ap.tensor, col_ap.offset, [list(col_ap.ap[0]), [0, n]])


def build_program(nc):
    pg = Prog()

    di = lambda n, s: nc.dram_tensor(n, s, F32, kind="ExternalInput").ap()
    XM = di("XM", [D, TM])
    XP = di("XP", [D, TM])
    MSK = di("MSK", [128, 1])
    WINB = di("WINB", [NPC, 128, D])
    F1B = di("F1B", [8, 11, 128, D])
    F2B = di("F2B", [8, 11, 128, D])
    PJB = di("PJB", [8, 16, 128, FE])
    SF1B = di("SF1B", [44, 128, D])
    SF2B = di("SF2B", [44, 128, D])
    SPB = di("SPB", [16, 128, 4 * FE])
    WOB = di("WOB", [16, 128, D])
    CONST_D = {}
    for nm, shp in [("GWC", [128, 128]), ("SGC", [128, 16]), ("BINC", [128, NPC]),
                    ("MUPC", [128, NPC]), ("CWC", [128, 80]), ("CBC", [128, 20]),
                    ("NWC", [128, 16]), ("BOC", [128, 16]), ("ALC", [32, 1]),
                    ("DTBC", [32, 1]), ("DROW", [1, 32]), ("IDN", [128, 128]),
                    ("MASKC", [128, 128]), ("CS8", [128, 64]), ("CS2", [128, 4]),
                    ("ONES1", [1, 512]), ("Z32", [32, 256])]:
        CONST_D[nm] = di(nm, shp)
    XNH = nc.dram_tensor("XNH", [D, TM], F32, kind="Internal").ap()
    XNH_H = nc.dram_tensor("XNH_H", [D, TM], BF16, kind="Internal").ap()
    XNH_L = nc.dram_tensor("XNH_L", [D, TM], BF16, kind="Internal").ap()
    OUT = nc.dram_tensor("OUT", [D, TM], F32, kind="ExternalOutput").ap()

    sb = lambda n, s: nc.alloc_sbuf_tensor(n, s, F32).ap()
    ACC = sb("ACC", [128, 16 * 1024])
    acc_b = [pg.buf() for _ in range(16)]
    WSLOT = sb("WSLOT", [128, 2 * 2048])
    ws_b = [pg.buf() for _ in range(4)]
    SALL = sb("SALL", [32, H * 64])
    s_b = [pg.buf() for _ in range(H)]
    TAIL = sb("TAIL", [128, 64])
    tail_b = [pg.buf() for _ in range(20)]
    C = {}
    for nm, ap in CONST_D.items():
        C[nm] = sb(nm + "_s", list(ap.tensor.shape))
    MSKC = sb("MSK_s", [128, 1])
    cN = pg.buf()
    NEGA = sb("NEGA", [32, 1]); nega_b = pg.buf()
    DALL = sb("DALL", [128, 32]); dall_b = pg.buf()

    ARENA_W = 29056
    AR = sb("AR", [128, ARENA_W])
    ar_pages = {}

    class Region:
        __slots__ = ("off", "width")
        def __init__(self, off, width):
            self.off, self.width = off, width
        def a(self, f0, f1, p0=0, p1=128):
            return AR[p0:p1, self.off + f0:self.off + f1]
        def col(self, f, p0=0, p1=128):
            return AR[p0:p1, self.off + f:self.off + f + 1]
        def b(self, f0=0, f1=None):
            if f1 is None:
                f1 = self.width
            lo, hi = (self.off + f0) // PAGE, (self.off + f1 - 1) // PAGE
            out = []
            for i in range(lo, hi + 1):
                if i not in ar_pages:
                    ar_pages[i] = pg.buf()
                out.append(ar_pages[i])
            return out

    ar_off = [0]
    def alloc(width):
        w = ((width + PAGE - 1) // PAGE) * PAGE
        r = Region(ar_off[0], w)
        ar_off[0] += w
        assert ar_off[0] <= ARENA_W, f"arena overflow {ar_off[0]}"
        return r
    mark = lambda: ar_off[0]
    def reset(m):
        ar_off[0] = m

    PS = [nc.alloc_psum_tensor(f"PS{i}", [128, 512], F32).ap() for i in range(8)]
    ps_b = [pg.buf(excl=True) for _ in range(8)]
    mmrot = [0]
    mmbanks = [[0, 1]]
    def mmbank():
        lst = mmbanks[0]
        b = lst[mmrot[0] % len(lst)]
        mmrot[0] += 1
        return b
    trrot = [0]
    def trbank():
        trrot[0] ^= 1
        return 3 + trrot[0]
    BK_G = 5
    BK_S = [5, 2]
    BK_LY = [6, 7]

    def ld_consts(e, sem):
        for nm in CONST_D:
            e.dma_start(out=C[nm], in_=CONST_D[nm]).then_inc(sem, 16)
        e.dma_start(out=MSKC, in_=MSK).then_inc(sem, 16)
    pg.emit("sync", ld_consts, w=[cN], dma="c", ndma=len(CONST_D) + 1)

    pg.emit(ES, lambda e: e.activation(NEGA, C["ALC"], AF.Exp), r=[cN], w=[nega_b])
    pg.emit(EV, lambda e: e.tensor_scalar_mul(NEGA, NEGA, -1.0), r=[nega_b], w=[nega_b])
    bk0 = mmbank()
    pg.emit(ET, lambda e, bk0=bk0: e.matmul(PS[bk0][:, 0:32], C["ONES1"][0:1, 0:128],
                                            C["DROW"], start=True, stop=True),
            r=[cN], w=[ps_b[bk0]])
    pg.emit(EV, lambda e, bk0=bk0: e.tensor_copy(DALL, PS[bk0][:, 0:32]),
            r=[ps_b[bk0]], w=[dall_b])
    pg.emit(EV, lambda e: e.memset(SALL, 0.0), w=s_b)
    pg.emit(EV, lambda e: e.memset(TAIL, 0.0), w=tail_b)

    wcount = [0]
    def wload(src_ap, width):
        f = wcount[0] % 2
        wcount[0] += 1
        pg.emit("sync", lambda e, sem, src=src_ap, f=f, width=width:
                e.dma_start(out=WSLOT[:, f * 2048:f * 2048 + width], in_=src)
                .then_inc(sem, 16), w=[ws_b[2 * f], ws_b[2 * f + 1]], dma=f"w{f}", ndma=1)
        return f

    def ln_phase(T, src_ap, src_bufs, sink, nhalf):
        statx = alloc(T); stats = alloc(T)
        mrow = alloc(T); rrow = alloc(T)
        tr1 = alloc(T); tr2 = alloc(T)
        mrep = alloc(T); rrep = alloc(T)
        sqs = alloc(2 * T)
        NW = T // nhalf
        for ci in range(16):
            sl = (ci % 2) * T
            pg.emit(ES, lambda e, ci=ci, sl=sl:
                    e.activation(sqs.a(sl, sl + T), src_ap(ci), AF.Square),
                    r=[src_bufs[ci]] if not isinstance(src_bufs[ci], list) else src_bufs[ci],
                    w=sqs.b(sl, sl + T))
            for n in range(nhalf):
                rb = [src_bufs[ci]] if not isinstance(src_bufs[ci], list) else list(src_bufs[ci])
                for acc_r, src_fn in ((statx, lambda ci, n: src_ap(ci)[:, n * NW:(n + 1) * NW]),
                                      (stats, lambda ci, n, sl=sl: sqs.a(sl, sl + T)[:, n * NW:(n + 1) * NW])):
                    bk = mmbank()
                    pg.emit(ET, lambda e, ci=ci, n=n, bk=bk, src_fn=src_fn:
                            e.matmul(PS[bk][0:1, 0:NW], C["CS2"][:, 0:1], src_fn(ci, n),
                                     start=True, stop=True),
                            r=rb + [cN] + sqs.b(sl, sl + T), w=[ps_b[bk]])
                    if ci == 0:
                        pg.emit(EV, lambda e, n=n, bk=bk, acc_r=acc_r:
                                e.tensor_copy(acc_r.a(n * NW, (n + 1) * NW, 0, 1),
                                              PS[bk][0:1, 0:NW]),
                                r=[ps_b[bk]], w=acc_r.b())
                    else:
                        pg.emit(EV, lambda e, n=n, bk=bk, acc_r=acc_r:
                                e.tensor_add(acc_r.a(n * NW, (n + 1) * NW, 0, 1),
                                             acc_r.a(n * NW, (n + 1) * NW, 0, 1),
                                             PS[bk][0:1, 0:NW]),
                                r=[ps_b[bk]] + acc_r.b(), w=acc_r.b())
        pg.emit(EV, lambda e: e.tensor_scalar_mul(mrow.a(0, T, 0, 1), statx.a(0, T, 0, 1), 1.0 / D),
                r=statx.b(), w=mrow.b())
        pg.emit(EV, lambda e: e.tensor_scalar_mul(tr1.a(0, T, 0, 1), stats.a(0, T, 0, 1), 1.0 / D),
                r=stats.b(), w=tr1.b())
        pg.emit(EV, lambda e: e.tensor_mul(tr2.a(0, T, 0, 1), mrow.a(0, T, 0, 1), mrow.a(0, T, 0, 1)),
                r=mrow.b(), w=tr2.b())
        pg.emit(EV, lambda e: e.scalar_tensor_tensor(tr1.a(0, T, 0, 1), tr2.a(0, T, 0, 1), -1.0,
                                                     tr1.a(0, T, 0, 1), ALU.mult, ALU.add),
                r=tr1.b() + tr2.b(), w=tr1.b())
        pg.emit(EV, lambda e: e.tensor_scalar_add(tr1.a(0, T, 0, 1), tr1.a(0, T, 0, 1), EPS),
                r=tr1.b(), w=tr1.b())
        pg.emit(EV, lambda e: e.reciprocal(tr1.a(0, T, 0, 1), tr1.a(0, T, 0, 1)),
                r=tr1.b(), w=tr1.b())
        pg.emit(ES, lambda e: e.activation(rrow.a(0, T, 0, 1), tr1.a(0, T, 0, 1), AF.Sqrt),
                r=tr1.b(), w=rrow.b())
        for n in range(nhalf):
            for row, rep in [(mrow, mrep), (rrow, rrep)]:
                bk = mmbank()
                pg.emit(ET, lambda e, n=n, bk=bk, row=row:
                        e.matmul(PS[bk][:, 0:NW], C["ONES1"][0:1, 0:128],
                                 row.a(n * NW, (n + 1) * NW, 0, 1), start=True, stop=True),
                        r=row.b() + [cN], w=[ps_b[bk]])
                pg.emit(EV, lambda e, n=n, bk=bk, rep=rep:
                        e.tensor_copy(rep.a(n * NW, (n + 1) * NW), PS[bk][:, 0:NW]),
                        r=[ps_b[bk]], w=rep.b())
        for ci in range(16):
            sink(ci, mrep, rrep)

    wbf = alloc(2 * 2048)
    wsb_b = [pg.buf() for _ in range(4)]

    def WHh(sl, kk, w=64):
        o = wbf.off + sl * 1024 + kk * 64
        return AR[:, o:o + w].bitcast(BF16)

    def WLh(sl, kk, w=64):
        o = wbf.off + sl * 1024 + 512 + kk * 64
        return AR[:, o:o + w].bitcast(BF16)

    def wload_h(src_half, widthh):
        sl = wcount[0] % 4
        wcount[0] += 1
        pg.emit("sync", lambda e, sem, src=src_half, sl=sl, widthh=widthh:
                e.dma_start(out=WSLOT[:, sl * 1024:sl * 1024 + widthh], in_=src)
                .then_inc(sem, 16), w=[ws_b[sl]], dma=f"h{sl}", ndma=1)
        wf = widthh // 2
        hi = AR[:, wbf.off + sl * 1024:wbf.off + sl * 1024 + wf].bitcast(BF16)
        lo = AR[:, wbf.off + sl * 1024 + 512:wbf.off + sl * 1024 + 512 + wf].bitcast(BF16)
        pg.emit("gpsimd", lambda e, sem, hi=hi, src=src_half:
                e.dma_start(out=hi, in_=src).then_inc(sem, 16),
                w=[wsb_b[sl]], dma=f"hh{sl}", ndma=1)
        pg.emit("gpsimd" if sl % 2 == 0 else EV,
                lambda e, lo=lo, sl=sl, widthh=widthh, hi=hi:
                e.tensor_sub(lo, WSLOT[:, sl * 1024:sl * 1024 + widthh], hi),
                r=[ws_b[sl], wsb_b[sl]], w=[wsb_b[sl]])
        return sl

    def wload_parts(src_full, Kc):
        ka = (Kc + 1) // 2
        return [(wload_h(src_full[:, 0:ka * 128], ka * 128), 0, ka),
                (wload_h(src_full[:, ka * 128:Kc * 128], (Kc - ka) * 128), ka, Kc - ka)]

    def emit_parts(parts, rh, rl, rhs_bufs, bk, N=512):
        for pi, (sl, k0, nk) in enumerate(parts):
            lastp = pi == len(parts) - 1
            rb = rhs_bufs[pi] if isinstance(rhs_bufs, tuple) else rhs_bufs
            def sub(e, sl=sl, k0=k0, nk=nk, firstp=(pi == 0), lastp=lastp, bk=bk, N=N):
                last = None
                for kk in range(nk):
                    ki = k0 + kk
                    e.matmul(PS[bk][:, 0:N], WHh(sl, kk), rh(ki),
                             start=(firstp and kk == 0), stop=False)
                    e.matmul(PS[bk][:, 0:N], WHh(sl, kk), rl(ki), start=False, stop=False)
                    last = e.matmul(PS[bk][:, 0:N], WLh(sl, kk), rh(ki),
                                    start=False, stop=(lastp and kk == nk - 1))
                return last
            pg.emit(ET, sub, r=[wsb_b[sl]] + rb, w=[ps_b[bk]])

    def ffn_group3(src_full, Kc, rh, rl, rhs_bufs, bk, N=512):
        emit_parts(wload_parts(src_full, Kc), rh, rl, rhs_bufs, bk, N)

    m0 = mark()
    for ci in range(16):
        pg.emit("sync", lambda e, sem, ci=ci:
                e.dma_start(out=ACC[:, ci * 1024:(ci + 1) * 1024],
                            in_=XM[ci * 128:(ci + 1) * 128, :]).then_inc(sem, 16),
                w=[acc_b[ci]], dma="x", ndma=1)

    xnh_b = [pg.buf() for _ in range(16)]
    lnm = mark()
    xnst = alloc(4 * 1024)
    xnsp = alloc(2 * 1024)

    def main_sink(ci, mrep, rrep):
        sl = (ci % 4) * 1024
        a = xnst.a(sl, sl + 1024)
        bufs = xnst.b(sl, sl + 1024)
        pg.emit(EV, lambda e, ci=ci, a=a:
                e.tensor_sub(a, ACC[:, ci * 1024:(ci + 1) * 1024], mrep.a(0, 1024)),
                r=[acc_b[ci]] + mrep.b(), w=bufs)
        pg.emit(EV, lambda e, a=a, rrep=rrep: e.tensor_mul(a, a, rrep.a(0, 1024)),
                r=bufs + rrep.b(), w=bufs)
        so = xnsp.off + (ci % 2) * 1024
        hi = AR[:, so:so + 512].bitcast(BF16)
        lo = AR[:, so + 512:so + 1024].bitcast(BF16)
        spb = xnsp.b((ci % 2) * 1024, (ci % 2) * 1024 + 1024)
        pg.emit(ES, lambda e, a=a, hi=hi: e.activation(hi, a, AF.Copy),
                r=bufs, w=spb)
        pg.emit(EV, lambda e, a=a, hi=hi, lo=lo: e.tensor_sub(lo, a, hi),
                r=bufs + spb, w=spb)
        def st(e, sem, ci=ci, hi=hi, lo=lo, a=a):
            e.dma_start(out=XNH[ci * 128:(ci + 1) * 128, :], in_=a).then_inc(sem, 16)
            e.dma_start(out=XNH_H[ci * 128:(ci + 1) * 128, :], in_=hi).then_inc(sem, 16)
            e.dma_start(out=XNH_L[ci * 128:(ci + 1) * 128, :], in_=lo).then_inc(sem, 16)
        pg.emit("gpsimd", st, r=spb + bufs, w=[xnh_b[ci]], dma=f"xo{ci % 2}", ndma=3)

    ln_phase(1024, lambda ci: ACC[:, ci * 1024:(ci + 1) * 1024], acc_b, main_sink, 2)
    reset(lnm)

    zA = alloc(21 * MTT)
    cv = alloc(20 * MTT)
    xnt = alloc(16 * MTT)
    xnt_c = lambda ki: AR[:, xnt.off + ki * 256:xnt.off + (ki + 1) * 256]
    xstm = alloc(2 * 2048)
    btm = alloc(2 * 256)
    dt6 = alloc(4 * MTT)
    vtm = alloc(2 * 64)
    lqr = alloc(64)
    bsc = alloc(64)
    aqb = alloc(64)
    gsb = alloc(128)
    cst = alloc(128)
    esb = alloc(4 * 128)
    abc = alloc(4 * 128)
    ytm = alloc(2 * 64)
    bes = alloc(2 * 64)
    ysq = alloc(2 * MTT)
    rsa = alloc(MTT)
    msc = mark()

    def proj_block(mi, T, rhs_ap, rhs_bufs, sink):
        sl = wload(WINB[mi], 2048)
        bk = mmbank()
        def grp(e, sl=sl, bk=bk, T=T):
            last = None
            for ki in range(16):
                last = e.matmul(PS[bk][:, 0:T],
                                WSLOT[:, sl * 2048 + ki * 128:sl * 2048 + (ki + 1) * 128],
                                rhs_ap(ki), start=(ki == 0), stop=(ki == 15))
            return last
        pg.emit(ET, grp, r=[ws_b[2 * sl], ws_b[2 * sl + 1]] + rhs_bufs, w=[ps_b[bk]])
        sink(mi, bk, T)

    def inproj(T, rhs_ap, rhs_bufs, mis, sink):
        for mi in mis:
            proj_block(mi, T, rhs_ap, rhs_bufs, sink)

    def sinkA(mi, bk, T):
        j = (mi - 16) if mi < 36 else 20
        pg.emit(EV, lambda e, j=j, mi=mi, bk=bk, T=T:
                e.tensor_scalar(zA.a(j * MTT, j * MTT + T), PS[bk][:, 0:T],
                                C["BINC"][:, mi:mi + 1], C["MUPC"][:, mi:mi + 1],
                                ALU.add, ALU.mult),
                r=[ps_b[bk], cN], w=zA.b(j * MTT, j * MTT + T))

    def sinkB(mi, bk, T):
        pg.emit(EV, lambda e, mi=mi, bk=bk, T=T:
                e.tensor_scalar(cv.a(mi * MTT, mi * MTT + T), PS[bk][:, 0:T],
                                C["BINC"][:, mi:mi + 1], C["MUPC"][:, mi:mi + 1],
                                ALU.add, ALU.mult),
                r=[ps_b[bk], cN], w=cv.b(mi * MTT, mi * MTT + T))

    def conv_chunk(j, T, mask_tail):
        za = zA.a(j * MTT, j * MTT + T)
        za_b = zA.b(j * MTT, j * MTT + T)
        cia = cvin.a(0, T + 3)
        cib = cvin.b(0, T + 64)
        cva = cv.a(j * MTT, j * MTT + T)
        cvb = cv.b(j * MTT, j * MTT + T)
        tl = TAIL[:, 3 * j:3 * j + 3]
        if mask_tail:
            pg.emit(EV, lambda e, tl=tl: e.tensor_scalar_mul(tl, tl, MSKC),
                    r=[tail_b[j], cN], w=[tail_b[j]])
        pg.emit(EV, lambda e, tl=tl, cia=cia: e.tensor_copy(cia[:, 0:3], tl),
                r=[tail_b[j]], w=cib)
        pg.emit(EV, lambda e, za=za, cia=cia, T=T: e.tensor_copy(cia[:, 3:3 + T], za),
                r=za_b, w=cib)
        pg.emit(EV, lambda e, za=za, tl=tl, T=T: e.tensor_copy(tl, za[:, T - 3:T]),
                r=za_b, w=[tail_b[j]])
        pg.emit(EV, lambda e, cia=cia, cva=cva, j=j, T=T:
                e.tensor_scalar_mul(cva, cia[:, 0:T], C["CWC"][:, 4 * j:4 * j + 1]),
                r=cib + [cN], w=cvb)
        for k in range(1, 4):
            pg.emit(EV, lambda e, cia=cia, cva=cva, j=j, k=k, T=T:
                    e.scalar_tensor_tensor(cva, cia[:, k:k + T],
                                           C["CWC"][:, 4 * j + k:4 * j + k + 1],
                                           cva, ALU.mult, ALU.add),
                    r=cib + [cN] + cvb, w=cvb)
        pg.emit(ES, lambda e, cva=cva, j=j:
                e.activation(cva, cva, AF.Silu, bias=C["CBC"][:, j:j + 1]),
                r=cvb + [cN], w=cvb)

    def tail_only(j, T):
        za = zA.a(j * MTT, j * MTT + T)
        za_b = zA.b(j * MTT, j * MTT + T)
        tl = TAIL[:, 3 * j:3 * j + 3]
        pg.emit(EV, lambda e, za=za, tl=tl, T=T: e.tensor_copy(tl, za[:, T - 3:T]),
                r=za_b, w=[tail_b[j]])

    def dt_pipeline(T, nchunk):
        xa = dt6.a(0, T, 0, 32)
        ta = dt6.a(MTT, MTT + T, 0, 32)
        da = dt6.a(2 * MTT, 2 * MTT + T, 0, 32)
        la = dt6.a(3 * MTT, 3 * MTT + T, 0, 32)
        xb = dt6.b(0, T); tb = dt6.b(MTT, MTT + T); db = dt6.b(2 * MTT, 2 * MTT + T)
        lb = dt6.b(3 * MTT, 3 * MTT + T)
        raw = zA.a(20 * MTT, 20 * MTT + T, 0, 32)
        rwb = zA.b(20 * MTT, 20 * MTT + T)
        pg.emit(EV, lambda e: e.tensor_scalar_add(xa, raw, C["DTBC"]), r=rwb + [cN], w=xb)
        pg.emit(ES, lambda e: e.activation(ta, raw, AF.Abs, bias=C["DTBC"]), r=rwb + [cN], w=tb)
        pg.emit(ES, lambda e: e.activation(ta, ta, AF.Exp, scale=-1.0), r=tb, w=tb)
        pg.emit(ES, lambda e: e.activation(ta, ta, AF.Ln, bias=1.0), r=tb, w=tb)
        pg.emit(EV, lambda e: e.tensor_scalar_max(da, xa, 0.0), r=xb, w=db)
        pg.emit(EV, lambda e: e.tensor_add(da, da, ta), r=db + tb, w=db)
        pg.emit(EV, lambda e: e.tensor_scalar_mul(xa, da, NEGA), r=db + [nega_b], w=xb)
        pg.emit(ES, lambda e: e.activation(ta, da, AF.Ln), r=db, w=tb)
        for c in range(nchunk):
            pg.emit(EV, lambda e, c=c: e.tensor_tensor_scan(
                la[:, c * 128:(c + 1) * 128], xa[:, c * 128:(c + 1) * 128],
                C["Z32"][:, 0:128], 0.0, ALU.add, ALU.add), r=xb + [cN], w=lb)
        pg.emit(EV, lambda e: e.tensor_sub(xa, la, ta), r=lb + tb, w=xb)
        return la, lb, xa, xb

    def transposes(nchunk, la, lb, va, vb):
        for c in range(nchunk):
            for fi in range(16):
                bk = trbank()
                f0 = fi * MTT + c * 128
                pg.emit(ET, lambda e, f0=f0, bk=bk:
                        e.transpose(PS[bk][:, 0:128], cv.a(f0, f0 + 128), C["IDN"]),
                        r=cv.b(f0, f0 + 128) + [cN], w=[ps_b[bk]])
                x0 = c * 2048 + fi * 128
                pg.emit(EV, lambda e, x0=x0, bk=bk:
                        e.tensor_copy(xstm.a(x0, x0 + 128), PS[bk][:, 0:128]),
                        r=[ps_b[bk]], w=xstm.b(x0, x0 + 128))
            for bi in range(2):
                bk = trbank()
                f0 = (16 + bi) * MTT + c * 128
                pg.emit(ET, lambda e, f0=f0, bk=bk:
                        e.transpose(PS[bk][:, 0:128], cv.a(f0, f0 + 128), C["IDN"]),
                        r=cv.b(f0, f0 + 128) + [cN], w=[ps_b[bk]])
                b0 = c * 256 + bi * 128
                pg.emit(EV, lambda e, b0=b0, bk=bk:
                        e.tensor_copy(btm.a(b0, b0 + 128), PS[bk][:, 0:128]),
                        r=[ps_b[bk]], w=btm.b(b0, b0 + 128))
            bk = trbank()
            pg.emit(ET, lambda e, c=c, bk=bk:
                    e.transpose(PS[bk][0:128, 0:32], va[:, c * 128:(c + 1) * 128],
                                C["IDN"][0:32, 0:32]),
                    r=vb + [cN], w=[ps_b[bk]])
            pg.emit(EV, lambda e, c=c, bk=bk:
                    e.tensor_copy(vtm.a(c * 64, c * 64 + 32), PS[bk][0:128, 0:32]),
                    r=[ps_b[bk]], w=vtm.b(c * 64, c * 64 + 64))

    def scan_chunk(c, la, lb, full, filler=None):
        lac = la[:, c * 128:(c + 1) * 128]
        pg.emit(ET, lambda e, lac=lac:
                e.transpose(PS[2][0:1, 0:32], lac[:, 127:128], C["IDN"][0:32, 0:32]),
                r=lb + [cN], w=[ps_b[2]])
        pg.emit(EV, lambda e: e.tensor_copy(lqr.a(0, 32, 0, 1), PS[2][0:1, 0:32]),
                r=[ps_b[2]], w=lqr.b())
        pg.emit(ET, lambda e:
                e.matmul(PS[2][0:32, 32:64], C["ONES1"][0:1, 0:32], lqr.a(0, 32, 0, 1),
                         start=True, stop=True), r=lqr.b() + [cN], w=[ps_b[2]])
        pg.emit(ES, lambda e:
                e.activation(aqb.a(0, 32, 0, 32), PS[2][0:32, 32:64], AF.Exp),
                r=[ps_b[2]], w=aqb.b())
        pg.emit(ET, lambda e:
                e.matmul(PS[2][:, 64:96], C["ONES1"][0:1, 0:128], lqr.a(0, 32, 0, 1),
                         start=True, stop=True), r=lqr.b() + [cN], w=[ps_b[2]])
        pg.emit(EV, lambda e, c=c:
                e.scalar_tensor_tensor(bsc.a(0, 32), vtm.a(c * 64, c * 64 + 32), -1.0,
                                       PS[2][:, 64:96], ALU.mult, ALU.add),
                r=vtm.b(c * 64, c * 64 + 64) + [ps_b[2]], w=bsc.b())
        pg.emit(ES, lambda e: e.activation(bsc.a(0, 32), bsc.a(0, 32), AF.Exp),
                r=bsc.b(), w=bsc.b())

        for g in range(Gg):
            if filler is not None:
                th = next(filler, None)
                if th is not None:
                    th()
            bch, boff = 16 + g // 4, 32 * (g % 4)
            bf0 = bch * MTT + c * 128
            b_fm = cv.a(bf0, bf0 + 128, boff, boff + 32)
            b_fm_b = cv.b(bf0, bf0 + 128)
            b_base = AR[0:128, btm.off + c * 256 + 32 * g:btm.off + c * 256 + 32 * g + 32]
            in0 = bass.AP(b_base.tensor, b_base.offset,
                          [list(b_base.ap[0]), [0, 4], [1, 32]])
            s_base = AR[0:128, bsc.off + 4 * g:bsc.off + 4 * g + 4]
            in1 = bass.AP(s_base.tensor, s_base.offset,
                          [list(s_base.ap[0]), [1, 4], [0, 32]])
            o_base = AR[0:128, bes.off:bes.off + 128]
            out3 = bass.AP(o_base.tensor, o_base.offset,
                           [list(o_base.ap[0]), [32, 4], [1, 32]])
            pg.emit(EV, lambda e, in0=in0, in1=in1, out3=out3:
                    e.tensor_mul(out3, in0, in1),
                    r=btm.b(c * 256, c * 256 + 256) + bsc.b(), w=bes.b())
            if full:
                cf0 = (18 + g // 4) * MTT + c * 128
                c_fm = cv.a(cf0, cf0 + 128, boff, boff + 32)
                c_fm_b = cv.b(cf0, cf0 + 128)
                pg.emit(ET, lambda e, b_fm=b_fm, c_fm=c_fm, boff=boff:
                        e.matmul(PS[2][:, 128:256], b_fm, c_fm, start=True, stop=True,
                                 tile_position=(boff, 0)),
                        r=b_fm_b + c_fm_b, w=[ps_b[2]])
                pg.emit(EV, lambda e: e.tensor_copy(gsb.a(0, 128), PS[2][:, 128:256]),
                        r=[ps_b[2]], w=gsb.b())
                pg.emit(ET, lambda e, c_fm=c_fm, boff=boff:
                        e.matmul(PS[2][0:32, 256:384],
                                 C["IDN"][boff:boff + 32, boff:boff + 32], c_fm,
                                 start=True, stop=True, tile_position=(boff, 0)),
                        r=c_fm_b + [cN], w=[ps_b[2]])
                pg.emit(EV, lambda e: e.tensor_copy(cst.a(0, 128, 0, 32),
                                                    PS[2][0:32, 256:384]),
                        r=[ps_b[2]], w=cst.b())
                for h in range(4 * g, 4 * g + 4):
                    par = h % 2
                    eo = 128 * (h % 4)
                    bly = BK_LY[par]
                    selh = bcast(C["IDN"][0:32, h:h + 1], 128)
                    pg.emit(ET, lambda e, selh=selh, lac=lac, bly=bly:
                            e.matmul(PS[bly][:, 0:128], selh, lac, start=True, stop=True),
                            r=lb + [cN], w=[ps_b[bly]])
                    vcol = vtm.col(c * 64 + h)
                    pg.emit(EV, lambda e, vcol=vcol, eo=eo, bly=bly:
                            e.scalar_tensor_tensor(esb.a(eo, eo + 128), PS[bly][:, 0:128],
                                                   vcol, C["MASKC"], ALU.subtract, ALU.add),
                            r=[ps_b[bly]] + vtm.b(c * 64, c * 64 + 64) + [cN],
                            w=esb.b(eo, eo + 128))
                    pg.emit(ES, lambda e, eo=eo, bly=bly:
                            e.activation(abc.a(eo, eo + 128, 0, 32), PS[bly][0:32, 0:128],
                                         AF.Exp),
                            r=[ps_b[bly]], w=abc.b(eo, eo + 128))
                    pg.emit(ES, lambda e, eo=eo:
                            e.activation(esb.a(eo, eo + 128), esb.a(eo, eo + 128), AF.Exp),
                            r=esb.b(eo, eo + 128), w=esb.b(eo, eo + 128))
                    pg.emit(EV, lambda e, eo=eo:
                            e.tensor_mul(esb.a(eo, eo + 128), esb.a(eo, eo + 128),
                                         gsb.a(0, 128)),
                            r=esb.b(eo, eo + 128) + gsb.b(), w=esb.b(eo, eo + 128))
                    pg.emit(EV, lambda e, eo=eo:
                            e.tensor_mul(abc.a(eo, eo + 128, 0, 32), cst.a(0, 128, 0, 32),
                                         abc.a(eo, eo + 128, 0, 32)),
                            r=cst.b() + abc.b(eo, eo + 128), w=abc.b(eo, eo + 128))
            for h in range(4 * g, 4 * g + 4):
                xs = xstm.a(c * 2048 + h * 64, c * 2048 + (h + 1) * 64)
                xs_b = xstm.b(c * 2048 + h * 64, c * 2048 + (h + 1) * 64)
                sl = SALL[:, h * 64:(h + 1) * 64]
                par = h % 2
                eo = 128 * (h % 4)
                if full:
                    bky = [3, 4][par]
                    def ygrp(e, xs=xs, sl=sl, eo=eo, bky=bky):
                        e.matmul(PS[bky][:, 0:64], esb.a(eo, eo + 128), xs,
                                 start=True, stop=False)
                        return e.matmul(PS[bky][:, 0:64], abc.a(eo, eo + 128, 0, 32), sl,
                                        start=False, stop=True)
                    pg.emit(ET, ygrp, r=esb.b(eo, eo + 128) + abc.b(eo, eo + 128) + xs_b
                            + [s_b[h]], w=[ps_b[bky]])
                    ysl = par * 64
                    ya = ytm.a(ysl, ysl + 64)
                    pg.emit(EV, lambda e, xs=xs, h=h, ya=ya, bky=bky:
                            e.scalar_tensor_tensor(ya, xs, DALL[:, h:h + 1],
                                                   PS[bky][:, 0:64], ALU.mult, ALU.add),
                            r=xs_b + [dall_b, ps_b[bky]], w=ytm.b(ysl, ysl + 64))
                    if par == 1:
                        pg.emit(ET, lambda e:
                                e.transpose(PS[2][:, 384:512], ytm.a(0, 128), C["IDN"]),
                                r=ytm.b(0, 128) + [cN], w=[ps_b[2]])
                        yf0 = (h // 2) * MTT + c * 128
                        pg.emit(EV, lambda e, yf0=yf0:
                                e.tensor_copy(zA.a(yf0, yf0 + 128), PS[2][:, 384:512]),
                                r=[ps_b[2]], w=zA.b(yf0, yf0 + 128))
                bo = 32 * (h % 4)
                pg.emit(ET, lambda e, xs=xs, bo=bo:
                        e.matmul(PS[5][0:32, 256:320], bes.a(bo, bo + 32), xs,
                                 start=True, stop=True),
                        r=bes.b(bo, bo + 64) + xs_b, w=[ps_b[5]])
                pg.emit(EV, lambda e, sl=sl, h=h:
                        e.scalar_tensor_tensor(sl, sl, aqb.col(h, 0, 32),
                                               PS[5][0:32, 256:320], ALU.mult, ALU.add),
                        r=[s_b[h], ps_b[5]] + aqb.b(), w=[s_b[h]])

    pln = alloc(7 * 256)
    cvin = Region(rsa.off, MTT + 64)
    sqst = alloc(256)

    def ln_stream(tok0):
        rg = lambda i: Region(pln.off + i * 256, 256)
        statx, stats, mrow, rrow, tr1, mrep, rrep = [rg(i) for i in range(7)]
        tr2 = rrow

        def chunk_dma(ci, stg, sl):
            pg.emit("sync", lambda e, sem, ci=ci, stg=stg:
                    e.dma_start(out=stg.a(0, 256),
                                in_=XP[ci * 128:(ci + 1) * 128, tok0:tok0 + 256])
                    .then_inc(sem, 16), w=stg.b(), dma=f"px{sl}", ndma=1)

        for ci in range(16):
            sl = ci % 2
            stg = Region(ysq.off + sl * 256, 256)
            sqg = Region(sqst.off, 256)
            chunk_dma(ci, stg, sl)
            pg.emit(ES, lambda e, stg=stg, sqg=sqg:
                    e.activation(sqg.a(0, 256), stg.a(0, 256), AF.Square),
                    r=stg.b(), w=sqg.b())
            for acc_r, src_r in ((statx, stg), (stats, sqg)):
                bk = mmbank()
                pg.emit(ET, lambda e, bk=bk, src_r=src_r:
                        e.matmul(PS[bk][0:1, 0:256], C["CS2"][:, 0:1], src_r.a(0, 256),
                                 start=True, stop=True),
                        r=src_r.b() + [cN], w=[ps_b[bk]])
                if ci == 0:
                    pg.emit(EV, lambda e, bk=bk, acc_r=acc_r:
                            e.tensor_copy(acc_r.a(0, 256, 0, 1), PS[bk][0:1, 0:256]),
                            r=[ps_b[bk]], w=acc_r.b())
                else:
                    pg.emit(EV, lambda e, bk=bk, acc_r=acc_r:
                            e.tensor_add(acc_r.a(0, 256, 0, 1), acc_r.a(0, 256, 0, 1),
                                         PS[bk][0:1, 0:256]),
                            r=[ps_b[bk]] + acc_r.b(), w=acc_r.b())
        pg.emit(EV, lambda e: e.tensor_scalar_mul(mrow.a(0, 256, 0, 1),
                                                  statx.a(0, 256, 0, 1), 1.0 / D),
                r=statx.b(), w=mrow.b())
        pg.emit(EV, lambda e: e.tensor_scalar_mul(tr1.a(0, 256, 0, 1),
                                                  stats.a(0, 256, 0, 1), 1.0 / D),
                r=stats.b(), w=tr1.b())
        pg.emit(EV, lambda e: e.tensor_mul(tr2.a(0, 256, 0, 1), mrow.a(0, 256, 0, 1),
                                           mrow.a(0, 256, 0, 1)),
                r=mrow.b(), w=tr2.b())
        pg.emit(EV, lambda e: e.scalar_tensor_tensor(tr1.a(0, 256, 0, 1),
                                                     tr2.a(0, 256, 0, 1), -1.0,
                                                     tr1.a(0, 256, 0, 1), ALU.mult, ALU.add),
                r=tr1.b() + tr2.b(), w=tr1.b())
        pg.emit(EV, lambda e: e.tensor_scalar_add(tr1.a(0, 256, 0, 1),
                                                  tr1.a(0, 256, 0, 1), EPS),
                r=tr1.b(), w=tr1.b())
        pg.emit(EV, lambda e: e.reciprocal(tr1.a(0, 256, 0, 1), tr1.a(0, 256, 0, 1)),
                r=tr1.b(), w=tr1.b())
        pg.emit(ES, lambda e: e.activation(rrow.a(0, 256, 0, 1), tr1.a(0, 256, 0, 1),
                                           AF.Sqrt),
                r=tr1.b(), w=rrow.b())
        for row, rep in ((mrow, mrep), (rrow, rrep)):
            bk = mmbank()
            pg.emit(ET, lambda e, bk=bk, row=row:
                    e.matmul(PS[bk][:, 0:256], C["ONES1"][0:1, 0:128],
                             row.a(0, 256, 0, 1), start=True, stop=True),
                    r=row.b() + [cN], w=[ps_b[bk]])
            pg.emit(EV, lambda e, bk=bk, rep=rep:
                    e.tensor_copy(rep.a(0, 256), PS[bk][:, 0:256]),
                    r=[ps_b[bk]], w=rep.b())
        for ci in range(16):
            sl = ci % 2
            stg = Region(ysq.off + sl * 256, 256)
            chunk_dma(ci, stg, sl)
            dst = xnt_c(ci)
            db = xnt.b(ci * 256, (ci + 1) * 256)
            pg.emit(EV, lambda e, stg=stg, dst=dst:
                    e.tensor_sub(dst, stg.a(0, 256), mrep.a(0, 256)),
                    r=stg.b() + mrep.b(), w=db)
            pg.emit(EV, lambda e, dst=dst: e.tensor_mul(dst, dst, rrep.a(0, 256)),
                    r=db + rrep.b(), w=db)

    PRE_MIS = list(range(16, 34)) + [36]
    ln_stream(0)
    for pt in range(4):
        mis = PRE_MIS if pt < 3 else (list(range(16, 36)) + [36])
        for mi in mis:
            proj_block(mi, 256, xnt_c, xnt.b(), sinkA)
            if 16 <= mi < 34:
                conv_chunk(mi - 16, 256, False)
            elif mi in (34, 35):
                tail_only(mi - 16, 256)
        if pt < 3:
            ln_stream((pt + 1) * 256)
        la, lb, va, vb = dt_pipeline(256, 2)
        transposes(2, la, lb, va, vb)
        for cch in range(2):
            scan_chunk(cch, la, lb, full=False)

    pg.emit(EV, lambda e: e.tensor_scalar_mul(SALL, SALL, MSKC[0:32, :]),
            r=s_b + [cN], w=s_b)

    def xt_loads_emit(mt):
        def xt_loads(e, sem, mt=mt):
            for ki in range(16):
                e.dma_start(out=xnt_c(ki),
                            in_=XNH[ki * 128:(ki + 1) * 128, mt * 256:(mt + 1) * 256]) \
                    .then_inc(sem, 16)
        pg.emit("sync", xt_loads, r=xnh_b, w=xnt.b(), dma="xt", ndma=16)

    xt_loads_emit(0)
    for mt in range(4):
        for mi in range(16, 37):
            proj_block(mi, 256, xnt_c, xnt.b(), sinkA)
            if mi < 36:
                conv_chunk(mi - 16, 256, mt == 0)
        la, lb, va, vb = dt_pipeline(256, 2)
        transposes(2, la, lb, va, vb)
        fillers = iter([
            (lambda mi=mi: proj_block(mi, 256, xnt_c, xnt.b(), sinkB))
            for mi in range(16)])
        for cch in range(2):
            scan_chunk(cch, la, lb, full=True, filler=fillers)
        for th in fillers:
            th()
        if mt < 3:
            xt_loads_emit(mt + 1)
        for ci in range(16):
            f0 = ci * MTT
            pg.emit(ES, lambda e, f0=f0: e.activation(cv.a(f0, f0 + 256), cv.a(f0, f0 + 256),
                                                      AF.Silu),
                    r=cv.b(f0, f0 + 256), w=cv.b(f0, f0 + 256))
            pg.emit(EV, lambda e, f0=f0:
                    e.tensor_mul(zA.a(f0, f0 + 256), zA.a(f0, f0 + 256), cv.a(f0, f0 + 256)),
                    r=zA.b(f0, f0 + 256) + cv.b(f0, f0 + 256), w=zA.b(f0, f0 + 256))
        for ci in range(16):
            f0 = ci * MTT
            sl = (ci % 2) * MTT
            pg.emit(ES, lambda e, f0=f0, sl=sl:
                    e.activation(ysq.a(sl, sl + 256), zA.a(f0, f0 + 256), AF.Square),
                    r=zA.b(f0, f0 + 256), w=ysq.b(sl, sl + 256))
            g = ci // 2
            bk = mmbank()
            pg.emit(ET, lambda e, sl=sl, g=g, bk=bk:
                    e.matmul(PS[bk][0:8, 0:256], C["CS8"][:, 8 * g:8 * g + 8],
                             ysq.a(sl, sl + 256), start=True, stop=True),
                    r=ysq.b(sl, sl + 256) + [cN], w=[ps_b[bk]])
            if ci == 0:
                pg.emit(EV, lambda e, bk=bk: e.tensor_copy(rsa.a(0, 256, 0, 8),
                                                           PS[bk][0:8, 0:256]),
                        r=[ps_b[bk]], w=rsa.b())
            else:
                pg.emit(EV, lambda e, bk=bk: e.tensor_add(rsa.a(0, 256, 0, 8),
                                                          rsa.a(0, 256, 0, 8),
                                                          PS[bk][0:8, 0:256]),
                        r=[ps_b[bk]] + rsa.b(), w=rsa.b())
        pg.emit(EV, lambda e: e.tensor_scalar_mul(rsa.a(0, 256, 0, 8), rsa.a(0, 256, 0, 8),
                                                  1.0 / 256.0), r=rsa.b(), w=rsa.b())
        pg.emit(EV, lambda e: e.tensor_scalar_add(rsa.a(0, 256, 0, 8), rsa.a(0, 256, 0, 8),
                                                  EPS), r=rsa.b(), w=rsa.b())
        pg.emit(EV, lambda e: e.reciprocal(rsa.a(0, 256, 0, 8), rsa.a(0, 256, 0, 8)),
                r=rsa.b(), w=rsa.b())
        pg.emit(ES, lambda e: e.activation(rsa.a(0, 256, 0, 8), rsa.a(0, 256, 0, 8), AF.Sqrt),
                r=rsa.b(), w=rsa.b())
        for g in range(8):
            bk = mmbank()
            selg = bcast(C["IDN"][0:8, g:g + 1], 128)
            pg.emit(ET, lambda e, bk=bk, selg=selg:
                    e.matmul(PS[bk][:, 0:256], selg, rsa.a(0, 256, 0, 8),
                             start=True, stop=True),
                    r=rsa.b() + [cN], w=[ps_b[bk]])
            for ci in (2 * g, 2 * g + 1):
                f0 = ci * MTT
                pg.emit(EV, lambda e, f0=f0, ci=ci, bk=bk:
                        e.scalar_tensor_tensor(zA.a(f0, f0 + 256), zA.a(f0, f0 + 256),
                                               C["NWC"][:, ci:ci + 1], PS[bk][:, 0:256],
                                               ALU.mult, ALU.mult),
                        r=zA.b(f0, f0 + 256) + [cN, ps_b[bk]], w=zA.b(f0, f0 + 256))
        for mo in range(16):
            sl = wload(WOB[mo], 2048)
            bk = mmbank()
            def wog(e, sl=sl, bk=bk):
                last = None
                for ki in range(16):
                    last = e.matmul(PS[bk][:, 0:256],
                                    WSLOT[:, sl * 2048 + ki * 128:sl * 2048 + (ki + 1) * 128],
                                    zA.a(ki * MTT, ki * MTT + 256),
                                    start=(ki == 0), stop=(ki == 15))
                return last
            pg.emit(ET, wog, r=[ws_b[2 * sl], ws_b[2 * sl + 1]] + zA.b(0, 16 * MTT), w=[ps_b[bk]])
            pg.emit(EV, lambda e, mo=mo, bk=bk, mt=mt:
                    e.scalar_tensor_tensor(ACC[:, mo * 1024 + mt * 256:mo * 1024 + (mt + 1) * 256],
                                           PS[bk][:, 0:256], C["BOC"][:, mo:mo + 1],
                                           ACC[:, mo * 1024 + mt * 256:mo * 1024 + (mt + 1) * 256],
                                           ALU.add, ALU.add),
                    r=[ps_b[bk], cN, acc_b[mo]], w=[acc_b[mo]])

    reset(m0)
    mmbanks[0] = [0, 1, 2, 3, 4, 5]
    ring = alloc(16 * 512)
    hbuf = alloc(11 * 1024)
    h1s = alloc(1024)
    wrep = alloc(1024)
    grep = alloc(1024)
    lfm = alloc(1024)
    wfm = alloc(1024)
    prb = alloc(64)
    ltm = Region(prb.off + 0, 8)
    p8a = Region(prb.off + 8, 8)
    cols = [Region(prb.off + 16 + i, 1) for i in range(6)]
    gwb = alloc(192)

    ring_h = lambda ki: AR[:, ring.off + ki * 256:ring.off + (ki + 1) * 256].bitcast(BF16)
    ring_l = lambda ki: AR[:, ring.off + 4096 + ki * 256:ring.off + 4096 + (ki + 1) * 256].bitcast(BF16)
    H_h = lambda fi, n: AR[:, hbuf.off + fi * 512 + n * 256:hbuf.off + fi * 512 + (n + 1) * 256].bitcast(BF16)
    H_l = lambda fi, n: AR[:, hbuf.off + 5632 + fi * 512 + n * 256:hbuf.off + 5632 + fi * 512 + (n + 1) * 256].bitcast(BF16)
    GWH = lambda ki: AR[:, gwb.off + ki * 4:gwb.off + ki * 4 + 4].bitcast(BF16)
    GWL = lambda ki: AR[:, gwb.off + 64 + ki * 4:gwb.off + 64 + ki * 4 + 4].bitcast(BF16)
    SGH = lambda ki: AR[:, gwb.off + 128:gwb.off + 136].bitcast(BF16)[:, ki:ki + 1]
    SGL = lambda ki: AR[:, gwb.off + 144:gwb.off + 152].bitcast(BF16)[:, ki:ki + 1]
    gw_hi = AR[:, gwb.off:gwb.off + 64].bitcast(BF16)
    gw_lo = AR[:, gwb.off + 64:gwb.off + 128].bitcast(BF16)
    sg_hi = AR[:, gwb.off + 128:gwb.off + 136].bitcast(BF16)
    sg_lo = AR[:, gwb.off + 144:gwb.off + 152].bitcast(BF16)
    pg.emit(ES, lambda e: e.activation(gw_hi, C["GWC"], AF.Copy), r=[cN], w=gwb.b())
    pg.emit(EV, lambda e: e.tensor_sub(gw_lo, C["GWC"], gw_hi), r=[cN] + gwb.b(), w=gwb.b())
    pg.emit(ES, lambda e: e.activation(sg_hi, C["SGC"], AF.Copy), r=[cN] + gwb.b(), w=gwb.b())
    pg.emit(EV, lambda e: e.tensor_sub(sg_lo, C["SGC"], sg_hi), r=[cN] + gwb.b(), w=gwb.b())

    ringA = ring.b(0, 2048) + ring.b(4096, 6144)
    ringB = ring.b(2048, 4096) + ring.b(6144, 8192)

    def ring_fill(n):
        for k0, wb in ((0, ringA), (8, ringB)):
            def fill(e, sem, n=n, k0=k0):
                for ki in range(k0, k0 + 8):
                    e.dma_start(out=ring_h(ki),
                                in_=XNH_H[ki * 128:(ki + 1) * 128, n * 512:(n + 1) * 512]) \
                        .then_inc(sem, 16)
                    e.dma_start(out=ring_l(ki),
                                in_=XNH_L[ki * 128:(ki + 1) * 128, n * 512:(n + 1) * 512]) \
                        .then_inc(sem, 16)
            pg.emit("sync", fill, r=xnh_b, w=wb, dma="xr", ndma=16)

    for n in range(2):
        ring_fill(n)
        bk = mmbank()
        def rlog(e, bk=bk):
            last = None
            for ki in range(16):
                e.matmul(PS[bk][0:8, 0:512], GWH(ki), ring_h(ki),
                         start=(ki == 0), stop=False)
                e.matmul(PS[bk][0:8, 0:512], GWH(ki), ring_l(ki), start=False, stop=False)
                last = e.matmul(PS[bk][0:8, 0:512], GWL(ki), ring_h(ki),
                                start=False, stop=(ki == 15))
            return last
        pg.emit(ET, rlog, r=ring.b() + gwb.b(), w=[ps_b[bk]])
        pg.emit(EV, lambda e, n=n, bk=bk:
                e.tensor_copy(lfm.a(n * 512, (n + 1) * 512, 0, 8), PS[bk][0:8, 0:512]),
                r=[ps_b[bk]], w=lfm.b(n * 512, (n + 1) * 512))
        bk = mmbank()
        def slog(e, bk=bk):
            last = None
            for ki in range(16):
                e.matmul(PS[bk][0:1, 0:512], SGH(ki), ring_h(ki),
                         start=(ki == 0), stop=False)
                e.matmul(PS[bk][0:1, 0:512], SGH(ki), ring_l(ki), start=False, stop=False)
                last = e.matmul(PS[bk][0:1, 0:512], SGL(ki), ring_h(ki),
                                start=False, stop=(ki == 15))
            return last
        pg.emit(ET, slog, r=ring.b() + gwb.b(), w=[ps_b[bk]])
        pg.emit(EV, lambda e, n=n, bk=bk:
                e.tensor_copy(wfm.a(n * 512, (n + 1) * 512, 0, 1), PS[bk][0:1, 0:512]),
                r=[ps_b[bk]], w=wfm.b(n * 512, (n + 1) * 512))
    pg.emit(ES, lambda e: e.activation(wfm.a(0, 1024, 0, 1), wfm.a(0, 1024, 0, 1),
                                       AF.Sigmoid),
            r=wfm.b(), w=wfm.b())
    for n in range(2):
        bk = mmbank()
        pg.emit(ET, lambda e, n=n, bk=bk:
                e.matmul(PS[bk][:, 0:512], C["ONES1"][0:1, 0:128],
                         wfm.a(n * 512, (n + 1) * 512, 0, 1), start=True, stop=True),
                r=wfm.b() + [cN], w=[ps_b[bk]])
        pg.emit(EV, lambda e, n=n, bk=bk:
                e.tensor_copy(grep.a(n * 512, (n + 1) * 512), PS[bk][:, 0:512]),
                r=[ps_b[bk]], w=grep.b())
    for tc in range(8):
        bk = trbank()
        pg.emit(ET, lambda e, tc=tc, bk=bk:
                e.transpose(PS[bk][0:128, 0:8], lfm.a(tc * 128, (tc + 1) * 128, 0, 8),
                            C["IDN"][0:8, 0:8]),
                r=lfm.b(tc * 128, (tc + 1) * 128) + [cN], w=[ps_b[bk]])
        pg.emit(EV, lambda e, bk=bk: e.tensor_copy(ltm.a(0, 8), PS[bk][0:128, 0:8]),
                r=[ps_b[bk]], w=ltm.b())
        c1, c2, c3, c4, c5, c6 = [cc.a(0, 1) for cc in cols]
        cb = [cc.b() for cc in cols]
        pg.emit(EV, lambda e, c1=c1: e.reduce_max(c1, ltm.a(0, 8), axis=AX.X),
                r=ltm.b(), w=cb[0])
        pg.emit(EV, lambda e, c1=c1, c2=c2: e.tensor_scalar_mul(c2, c1, -1.0),
                r=cb[0], w=cb[1])
        pg.emit(ES, lambda e, c2=c2: e.activation(ltm.a(0, 8), ltm.a(0, 8), AF.Exp, bias=c2),
                r=ltm.b() + cb[1], w=ltm.b())
        pg.emit(EV, lambda e, c3=c3: e.reduce_sum(c3, ltm.a(0, 8), axis=AX.X),
                r=ltm.b(), w=cb[2])
        pg.emit(EV, lambda e, c3=c3: e.reciprocal(c3, c3), r=cb[2], w=cb[2])
        pg.emit(EV, lambda e, c3=c3: e.tensor_scalar_mul(ltm.a(0, 8), ltm.a(0, 8), c3),
                r=ltm.b() + cb[2], w=ltm.b())
        pg.emit(EV, lambda e, c1=c1: e.reduce_max(c1, ltm.a(0, 8), axis=AX.X),
                r=ltm.b(), w=cb[0])
        pg.emit(EV, lambda e, c1=c1: e.tensor_scalar(p8a.a(0, 8), ltm.a(0, 8), c1, None,
                                                     ALU.is_equal),
                r=ltm.b() + cb[0], w=p8a.b())
        pg.emit(EV, lambda e, c4=c4: e.reduce_sum(c4, p8a.a(0, 8), axis=AX.X),
                r=p8a.b(), w=cb[3])
        pg.emit(EV, lambda e: e.scalar_tensor_tensor(p8a.a(0, 8), p8a.a(0, 8), -1e30,
                                                     ltm.a(0, 8), ALU.mult, ALU.add),
                r=p8a.b() + ltm.b(), w=p8a.b())
        pg.emit(EV, lambda e, c5=c5: e.reduce_max(c5, p8a.a(0, 8), axis=AX.X),
                r=p8a.b(), w=cb[4])
        pg.emit(EV, lambda e, c4=c4: e.tensor_scalar(c4, c4, 1.5, None, ALU.is_ge),
                r=cb[3], w=cb[3])
        pg.emit(EV, lambda e, c1=c1, c5=c5, c6=c6: e.tensor_sub(c6, c1, c5),
                r=cb[0] + cb[4], w=cb[5])
        pg.emit(EV, lambda e, c4=c4, c5=c5, c6=c6:
                e.scalar_tensor_tensor(c5, c4, c6, c5, ALU.mult, ALU.add),
                r=cb[3] + cb[5] + cb[4], w=cb[4])
        pg.emit(EV, lambda e, c5=c5: e.tensor_scalar(p8a.a(0, 8), ltm.a(0, 8), c5, None,
                                                     ALU.is_ge),
                r=ltm.b() + cb[4], w=p8a.b())
        pg.emit(EV, lambda e: e.tensor_mul(p8a.a(0, 8), p8a.a(0, 8), ltm.a(0, 8)),
                r=p8a.b() + ltm.b(), w=p8a.b())
        bk = trbank()
        pg.emit(ET, lambda e, bk=bk: e.transpose(PS[bk][0:8, 0:128], p8a.a(0, 8), C["IDN"]),
                r=p8a.b() + [cN], w=[ps_b[bk]])
        pg.emit(EV, lambda e, tc=tc, bk=bk:
                e.tensor_copy(lfm.a(tc * 128, (tc + 1) * 128, 0, 8), PS[bk][0:8, 0:128]),
                r=[ps_b[bk]], w=lfm.b(tc * 128, (tc + 1) * 128))
    ring_fill(0)


    def ffn_pass(fc1_blk, fc2_blk, proj_blk, prep_region, nfi, tag,
                 fill0_done=False, next_fill0=None):
        for n in range(2):
            if n == 1 or not fill0_done:
                ring_fill(n)
            for fi in range(nfi):
                bk1 = mmbank()
                ffn_group3(fc1_blk(fi), 16, ring_h, ring_l, (ringA, ringB), bk1)
                pg.emit(ES, lambda e, bk1=bk1: e.activation(h1s.a(0, 512), PS[bk1], AF.Silu),
                        r=[ps_b[bk1]], w=h1s.b(0, 512))
                bk2 = mmbank()
                ffn_group3(fc2_blk(fi), 16, ring_h, ring_l, (ringA, ringB), bk2)
                pg.emit(EV, lambda e, bk2=bk2:
                        e.tensor_mul(h1s.a(0, 512), h1s.a(0, 512), PS[bk2]),
                        r=h1s.b(0, 512) + [ps_b[bk2]], w=h1s.b(0, 512))
                pg.emit(EV, lambda e, n=n:
                        e.tensor_mul(h1s.a(0, 512), h1s.a(0, 512),
                                     prep_region.a(n * 512, (n + 1) * 512)),
                        r=h1s.b(0, 512) + prep_region.b(), w=h1s.b(0, 512))
                hb = (hbuf.b(fi * 512 + n * 256, fi * 512 + (n + 1) * 256)
                      + hbuf.b(5632 + fi * 512 + n * 256, 5632 + fi * 512 + (n + 1) * 256))
                pg.emit(ES, lambda e, fi=fi, n=n:
                        e.activation(H_h(fi, n), h1s.a(0, 512), AF.Copy),
                        r=h1s.b(0, 512), w=hb)
                pg.emit(EV, lambda e, fi=fi, n=n:
                        e.tensor_sub(H_l(fi, n), h1s.a(0, 512), H_h(fi, n)),
                        r=h1s.b(0, 512) + hb, w=hb)
        if next_fill0 is not None:
            next_fill0()
        for ci in range(16):
            parts = wload_parts(proj_blk(ci), nfi)
            for n in range(2):
                bk = mmbank()
                emit_parts(parts, lambda fi, n=n: H_h(fi, n),
                           lambda fi, n=n: H_l(fi, n), hbuf.b(), bk)
                pg.emit(EV, lambda e, ci=ci, n=n, bk=bk:
                        e.tensor_add(ACC[:, ci * 1024 + n * 512:ci * 1024 + (n + 1) * 512],
                                     ACC[:, ci * 1024 + n * 512:ci * 1024 + (n + 1) * 512],
                                     PS[bk]),
                        r=[acc_b[ci], ps_b[bk]], w=[acc_b[ci]])

    for ex in range(8):
        for n in range(2):
            bk = mmbank()
            sele = bcast(C["IDN"][0:8, ex:ex + 1], 128)
            pg.emit(ET, lambda e, n=n, bk=bk, sele=sele:
                    e.matmul(PS[bk][:, 0:512], sele, lfm.a(n * 512, (n + 1) * 512, 0, 8),
                             start=True, stop=True),
                    r=lfm.b() + [cN], w=[ps_b[bk]])
            pg.emit(EV, lambda e, n=n, bk=bk:
                    e.tensor_copy(wrep.a(n * 512, (n + 1) * 512), PS[bk][:, 0:512]),
                    r=[ps_b[bk]], w=wrep.b())
        ffn_pass(lambda fi, ex=ex: F1B[ex, fi], lambda fi, ex=ex: F2B[ex, fi],
                 lambda ci, ex=ex: PJB[ex, ci], wrep, 11, f"e{ex}",
                 fill0_done=True, next_fill0=lambda: ring_fill(0))

    for p in range(4):
        ffn_pass(lambda fi, p=p: SF1B[11 * p + fi], lambda fi, p=p: SF2B[11 * p + fi],
                 lambda ci, p=p: SPB[ci][:, p * FE:(p + 1) * FE], grep, 11, f"s{p}",
                 fill0_done=True, next_fill0=(lambda: ring_fill(0)) if p < 3 else None)

    out_b = pg.buf()

    out_bs = [pg.buf() for _ in range(16)]
    for ci in range(16):
        pg.emit("gpsimd", lambda e, sem, ci=ci:
                e.dma_start(out=OUT[ci * 128:(ci + 1) * 128, :],
                            in_=ACC[:, ci * 1024:(ci + 1) * 1024]).then_inc(sem, 16),
                r=[acc_b[ci]], w=[out_bs[ci]], dma="o", ndma=1)
    pg.emit("gpsimd", lambda e: None, r=out_bs)

    counts = pg.finalize()
    semkeys = sorted({op.sem for op in pg.ops if op.sem}) + \
              [("e", ET), ("e", EV), ("e", ES), ("e", "gpsimd"), ("e", "sync")]
    import contextlib
    with contextlib.ExitStack() as st:
        sems = {}
        for i, k in enumerate(semkeys):
            nm = k if isinstance(k, str) else f"es_{k[1]}"
            sems[k] = st.enter_context(nc.semaphore(f"sm_{nm}_{i}"))
        blk = st.enter_context(nc.Block())
        marks = {}
        blk.sync(lambda e: pg.stream(e, "sync", sems, marks))
        blk.gpsimd(lambda e: pg.stream(e, "gpsimd", sems, marks))
        blk.tensor(lambda e: pg.stream(e, ET, sems, marks))
        blk.vector(lambda e: pg.stream(e, EV, sems, marks))
        blk.scalar(lambda e: pg.stream(e, ES, sems, marks))
    return nc



def _blockify(W):
    K, M = W.shape
    Kc, Mc = K // 128, M // 128
    return np.ascontiguousarray(
        W.reshape(Kc, 128, Mc, 128).transpose(2, 1, 0, 3).reshape(Mc, 128, K))


def kernel(**inputs):
    f = lambda k: np.asarray(inputs[k], dtype=np.float32)
    hs = f("L0_attn_hidden_states")
    W_in = f("L0_attn_W_in"); b_in = f("L0_attn_b_in")
    conv_w = f("L0_attn_conv_w"); conv_b = f("L0_attn_conv_b")
    A_log = f("L0_attn_A_log"); D_param = f("L0_attn_D_param")
    dt_bias = f("L0_attn_dt_bias"); norm_w = f("L0_attn_norm_w")
    W_out = f("L0_attn_W_out"); b_out = f("L0_attn_b_out")
    mup = f("L0_attn_mup_vector")
    gate_w = f("L0_ffn_gate_w")
    efc1 = f("L0_ffn_expert_fc1"); efc2 = f("L0_ffn_expert_fc2")
    eproj = f("L0_ffn_expert_proj")
    sfc1 = f("L0_ffn_shared_fc1"); sfc2 = f("L0_ffn_shared_fc2")
    sproj = f("L0_ffn_shared_proj"); sgw = f("L0_ffn_shared_gate_w")

    Wp = np.zeros((D, NPC * 128), dtype=np.float32)
    Wp[:, :4640] = W_in
    WINB = _blockify(Wp)
    F1B = np.stack([_blockify(np.ascontiguousarray(efc1[e].T)) for e in range(8)])
    F2B = np.stack([_blockify(np.ascontiguousarray(efc2[e].T)) for e in range(8)])
    PJB = np.stack([_blockify(np.ascontiguousarray(eproj[e].T)) for e in range(8)])
    SF1B = _blockify(np.ascontiguousarray(sfc1.T))
    SF2B = _blockify(np.ascontiguousarray(sfc2.T))
    SPB = _blockify(np.ascontiguousarray(sproj.T))
    WOB = _blockify(W_out)
    GWC = np.ascontiguousarray(
        gate_w.T.reshape(16, 128, 8).transpose(1, 0, 2).reshape(128, 128))
    SGC = np.ascontiguousarray(sgw.reshape(16, 128).T)
    vpad = lambda v, n: np.ascontiguousarray(
        np.pad(v, (0, n * 128 - v.shape[0])).reshape(n, 128).T)
    BINC = vpad(b_in, NPC); MUPC = vpad(mup, NPC)
    CWC = np.ascontiguousarray(conv_w.reshape(20, 128, 4).transpose(1, 0, 2).reshape(128, 80))
    CBC = vpad(conv_b, 20); NWC = vpad(norm_w, 16); BOC = vpad(b_out, 16)
    ALC = A_log.reshape(32, 1).copy()
    DTBC = dt_bias.reshape(32, 1).copy()
    DROW = D_param.reshape(1, 32).copy()
    IDN = np.eye(128, dtype=np.float32)
    jj, tt = np.meshgrid(np.arange(128), np.arange(128), indexing="ij")
    MASKC = np.where(jj <= tt, 0.0, -1e30).astype(np.float32)
    CS8 = np.zeros((128, 64), dtype=np.float32)
    for g in range(8):
        CS8[:, 8 * g + g] = 1.0
    CS2 = np.zeros((128, 4), dtype=np.float32)
    CS2[:, 0] = 1.0
    CS2[:, 3] = 1.0
    ONES1 = np.ones((1, 512), dtype=np.float32)
    Z32 = np.zeros((32, 256), dtype=np.float32)

    shared = dict(WINB=WINB, F1B=F1B, F2B=F2B, PJB=PJB, SF1B=SF1B, SF2B=SF2B,
                  SPB=SPB, WOB=WOB, GWC=GWC, SGC=SGC, BINC=BINC, MUPC=MUPC,
                  CWC=CWC, CBC=CBC, NWC=NWC, BOC=BOC, ALC=ALC, DTBC=DTBC,
                  DROW=DROW, IDN=IDN, MASKC=MASKC, CS8=CS8, CS2=CS2,
                  ONES1=ONES1, Z32=Z32)

    in_maps = []
    for c in range(NCORE):
        b, k = c // 2, c % 2
        m = dict(shared)
        m["XM"] = np.ascontiguousarray(hs[b, 1024 * k:1024 * (k + 1), :].T)
        m["XP"] = np.ascontiguousarray(hs[b, 0:1024, :].T)
        m["MSK"] = np.full((128, 1), float(k), dtype=np.float32)
        in_maps.append(m)

    nc = bass.Bass("TRN2", target_bir_lowering=False)
    build_program(nc)
    res = run_bass_kernel_spmd(nc, in_maps, core_ids=list(range(NCORE)))

    out = np.empty((Bsz, Lseq, D), dtype=np.float32)
    for c in range(NCORE):
        b, k = c // 2, c % 2
        out[b, 1024 * k:1024 * (k + 1), :] = np.asarray(res.results[c]["OUT"]).T
    return out


if __name__ == "__main__":
    import reference
    ins = reference.setup_inputs()
    ins = {k: np.asarray(v) for k, v in ins.items()}
    got = kernel(**ins)
    want = np.asarray(reference.reference(**ins))
    err = np.linalg.norm(got - want) / np.linalg.norm(want)
    print("rel_err:", err)


# version 20
# speedup vs baseline: 1.0996x; 1.0656x over previous
import numpy as np
import concourse.bass as bass
import concourse.mybir as mybir
from concourse.bass_utils import run_bass_kernel_spmd

F32 = mybir.dt.float32
BF16 = mybir.dt.bfloat16
AX = mybir.AxisListType
AF = mybir.ActivationFunctionType
ALU = mybir.AluOpType

Bsz, Lseq, D = 4, 2048, 2048
NCORE = 8
H, P, Gg, S = 32, 64, 8, 32
NPC = 37
EPS = 1e-5
TM = 1024
MTT = 256
PTT = 128
FE = 1408
PAGE = 64
ET, EV, ES = "tensor", "vector", "scalar"


class _Buf:
    __slots__ = ("writer", "readers", "excl")
    def __init__(self, excl=False):
        self.writer, self.readers, self.excl = None, [], excl


class _Op:
    __slots__ = ("eng", "build", "deps", "needed", "sem", "step", "val")


class Prog:
    def __init__(self):
        self.ops = []

    @staticmethod
    def buf(excl=False):
        return _Buf(excl)

    def emit(self, eng, build, r=(), w=(), dma=None, ndma=1):
        op = _Op()
        op.eng, op.build, op.needed = eng, build, False
        op.sem = dma
        op.step = 16 * ndma if dma else 1
        deps = set()
        for b in r:
            if b.excl:
                if b.writer is not None:
                    deps.add(b.writer)
                deps.update(b.readers)
                b.readers, b.writer = [], op
            else:
                if b.writer is not None:
                    deps.add(b.writer)
                b.readers.append(op)
        for b in w:
            if b.writer is not None:
                deps.add(b.writer)
            deps.update(b.readers)
            b.writer, b.readers = op, []
        deps.discard(op)
        op.deps = deps
        self.ops.append(op)
        return op

    def finalize(self):
        for op in self.ops:
            for d in op.deps:
                d.needed = True
        counts = {}
        for op in self.ops:
            key = op.sem if op.sem else ("e", op.eng)
            if op.needed or op.sem:
                counts[key] = counts.get(key, 0) + op.step
                op.val = counts[key]
            else:
                op.val = None
        return counts

    def stream(self, e, eng, sems, marks):
        wm = marks.setdefault(eng, {})
        for op in self.ops:
            if op.eng != eng:
                continue
            for d in sorted(op.deps, key=lambda o: (str(o.sem), o.val or 0)):
                if d.val is None:
                    continue
                key = d.sem if d.sem else ("e", d.eng)
                if wm.get(key, 0) >= d.val:
                    continue
                e.wait_ge(sems[key], d.val)
                wm[key] = d.val
            if op.sem:
                op.build(e, sems[op.sem])
            else:
                instr = op.build(e)
                if op.needed:
                    instr.then_inc(sems[("e", eng)], 1)


def bcast(col_ap, n):
    return bass.AP(col_ap.tensor, col_ap.offset, [list(col_ap.ap[0]), [0, n]])


def build_program(nc):
    pg = Prog()

    di = lambda n, s: nc.dram_tensor(n, s, F32, kind="ExternalInput").ap()
    XM = di("XM", [D, TM])
    XP = di("XP", [D, TM])
    MSK = di("MSK", [128, 1])
    WINB = di("WINB", [NPC, 128, D])
    F1B = di("F1B", [8, 11, 128, D])
    F2B = di("F2B", [8, 11, 128, D])
    PJB = di("PJB", [8, 16, 128, FE])
    SF1B = di("SF1B", [44, 128, D])
    SF2B = di("SF2B", [44, 128, D])
    SPB = di("SPB", [16, 128, 4 * FE])
    WOB = di("WOB", [16, 128, D])
    CONST_D = {}
    for nm, shp in [("GWC", [128, 128]), ("SGC", [128, 16]), ("BINC", [128, NPC]),
                    ("MUPC", [128, NPC]), ("CWC", [128, 80]), ("CBC", [128, 20]),
                    ("NWC", [128, 16]), ("BOC", [128, 16]), ("ALC", [32, 1]),
                    ("DTBC", [32, 1]), ("DROW", [1, 32]), ("IDN", [128, 128]),
                    ("MASKC", [128, 128]), ("CS8", [128, 64]), ("CS2", [128, 4]),
                    ("ONES1", [1, 512]), ("Z32", [32, 256])]:
        CONST_D[nm] = di(nm, shp)
    XNH = nc.dram_tensor("XNH", [D, TM], F32, kind="Internal").ap()
    XNH_H = nc.dram_tensor("XNH_H", [D, TM], BF16, kind="Internal").ap()
    XNH_L = nc.dram_tensor("XNH_L", [D, TM], BF16, kind="Internal").ap()
    OUT = nc.dram_tensor("OUT", [D, TM], F32, kind="ExternalOutput").ap()

    sb = lambda n, s: nc.alloc_sbuf_tensor(n, s, F32).ap()
    ACC = sb("ACC", [128, 16 * 1024])
    acc_b = [pg.buf() for _ in range(16)]
    WSLOT = sb("WSLOT", [128, 2 * 2048])
    ws_b = [pg.buf() for _ in range(4)]
    SALL = sb("SALL", [32, H * 64])
    s_b = [pg.buf() for _ in range(H)]
    TAIL = sb("TAIL", [128, 64])
    tail_b = [pg.buf() for _ in range(20)]
    C = {}
    for nm, ap in CONST_D.items():
        C[nm] = sb(nm + "_s", list(ap.tensor.shape))
    MSKC = sb("MSK_s", [128, 1])
    cN = pg.buf()
    NEGA = sb("NEGA", [32, 1]); nega_b = pg.buf()
    DALL = sb("DALL", [128, 32]); dall_b = pg.buf()

    ARENA_W = 29056
    AR = sb("AR", [128, ARENA_W])
    ar_pages = {}

    class Region:
        __slots__ = ("off", "width")
        def __init__(self, off, width):
            self.off, self.width = off, width
        def a(self, f0, f1, p0=0, p1=128):
            return AR[p0:p1, self.off + f0:self.off + f1]
        def col(self, f, p0=0, p1=128):
            return AR[p0:p1, self.off + f:self.off + f + 1]
        def b(self, f0=0, f1=None):
            if f1 is None:
                f1 = self.width
            lo, hi = (self.off + f0) // PAGE, (self.off + f1 - 1) // PAGE
            out = []
            for i in range(lo, hi + 1):
                if i not in ar_pages:
                    ar_pages[i] = pg.buf()
                out.append(ar_pages[i])
            return out

    ar_off = [0]
    def alloc(width):
        w = ((width + PAGE - 1) // PAGE) * PAGE
        r = Region(ar_off[0], w)
        ar_off[0] += w
        assert ar_off[0] <= ARENA_W, f"arena overflow {ar_off[0]}"
        return r
    mark = lambda: ar_off[0]
    def reset(m):
        ar_off[0] = m

    PS = [nc.alloc_psum_tensor(f"PS{i}", [128, 512], F32).ap() for i in range(8)]
    ps_b = [pg.buf(excl=True) for _ in range(8)]
    mmrot = [0]
    mmbanks = [[0, 1]]
    def mmbank():
        lst = mmbanks[0]
        b = lst[mmrot[0] % len(lst)]
        mmrot[0] += 1
        return b
    trrot = [0]
    def trbank():
        trrot[0] ^= 1
        return 3 + trrot[0]
    BK_G = 5
    BK_S = [5, 2]
    BK_LY = [6, 7]

    def ld_consts(e, sem):
        for nm in CONST_D:
            e.dma_start(out=C[nm], in_=CONST_D[nm]).then_inc(sem, 16)
        e.dma_start(out=MSKC, in_=MSK).then_inc(sem, 16)
    pg.emit("sync", ld_consts, w=[cN], dma="c", ndma=len(CONST_D) + 1)

    pg.emit(ES, lambda e: e.activation(NEGA, C["ALC"], AF.Exp), r=[cN], w=[nega_b])
    pg.emit(EV, lambda e: e.tensor_scalar_mul(NEGA, NEGA, -1.0), r=[nega_b], w=[nega_b])
    bk0 = mmbank()
    pg.emit(ET, lambda e, bk0=bk0: e.matmul(PS[bk0][:, 0:32], C["ONES1"][0:1, 0:128],
                                            C["DROW"], start=True, stop=True),
            r=[cN], w=[ps_b[bk0]])
    pg.emit(EV, lambda e, bk0=bk0: e.tensor_copy(DALL, PS[bk0][:, 0:32]),
            r=[ps_b[bk0]], w=[dall_b])
    pg.emit(EV, lambda e: e.memset(SALL, 0.0), w=s_b)
    pg.emit(EV, lambda e: e.memset(TAIL, 0.0), w=tail_b)

    wcount = [0]
    def wload(src_ap, width):
        f = wcount[0] % 2
        wcount[0] += 1
        pg.emit("sync", lambda e, sem, src=src_ap, f=f, width=width:
                e.dma_start(out=WSLOT[:, f * 2048:f * 2048 + width], in_=src)
                .then_inc(sem, 16), w=[ws_b[2 * f], ws_b[2 * f + 1]], dma=f"w{f}", ndma=1)
        return f

    def ln_phase(T, src_ap, src_bufs, sink, nhalf):
        statx = alloc(T); stats = alloc(T)
        mrow = alloc(T); rrow = alloc(T)
        tr1 = alloc(T); tr2 = alloc(T)
        mrep = alloc(T); rrep = alloc(T)
        sqs = alloc(2 * T)
        NW = T // nhalf
        for ci in range(16):
            sl = (ci % 2) * T
            pg.emit(ES, lambda e, ci=ci, sl=sl:
                    e.activation(sqs.a(sl, sl + T), src_ap(ci), AF.Square),
                    r=[src_bufs[ci]] if not isinstance(src_bufs[ci], list) else src_bufs[ci],
                    w=sqs.b(sl, sl + T))
            for n in range(nhalf):
                rb = [src_bufs[ci]] if not isinstance(src_bufs[ci], list) else list(src_bufs[ci])
                for acc_r, src_fn in ((statx, lambda ci, n: src_ap(ci)[:, n * NW:(n + 1) * NW]),
                                      (stats, lambda ci, n, sl=sl: sqs.a(sl, sl + T)[:, n * NW:(n + 1) * NW])):
                    bk = mmbank()
                    pg.emit(ET, lambda e, ci=ci, n=n, bk=bk, src_fn=src_fn:
                            e.matmul(PS[bk][0:1, 0:NW], C["CS2"][:, 0:1], src_fn(ci, n),
                                     start=True, stop=True),
                            r=rb + [cN] + sqs.b(sl, sl + T), w=[ps_b[bk]])
                    if ci == 0:
                        pg.emit(EV, lambda e, n=n, bk=bk, acc_r=acc_r:
                                e.tensor_copy(acc_r.a(n * NW, (n + 1) * NW, 0, 1),
                                              PS[bk][0:1, 0:NW]),
                                r=[ps_b[bk]], w=acc_r.b())
                    else:
                        pg.emit(EV, lambda e, n=n, bk=bk, acc_r=acc_r:
                                e.tensor_add(acc_r.a(n * NW, (n + 1) * NW, 0, 1),
                                             acc_r.a(n * NW, (n + 1) * NW, 0, 1),
                                             PS[bk][0:1, 0:NW]),
                                r=[ps_b[bk]] + acc_r.b(), w=acc_r.b())
        pg.emit(EV, lambda e: e.tensor_scalar_mul(mrow.a(0, T, 0, 1), statx.a(0, T, 0, 1), 1.0 / D),
                r=statx.b(), w=mrow.b())
        pg.emit(EV, lambda e: e.tensor_scalar_mul(tr1.a(0, T, 0, 1), stats.a(0, T, 0, 1), 1.0 / D),
                r=stats.b(), w=tr1.b())
        pg.emit(EV, lambda e: e.tensor_mul(tr2.a(0, T, 0, 1), mrow.a(0, T, 0, 1), mrow.a(0, T, 0, 1)),
                r=mrow.b(), w=tr2.b())
        pg.emit(EV, lambda e: e.scalar_tensor_tensor(tr1.a(0, T, 0, 1), tr2.a(0, T, 0, 1), -1.0,
                                                     tr1.a(0, T, 0, 1), ALU.mult, ALU.add),
                r=tr1.b() + tr2.b(), w=tr1.b())
        pg.emit(EV, lambda e: e.tensor_scalar_add(tr1.a(0, T, 0, 1), tr1.a(0, T, 0, 1), EPS),
                r=tr1.b(), w=tr1.b())
        pg.emit(EV, lambda e: e.reciprocal(tr1.a(0, T, 0, 1), tr1.a(0, T, 0, 1)),
                r=tr1.b(), w=tr1.b())
        pg.emit(ES, lambda e: e.activation(rrow.a(0, T, 0, 1), tr1.a(0, T, 0, 1), AF.Sqrt),
                r=tr1.b(), w=rrow.b())
        for n in range(nhalf):
            for row, rep in [(mrow, mrep), (rrow, rrep)]:
                bk = mmbank()
                pg.emit(ET, lambda e, n=n, bk=bk, row=row:
                        e.matmul(PS[bk][:, 0:NW], C["ONES1"][0:1, 0:128],
                                 row.a(n * NW, (n + 1) * NW, 0, 1), start=True, stop=True),
                        r=row.b() + [cN], w=[ps_b[bk]])
                pg.emit(EV, lambda e, n=n, bk=bk, rep=rep:
                        e.tensor_copy(rep.a(n * NW, (n + 1) * NW), PS[bk][:, 0:NW]),
                        r=[ps_b[bk]], w=rep.b())
        for ci in range(16):
            sink(ci, mrep, rrep)

    wbf = alloc(2 * 2048)
    wsb_b = [pg.buf() for _ in range(4)]

    def WHh(sl, kk, w=64):
        o = wbf.off + sl * 1024 + kk * 64
        return AR[:, o:o + w].bitcast(BF16)

    def WLh(sl, kk, w=64):
        o = wbf.off + sl * 1024 + 512 + kk * 64
        return AR[:, o:o + w].bitcast(BF16)

    def wload_h(src_half, widthh):
        sl = wcount[0] % 4
        wcount[0] += 1
        pg.emit("sync", lambda e, sem, src=src_half, sl=sl, widthh=widthh:
                e.dma_start(out=WSLOT[:, sl * 1024:sl * 1024 + widthh], in_=src)
                .then_inc(sem, 16), w=[ws_b[sl]], dma=f"h{sl}", ndma=1)
        wf = widthh // 2
        hi = AR[:, wbf.off + sl * 1024:wbf.off + sl * 1024 + wf].bitcast(BF16)
        lo = AR[:, wbf.off + sl * 1024 + 512:wbf.off + sl * 1024 + 512 + wf].bitcast(BF16)
        pg.emit("gpsimd", lambda e, sem, hi=hi, src=src_half:
                e.dma_start(out=hi, in_=src).then_inc(sem, 16),
                w=[wsb_b[sl]], dma=f"hh{sl}", ndma=1)
        pg.emit("gpsimd" if sl % 2 == 0 else EV,
                lambda e, lo=lo, sl=sl, widthh=widthh, hi=hi:
                e.tensor_sub(lo, WSLOT[:, sl * 1024:sl * 1024 + widthh], hi),
                r=[ws_b[sl], wsb_b[sl]], w=[wsb_b[sl]])
        return sl

    def wload_parts(src_full, Kc):
        ka = (Kc + 1) // 2
        return [(wload_h(src_full[:, 0:ka * 128], ka * 128), 0, ka),
                (wload_h(src_full[:, ka * 128:Kc * 128], (Kc - ka) * 128), ka, Kc - ka)]

    def emit_parts(parts, rh, rl, rhs_bufs, bk, N=512, qbufs=None):
        for pi, (sl, k0, nk) in enumerate(parts):
            lastp = pi == len(parts) - 1
            rb = rhs_bufs[pi] if isinstance(rhs_bufs, tuple) else rhs_bufs
            qs = [(q, min(q + 4, nk)) for q in range(0, nk, 4)] \
                if (pi == 0 and qbufs is not None) else [(0, nk)]
            for q0, q1 in qs:
                rbq = qbufs(k0 + q0, k0 + q1) if (pi == 0 and qbufs is not None) else rb
                def sub(e, sl=sl, k0=k0, q0=q0, q1=q1, bk=bk, N=N,
                        first=(pi == 0 and q0 == 0), last_=(lastp and q1 == nk)):
                    last = None
                    for kk in range(q0, q1):
                        ki = k0 + kk
                        e.matmul(PS[bk][:, 0:N], WHh(sl, kk), rh(ki),
                                 start=(first and kk == q0), stop=False)
                        e.matmul(PS[bk][:, 0:N], WHh(sl, kk), rl(ki),
                                 start=False, stop=False)
                        last = e.matmul(PS[bk][:, 0:N], WLh(sl, kk), rh(ki),
                                        start=False, stop=(last_ and kk == q1 - 1))
                    return last
                pg.emit(ET, sub, r=[wsb_b[sl]] + rbq, w=[ps_b[bk]])

    def ffn_group3(src_full, Kc, rh, rl, rhs_bufs, bk, N=512, qbufs=None):
        emit_parts(wload_parts(src_full, Kc), rh, rl, rhs_bufs, bk, N, qbufs=qbufs)

    m0 = mark()
    def x_loads(e, sem):
        for ci in range(16):
            e.dma_start(out=ACC[:, ci * 1024:(ci + 1) * 1024],
                        in_=XM[ci * 128:(ci + 1) * 128, :]).then_inc(sem, 16)
    pg.emit("sync", x_loads, w=acc_b, dma="x", ndma=16)

    xnh_b = [pg.buf() for _ in range(16)]
    lnm = mark()
    xnst = alloc(4 * 1024)
    xnsp = alloc(2 * 1024)

    def main_sink(ci, mrep, rrep):
        sl = (ci % 4) * 1024
        a = xnst.a(sl, sl + 1024)
        bufs = xnst.b(sl, sl + 1024)
        pg.emit(EV, lambda e, ci=ci, a=a:
                e.tensor_sub(a, ACC[:, ci * 1024:(ci + 1) * 1024], mrep.a(0, 1024)),
                r=[acc_b[ci]] + mrep.b(), w=bufs)
        pg.emit(EV, lambda e, a=a, rrep=rrep: e.tensor_mul(a, a, rrep.a(0, 1024)),
                r=bufs + rrep.b(), w=bufs)
        so = xnsp.off + (ci % 2) * 1024
        hi = AR[:, so:so + 512].bitcast(BF16)
        lo = AR[:, so + 512:so + 1024].bitcast(BF16)
        spb = xnsp.b((ci % 2) * 1024, (ci % 2) * 1024 + 1024)
        pg.emit(ES, lambda e, a=a, hi=hi: e.activation(hi, a, AF.Copy),
                r=bufs, w=spb)
        pg.emit(EV, lambda e, a=a, hi=hi, lo=lo: e.tensor_sub(lo, a, hi),
                r=bufs + spb, w=spb)
        def st(e, sem, ci=ci, hi=hi, lo=lo, a=a):
            e.dma_start(out=XNH[ci * 128:(ci + 1) * 128, :], in_=a).then_inc(sem, 16)
            e.dma_start(out=XNH_H[ci * 128:(ci + 1) * 128, :], in_=hi).then_inc(sem, 16)
            e.dma_start(out=XNH_L[ci * 128:(ci + 1) * 128, :], in_=lo).then_inc(sem, 16)
        pg.emit("gpsimd", st, r=spb + bufs, w=[xnh_b[ci]], dma=f"xo{ci % 2}", ndma=3)

    ln_phase(1024, lambda ci: ACC[:, ci * 1024:(ci + 1) * 1024], acc_b, main_sink, 2)
    reset(lnm)

    zA = alloc(21 * MTT)
    cv = alloc(20 * MTT)
    xnt = alloc(16 * MTT)
    xnt_c = lambda ki: AR[:, xnt.off + ki * 256:xnt.off + (ki + 1) * 256]
    xstm = alloc(2 * 2048)
    btm = alloc(2 * 256)
    dt6 = alloc(4 * MTT)
    vtm = alloc(2 * 64)
    lqr = alloc(64)
    bsc = alloc(64)
    aqb = alloc(64)
    gsb = alloc(128)
    cst = alloc(128)
    esb = alloc(4 * 128)
    abc = alloc(4 * 128)
    ytm = alloc(2 * 64)
    bes = alloc(2 * 64)
    ysq = alloc(2 * MTT)
    rsa = alloc(MTT)
    msc = mark()

    def proj_block(mi, T, rhs_ap, rhs_bufs, sink):
        sl = wload(WINB[mi], 2048)
        bk = mmbank()
        def grp(e, sl=sl, bk=bk, T=T):
            last = None
            for ki in range(16):
                last = e.matmul(PS[bk][:, 0:T],
                                WSLOT[:, sl * 2048 + ki * 128:sl * 2048 + (ki + 1) * 128],
                                rhs_ap(ki), start=(ki == 0), stop=(ki == 15))
            return last
        pg.emit(ET, grp, r=[ws_b[2 * sl], ws_b[2 * sl + 1]] + rhs_bufs, w=[ps_b[bk]])
        sink(mi, bk, T)

    def inproj(T, rhs_ap, rhs_bufs, mis, sink):
        for mi in mis:
            proj_block(mi, T, rhs_ap, rhs_bufs, sink)

    def sinkA(mi, bk, T):
        j = (mi - 16) if mi < 36 else 20
        pg.emit(EV, lambda e, j=j, mi=mi, bk=bk, T=T:
                e.tensor_scalar(zA.a(j * MTT, j * MTT + T), PS[bk][:, 0:T],
                                C["BINC"][:, mi:mi + 1], C["MUPC"][:, mi:mi + 1],
                                ALU.add, ALU.mult),
                r=[ps_b[bk], cN], w=zA.b(j * MTT, j * MTT + T))

    def sinkB(mi, bk, T):
        pg.emit(EV, lambda e, mi=mi, bk=bk, T=T:
                e.tensor_scalar(cv.a(mi * MTT, mi * MTT + T), PS[bk][:, 0:T],
                                C["BINC"][:, mi:mi + 1], C["MUPC"][:, mi:mi + 1],
                                ALU.add, ALU.mult),
                r=[ps_b[bk], cN], w=cv.b(mi * MTT, mi * MTT + T))

    def conv_chunk(j, T, mask_tail):
        za = zA.a(j * MTT, j * MTT + T)
        za_b = zA.b(j * MTT, j * MTT + T)
        cia = cvin.a(0, T + 3)
        cib = cvin.b(0, T + 64)
        cva = cv.a(j * MTT, j * MTT + T)
        cvb = cv.b(j * MTT, j * MTT + T)
        tl = TAIL[:, 3 * j:3 * j + 3]
        if mask_tail:
            pg.emit(EV, lambda e, tl=tl: e.tensor_scalar_mul(tl, tl, MSKC),
                    r=[tail_b[j], cN], w=[tail_b[j]])
        pg.emit(EV, lambda e, tl=tl, cia=cia: e.tensor_copy(cia[:, 0:3], tl),
                r=[tail_b[j]], w=cib)
        pg.emit(EV, lambda e, za=za, cia=cia, T=T: e.tensor_copy(cia[:, 3:3 + T], za),
                r=za_b, w=cib)
        pg.emit(EV, lambda e, za=za, tl=tl, T=T: e.tensor_copy(tl, za[:, T - 3:T]),
                r=za_b, w=[tail_b[j]])
        pg.emit(EV, lambda e, cia=cia, cva=cva, j=j, T=T:
                e.tensor_scalar_mul(cva, cia[:, 0:T], C["CWC"][:, 4 * j:4 * j + 1]),
                r=cib + [cN], w=cvb)
        for k in range(1, 4):
            pg.emit(EV, lambda e, cia=cia, cva=cva, j=j, k=k, T=T:
                    e.scalar_tensor_tensor(cva, cia[:, k:k + T],
                                           C["CWC"][:, 4 * j + k:4 * j + k + 1],
                                           cva, ALU.mult, ALU.add),
                    r=cib + [cN] + cvb, w=cvb)
        pg.emit(ES, lambda e, cva=cva, j=j:
                e.activation(cva, cva, AF.Silu, bias=C["CBC"][:, j:j + 1]),
                r=cvb + [cN], w=cvb)

    def tail_only(j, T):
        za = zA.a(j * MTT, j * MTT + T)
        za_b = zA.b(j * MTT, j * MTT + T)
        tl = TAIL[:, 3 * j:3 * j + 3]
        pg.emit(EV, lambda e, za=za, tl=tl, T=T: e.tensor_copy(tl, za[:, T - 3:T]),
                r=za_b, w=[tail_b[j]])

    def dt_pipeline(T, nchunk):
        xa = dt6.a(0, T, 0, 32)
        ta = dt6.a(MTT, MTT + T, 0, 32)
        da = dt6.a(2 * MTT, 2 * MTT + T, 0, 32)
        la = dt6.a(3 * MTT, 3 * MTT + T, 0, 32)
        xb = dt6.b(0, T); tb = dt6.b(MTT, MTT + T); db = dt6.b(2 * MTT, 2 * MTT + T)
        lb = dt6.b(3 * MTT, 3 * MTT + T)
        raw = zA.a(20 * MTT, 20 * MTT + T, 0, 32)
        rwb = zA.b(20 * MTT, 20 * MTT + T)
        pg.emit(EV, lambda e: e.tensor_scalar_add(xa, raw, C["DTBC"]), r=rwb + [cN], w=xb)
        pg.emit(ES, lambda e: e.activation(ta, raw, AF.Abs, bias=C["DTBC"]), r=rwb + [cN], w=tb)
        pg.emit(ES, lambda e: e.activation(ta, ta, AF.Exp, scale=-1.0), r=tb, w=tb)
        pg.emit(ES, lambda e: e.activation(ta, ta, AF.Ln, bias=1.0), r=tb, w=tb)
        pg.emit(EV, lambda e: e.tensor_scalar_max(da, xa, 0.0), r=xb, w=db)
        pg.emit(EV, lambda e: e.tensor_add(da, da, ta), r=db + tb, w=db)
        pg.emit(EV, lambda e: e.tensor_scalar_mul(xa, da, NEGA), r=db + [nega_b], w=xb)
        pg.emit(ES, lambda e: e.activation(ta, da, AF.Ln), r=db, w=tb)
        for c in range(nchunk):
            pg.emit(EV, lambda e, c=c: e.tensor_tensor_scan(
                la[:, c * 128:(c + 1) * 128], xa[:, c * 128:(c + 1) * 128],
                C["Z32"][:, 0:128], 0.0, ALU.add, ALU.add), r=xb + [cN], w=lb)
        pg.emit(EV, lambda e: e.tensor_sub(xa, la, ta), r=lb + tb, w=xb)
        return la, lb, xa, xb

    def transposes(nchunk, la, lb, va, vb):
        for c in range(nchunk):
            for fi in range(16):
                bk = trbank()
                f0 = fi * MTT + c * 128
                pg.emit(ET, lambda e, f0=f0, bk=bk:
                        e.transpose(PS[bk][:, 0:128], cv.a(f0, f0 + 128), C["IDN"]),
                        r=cv.b(f0, f0 + 128) + [cN], w=[ps_b[bk]])
                x0 = c * 2048 + fi * 128
                pg.emit(EV, lambda e, x0=x0, bk=bk:
                        e.tensor_copy(xstm.a(x0, x0 + 128), PS[bk][:, 0:128]),
                        r=[ps_b[bk]], w=xstm.b(x0, x0 + 128))
            for bi in range(2):
                bk = trbank()
                f0 = (16 + bi) * MTT + c * 128
                pg.emit(ET, lambda e, f0=f0, bk=bk:
                        e.transpose(PS[bk][:, 0:128], cv.a(f0, f0 + 128), C["IDN"]),
                        r=cv.b(f0, f0 + 128) + [cN], w=[ps_b[bk]])
                b0 = c * 256 + bi * 128
                pg.emit(EV, lambda e, b0=b0, bk=bk:
                        e.tensor_copy(btm.a(b0, b0 + 128), PS[bk][:, 0:128]),
                        r=[ps_b[bk]], w=btm.b(b0, b0 + 128))
            bk = trbank()
            pg.emit(ET, lambda e, c=c, bk=bk:
                    e.transpose(PS[bk][0:128, 0:32], va[:, c * 128:(c + 1) * 128],
                                C["IDN"][0:32, 0:32]),
                    r=vb + [cN], w=[ps_b[bk]])
            pg.emit(EV, lambda e, c=c, bk=bk:
                    e.tensor_copy(vtm.a(c * 64, c * 64 + 32), PS[bk][0:128, 0:32]),
                    r=[ps_b[bk]], w=vtm.b(c * 64, c * 64 + 64))

    def scan_chunk(c, la, lb, full, filler=None):
        lac = la[:, c * 128:(c + 1) * 128]
        pg.emit(ET, lambda e, lac=lac:
                e.transpose(PS[2][0:1, 0:32], lac[:, 127:128], C["IDN"][0:32, 0:32]),
                r=lb + [cN], w=[ps_b[2]])
        pg.emit(EV, lambda e: e.tensor_copy(lqr.a(0, 32, 0, 1), PS[2][0:1, 0:32]),
                r=[ps_b[2]], w=lqr.b())
        pg.emit(ET, lambda e:
                e.matmul(PS[2][0:32, 32:64], C["ONES1"][0:1, 0:32], lqr.a(0, 32, 0, 1),
                         start=True, stop=True), r=lqr.b() + [cN], w=[ps_b[2]])
        pg.emit(ES, lambda e:
                e.activation(aqb.a(0, 32, 0, 32), PS[2][0:32, 32:64], AF.Exp),
                r=[ps_b[2]], w=aqb.b())
        pg.emit(ET, lambda e:
                e.matmul(PS[2][:, 64:96], C["ONES1"][0:1, 0:128], lqr.a(0, 32, 0, 1),
                         start=True, stop=True), r=lqr.b() + [cN], w=[ps_b[2]])
        pg.emit(EV, lambda e, c=c:
                e.scalar_tensor_tensor(bsc.a(0, 32), vtm.a(c * 64, c * 64 + 32), -1.0,
                                       PS[2][:, 64:96], ALU.mult, ALU.add),
                r=vtm.b(c * 64, c * 64 + 64) + [ps_b[2]], w=bsc.b())
        pg.emit(ES, lambda e: e.activation(bsc.a(0, 32), bsc.a(0, 32), AF.Exp),
                r=bsc.b(), w=bsc.b())

        for g in range(Gg):
            if filler is not None:
                th = next(filler, None)
                if th is not None:
                    th()
            bch, boff = 16 + g // 4, 32 * (g % 4)
            bf0 = bch * MTT + c * 128
            b_fm = cv.a(bf0, bf0 + 128, boff, boff + 32)
            b_fm_b = cv.b(bf0, bf0 + 128)
            b_base = AR[0:128, btm.off + c * 256 + 32 * g:btm.off + c * 256 + 32 * g + 32]
            in0 = bass.AP(b_base.tensor, b_base.offset,
                          [list(b_base.ap[0]), [0, 4], [1, 32]])
            s_base = AR[0:128, bsc.off + 4 * g:bsc.off + 4 * g + 4]
            in1 = bass.AP(s_base.tensor, s_base.offset,
                          [list(s_base.ap[0]), [1, 4], [0, 32]])
            o_base = AR[0:128, bes.off:bes.off + 128]
            out3 = bass.AP(o_base.tensor, o_base.offset,
                           [list(o_base.ap[0]), [32, 4], [1, 32]])
            pg.emit(EV, lambda e, in0=in0, in1=in1, out3=out3:
                    e.tensor_mul(out3, in0, in1),
                    r=btm.b(c * 256, c * 256 + 256) + bsc.b(), w=bes.b())
            if full:
                cf0 = (18 + g // 4) * MTT + c * 128
                c_fm = cv.a(cf0, cf0 + 128, boff, boff + 32)
                c_fm_b = cv.b(cf0, cf0 + 128)
                pg.emit(ET, lambda e, b_fm=b_fm, c_fm=c_fm, boff=boff:
                        e.matmul(PS[2][:, 128:256], b_fm, c_fm, start=True, stop=True,
                                 tile_position=(boff, 0)),
                        r=b_fm_b + c_fm_b, w=[ps_b[2]])
                pg.emit(EV, lambda e: e.tensor_copy(gsb.a(0, 128), PS[2][:, 128:256]),
                        r=[ps_b[2]], w=gsb.b())
                pg.emit(ET, lambda e, c_fm=c_fm, boff=boff:
                        e.matmul(PS[2][0:32, 256:384],
                                 C["IDN"][boff:boff + 32, boff:boff + 32], c_fm,
                                 start=True, stop=True, tile_position=(boff, 0)),
                        r=c_fm_b + [cN], w=[ps_b[2]])
                pg.emit(EV, lambda e: e.tensor_copy(cst.a(0, 128, 0, 32),
                                                    PS[2][0:32, 256:384]),
                        r=[ps_b[2]], w=cst.b())
                for h in range(4 * g, 4 * g + 4):
                    par = h % 2
                    eo = 128 * (h % 4)
                    bly = BK_LY[par]
                    selh = bcast(C["IDN"][0:32, h:h + 1], 128)
                    pg.emit(ET, lambda e, selh=selh, lac=lac, bly=bly:
                            e.matmul(PS[bly][:, 0:128], selh, lac, start=True, stop=True),
                            r=lb + [cN], w=[ps_b[bly]])
                    vcol = vtm.col(c * 64 + h)
                    pg.emit(EV, lambda e, vcol=vcol, eo=eo, bly=bly:
                            e.scalar_tensor_tensor(esb.a(eo, eo + 128), PS[bly][:, 0:128],
                                                   vcol, C["MASKC"], ALU.subtract, ALU.add),
                            r=[ps_b[bly]] + vtm.b(c * 64, c * 64 + 64) + [cN],
                            w=esb.b(eo, eo + 128))
                    pg.emit(ES, lambda e, eo=eo, bly=bly:
                            e.activation(abc.a(eo, eo + 128, 0, 32), PS[bly][0:32, 0:128],
                                         AF.Exp),
                            r=[ps_b[bly]], w=abc.b(eo, eo + 128))
                    pg.emit(ES, lambda e, eo=eo:
                            e.activation(esb.a(eo, eo + 128), esb.a(eo, eo + 128), AF.Exp),
                            r=esb.b(eo, eo + 128), w=esb.b(eo, eo + 128))
                    pg.emit(EV, lambda e, eo=eo:
                            e.tensor_mul(esb.a(eo, eo + 128), esb.a(eo, eo + 128),
                                         gsb.a(0, 128)),
                            r=esb.b(eo, eo + 128) + gsb.b(), w=esb.b(eo, eo + 128))
                    pg.emit(EV, lambda e, eo=eo:
                            e.tensor_mul(abc.a(eo, eo + 128, 0, 32), cst.a(0, 128, 0, 32),
                                         abc.a(eo, eo + 128, 0, 32)),
                            r=cst.b() + abc.b(eo, eo + 128), w=abc.b(eo, eo + 128))
            for h in range(4 * g, 4 * g + 4):
                xs = xstm.a(c * 2048 + h * 64, c * 2048 + (h + 1) * 64)
                xs_b = xstm.b(c * 2048 + h * 64, c * 2048 + (h + 1) * 64)
                sl = SALL[:, h * 64:(h + 1) * 64]
                par = h % 2
                eo = 128 * (h % 4)
                if full:
                    bky = [3, 4][par]
                    def ygrp(e, xs=xs, sl=sl, eo=eo, bky=bky):
                        e.matmul(PS[bky][:, 0:64], esb.a(eo, eo + 128), xs,
                                 start=True, stop=False)
                        return e.matmul(PS[bky][:, 0:64], abc.a(eo, eo + 128, 0, 32), sl,
                                        start=False, stop=True)
                    pg.emit(ET, ygrp, r=esb.b(eo, eo + 128) + abc.b(eo, eo + 128) + xs_b
                            + [s_b[h]], w=[ps_b[bky]])
                    ysl = par * 64
                    ya = ytm.a(ysl, ysl + 64)
                    pg.emit(EV, lambda e, xs=xs, h=h, ya=ya, bky=bky:
                            e.scalar_tensor_tensor(ya, xs, DALL[:, h:h + 1],
                                                   PS[bky][:, 0:64], ALU.mult, ALU.add),
                            r=xs_b + [dall_b, ps_b[bky]], w=ytm.b(ysl, ysl + 64))
                    if par == 1:
                        pg.emit(ET, lambda e:
                                e.transpose(PS[2][:, 384:512], ytm.a(0, 128), C["IDN"]),
                                r=ytm.b(0, 128) + [cN], w=[ps_b[2]])
                        yf0 = (h // 2) * MTT + c * 128
                        pg.emit(EV, lambda e, yf0=yf0:
                                e.tensor_copy(zA.a(yf0, yf0 + 128), PS[2][:, 384:512]),
                                r=[ps_b[2]], w=zA.b(yf0, yf0 + 128))
                bo = 32 * (h % 4)
                pg.emit(ET, lambda e, xs=xs, bo=bo:
                        e.matmul(PS[5][0:32, 256:320], bes.a(bo, bo + 32), xs,
                                 start=True, stop=True),
                        r=bes.b(bo, bo + 64) + xs_b, w=[ps_b[5]])
                pg.emit(EV, lambda e, sl=sl, h=h:
                        e.scalar_tensor_tensor(sl, sl, aqb.col(h, 0, 32),
                                               PS[5][0:32, 256:320], ALU.mult, ALU.add),
                        r=[s_b[h], ps_b[5]] + aqb.b(), w=[s_b[h]])

    pln = alloc(7 * 256)
    cvin = Region(rsa.off, MTT + 64)
    sqst = alloc(256)

    def ln_stream(tok0):
        rg = lambda i: Region(pln.off + i * 256, 256)
        statx, stats, mrow, rrow, tr1, mrep, rrep = [rg(i) for i in range(7)]
        tr2 = rrow

        def chunk_dma(ci, stg, sl):
            pg.emit("sync", lambda e, sem, ci=ci, stg=stg:
                    e.dma_start(out=stg.a(0, 256),
                                in_=XP[ci * 128:(ci + 1) * 128, tok0:tok0 + 256])
                    .then_inc(sem, 16), w=stg.b(), dma=f"px{sl}", ndma=1)

        for ci in range(16):
            sl = ci % 2
            stg = Region(ysq.off + sl * 256, 256)
            sqg = Region(sqst.off, 256)
            chunk_dma(ci, stg, sl)
            pg.emit(ES, lambda e, stg=stg, sqg=sqg:
                    e.activation(sqg.a(0, 256), stg.a(0, 256), AF.Square),
                    r=stg.b(), w=sqg.b())
            for acc_r, src_r in ((statx, stg), (stats, sqg)):
                bk = mmbank()
                pg.emit(ET, lambda e, bk=bk, src_r=src_r:
                        e.matmul(PS[bk][0:1, 0:256], C["CS2"][:, 0:1], src_r.a(0, 256),
                                 start=True, stop=True),
                        r=src_r.b() + [cN], w=[ps_b[bk]])
                if ci == 0:
                    pg.emit(EV, lambda e, bk=bk, acc_r=acc_r:
                            e.tensor_copy(acc_r.a(0, 256, 0, 1), PS[bk][0:1, 0:256]),
                            r=[ps_b[bk]], w=acc_r.b())
                else:
                    pg.emit(EV, lambda e, bk=bk, acc_r=acc_r:
                            e.tensor_add(acc_r.a(0, 256, 0, 1), acc_r.a(0, 256, 0, 1),
                                         PS[bk][0:1, 0:256]),
                            r=[ps_b[bk]] + acc_r.b(), w=acc_r.b())
        pg.emit(EV, lambda e: e.tensor_scalar_mul(mrow.a(0, 256, 0, 1),
                                                  statx.a(0, 256, 0, 1), 1.0 / D),
                r=statx.b(), w=mrow.b())
        pg.emit(EV, lambda e: e.tensor_scalar_mul(tr1.a(0, 256, 0, 1),
                                                  stats.a(0, 256, 0, 1), 1.0 / D),
                r=stats.b(), w=tr1.b())
        pg.emit(EV, lambda e: e.tensor_mul(tr2.a(0, 256, 0, 1), mrow.a(0, 256, 0, 1),
                                           mrow.a(0, 256, 0, 1)),
                r=mrow.b(), w=tr2.b())
        pg.emit(EV, lambda e: e.scalar_tensor_tensor(tr1.a(0, 256, 0, 1),
                                                     tr2.a(0, 256, 0, 1), -1.0,
                                                     tr1.a(0, 256, 0, 1), ALU.mult, ALU.add),
                r=tr1.b() + tr2.b(), w=tr1.b())
        pg.emit(EV, lambda e: e.tensor_scalar_add(tr1.a(0, 256, 0, 1),
                                                  tr1.a(0, 256, 0, 1), EPS),
                r=tr1.b(), w=tr1.b())
        pg.emit(EV, lambda e: e.reciprocal(tr1.a(0, 256, 0, 1), tr1.a(0, 256, 0, 1)),
                r=tr1.b(), w=tr1.b())
        pg.emit(ES, lambda e: e.activation(rrow.a(0, 256, 0, 1), tr1.a(0, 256, 0, 1),
                                           AF.Sqrt),
                r=tr1.b(), w=rrow.b())
        for row, rep in ((mrow, mrep), (rrow, rrep)):
            bk = mmbank()
            pg.emit(ET, lambda e, bk=bk, row=row:
                    e.matmul(PS[bk][:, 0:256], C["ONES1"][0:1, 0:128],
                             row.a(0, 256, 0, 1), start=True, stop=True),
                    r=row.b() + [cN], w=[ps_b[bk]])
            pg.emit(EV, lambda e, bk=bk, rep=rep:
                    e.tensor_copy(rep.a(0, 256), PS[bk][:, 0:256]),
                    r=[ps_b[bk]], w=rep.b())
        for ci in range(16):
            sl = ci % 2
            stg = Region(ysq.off + sl * 256, 256)
            chunk_dma(ci, stg, sl)
            dst = xnt_c(ci)
            db = xnt.b(ci * 256, (ci + 1) * 256)
            pg.emit(EV, lambda e, stg=stg, dst=dst:
                    e.tensor_sub(dst, stg.a(0, 256), mrep.a(0, 256)),
                    r=stg.b() + mrep.b(), w=db)
            pg.emit(EV, lambda e, dst=dst: e.tensor_mul(dst, dst, rrep.a(0, 256)),
                    r=db + rrep.b(), w=db)

    PRE_MIS = list(range(16, 34)) + [36]
    ln_stream(0)
    for pt in range(4):
        mis = PRE_MIS if pt < 3 else (list(range(16, 36)) + [36])
        for mi in mis:
            proj_block(mi, 256, xnt_c, xnt.b(), sinkA)
            if 16 <= mi < 34:
                conv_chunk(mi - 16, 256, False)
            elif mi in (34, 35):
                tail_only(mi - 16, 256)
        if pt < 3:
            ln_stream((pt + 1) * 256)
        la, lb, va, vb = dt_pipeline(256, 2)
        transposes(2, la, lb, va, vb)
        for cch in range(2):
            scan_chunk(cch, la, lb, full=False)

    pg.emit(EV, lambda e: e.tensor_scalar_mul(SALL, SALL, MSKC[0:32, :]),
            r=s_b + [cN], w=s_b)

    def xt_loads_emit(mt):
        def xt_loads(e, sem, mt=mt):
            for ki in range(16):
                e.dma_start(out=xnt_c(ki),
                            in_=XNH[ki * 128:(ki + 1) * 128, mt * 256:(mt + 1) * 256]) \
                    .then_inc(sem, 16)
        pg.emit("sync", xt_loads, r=xnh_b, w=xnt.b(), dma="xt", ndma=16)

    xt_loads_emit(0)
    for mt in range(4):
        for mi in range(16, 37):
            proj_block(mi, 256, xnt_c, xnt.b(), sinkA)
            if mi < 36:
                conv_chunk(mi - 16, 256, mt == 0)
        la, lb, va, vb = dt_pipeline(256, 2)
        transposes(2, la, lb, va, vb)
        fillers = iter([
            (lambda mi=mi: proj_block(mi, 256, xnt_c, xnt.b(), sinkB))
            for mi in range(16)])
        for cch in range(2):
            scan_chunk(cch, la, lb, full=True, filler=fillers)
        for th in fillers:
            th()
        if mt < 3:
            xt_loads_emit(mt + 1)
        for ci in range(16):
            f0 = ci * MTT
            pg.emit(ES, lambda e, f0=f0: e.activation(cv.a(f0, f0 + 256), cv.a(f0, f0 + 256),
                                                      AF.Silu),
                    r=cv.b(f0, f0 + 256), w=cv.b(f0, f0 + 256))
            pg.emit(EV, lambda e, f0=f0:
                    e.tensor_mul(zA.a(f0, f0 + 256), zA.a(f0, f0 + 256), cv.a(f0, f0 + 256)),
                    r=zA.b(f0, f0 + 256) + cv.b(f0, f0 + 256), w=zA.b(f0, f0 + 256))
        for ci in range(16):
            f0 = ci * MTT
            sl = (ci % 2) * MTT
            pg.emit(ES, lambda e, f0=f0, sl=sl:
                    e.activation(ysq.a(sl, sl + 256), zA.a(f0, f0 + 256), AF.Square),
                    r=zA.b(f0, f0 + 256), w=ysq.b(sl, sl + 256))
            g = ci // 2
            bk = mmbank()
            pg.emit(ET, lambda e, sl=sl, g=g, bk=bk:
                    e.matmul(PS[bk][0:8, 0:256], C["CS8"][:, 8 * g:8 * g + 8],
                             ysq.a(sl, sl + 256), start=True, stop=True),
                    r=ysq.b(sl, sl + 256) + [cN], w=[ps_b[bk]])
            if ci == 0:
                pg.emit(EV, lambda e, bk=bk: e.tensor_copy(rsa.a(0, 256, 0, 8),
                                                           PS[bk][0:8, 0:256]),
                        r=[ps_b[bk]], w=rsa.b())
            else:
                pg.emit(EV, lambda e, bk=bk: e.tensor_add(rsa.a(0, 256, 0, 8),
                                                          rsa.a(0, 256, 0, 8),
                                                          PS[bk][0:8, 0:256]),
                        r=[ps_b[bk]] + rsa.b(), w=rsa.b())
        pg.emit(EV, lambda e: e.tensor_scalar_mul(rsa.a(0, 256, 0, 8), rsa.a(0, 256, 0, 8),
                                                  1.0 / 256.0), r=rsa.b(), w=rsa.b())
        pg.emit(EV, lambda e: e.tensor_scalar_add(rsa.a(0, 256, 0, 8), rsa.a(0, 256, 0, 8),
                                                  EPS), r=rsa.b(), w=rsa.b())
        pg.emit(EV, lambda e: e.reciprocal(rsa.a(0, 256, 0, 8), rsa.a(0, 256, 0, 8)),
                r=rsa.b(), w=rsa.b())
        pg.emit(ES, lambda e: e.activation(rsa.a(0, 256, 0, 8), rsa.a(0, 256, 0, 8), AF.Sqrt),
                r=rsa.b(), w=rsa.b())
        for g in range(8):
            bk = mmbank()
            selg = bcast(C["IDN"][0:8, g:g + 1], 128)
            pg.emit(ET, lambda e, bk=bk, selg=selg:
                    e.matmul(PS[bk][:, 0:256], selg, rsa.a(0, 256, 0, 8),
                             start=True, stop=True),
                    r=rsa.b() + [cN], w=[ps_b[bk]])
            for ci in (2 * g, 2 * g + 1):
                f0 = ci * MTT
                pg.emit(EV, lambda e, f0=f0, ci=ci, bk=bk:
                        e.scalar_tensor_tensor(zA.a(f0, f0 + 256), zA.a(f0, f0 + 256),
                                               C["NWC"][:, ci:ci + 1], PS[bk][:, 0:256],
                                               ALU.mult, ALU.mult),
                        r=zA.b(f0, f0 + 256) + [cN, ps_b[bk]], w=zA.b(f0, f0 + 256))
        for mo in range(16):
            sl = wload(WOB[mo], 2048)
            bk = mmbank()
            def wog(e, sl=sl, bk=bk):
                last = None
                for ki in range(16):
                    last = e.matmul(PS[bk][:, 0:256],
                                    WSLOT[:, sl * 2048 + ki * 128:sl * 2048 + (ki + 1) * 128],
                                    zA.a(ki * MTT, ki * MTT + 256),
                                    start=(ki == 0), stop=(ki == 15))
                return last
            pg.emit(ET, wog, r=[ws_b[2 * sl], ws_b[2 * sl + 1]] + zA.b(0, 16 * MTT), w=[ps_b[bk]])
            pg.emit(EV, lambda e, mo=mo, bk=bk, mt=mt:
                    e.scalar_tensor_tensor(ACC[:, mo * 1024 + mt * 256:mo * 1024 + (mt + 1) * 256],
                                           PS[bk][:, 0:256], C["BOC"][:, mo:mo + 1],
                                           ACC[:, mo * 1024 + mt * 256:mo * 1024 + (mt + 1) * 256],
                                           ALU.add, ALU.add),
                    r=[ps_b[bk], cN, acc_b[mo]], w=[acc_b[mo]])

    reset(m0)
    mmbanks[0] = [0, 1, 2, 3, 4, 5]
    ring = alloc(16 * 512)
    hbuf = alloc(11 * 1024)
    h1s = alloc(1024)
    wrep = alloc(1024)
    grep = alloc(1024)
    lfm = alloc(1024)
    wfm = alloc(1024)
    prb = alloc(64)
    ltm = Region(prb.off + 0, 8)
    p8a = Region(prb.off + 8, 8)
    cols = [Region(prb.off + 16 + i, 1) for i in range(6)]
    gwb = alloc(192)

    ring_h = lambda ki: AR[:, ring.off + ki * 256:ring.off + (ki + 1) * 256].bitcast(BF16)
    ring_l = lambda ki: AR[:, ring.off + 4096 + ki * 256:ring.off + 4096 + (ki + 1) * 256].bitcast(BF16)
    H_h = lambda fi, n: AR[:, hbuf.off + fi * 512 + n * 256:hbuf.off + fi * 512 + (n + 1) * 256].bitcast(BF16)
    H_l = lambda fi, n: AR[:, hbuf.off + 5632 + fi * 512 + n * 256:hbuf.off + 5632 + fi * 512 + (n + 1) * 256].bitcast(BF16)
    GWH = lambda ki: AR[:, gwb.off + ki * 4:gwb.off + ki * 4 + 4].bitcast(BF16)
    GWL = lambda ki: AR[:, gwb.off + 64 + ki * 4:gwb.off + 64 + ki * 4 + 4].bitcast(BF16)
    SGH = lambda ki: AR[:, gwb.off + 128:gwb.off + 136].bitcast(BF16)[:, ki:ki + 1]
    SGL = lambda ki: AR[:, gwb.off + 144:gwb.off + 152].bitcast(BF16)[:, ki:ki + 1]
    gw_hi = AR[:, gwb.off:gwb.off + 64].bitcast(BF16)
    gw_lo = AR[:, gwb.off + 64:gwb.off + 128].bitcast(BF16)
    sg_hi = AR[:, gwb.off + 128:gwb.off + 136].bitcast(BF16)
    sg_lo = AR[:, gwb.off + 144:gwb.off + 152].bitcast(BF16)
    pg.emit(ES, lambda e: e.activation(gw_hi, C["GWC"], AF.Copy), r=[cN], w=gwb.b())
    pg.emit(EV, lambda e: e.tensor_sub(gw_lo, C["GWC"], gw_hi), r=[cN] + gwb.b(), w=gwb.b())
    pg.emit(ES, lambda e: e.activation(sg_hi, C["SGC"], AF.Copy), r=[cN] + gwb.b(), w=gwb.b())
    pg.emit(EV, lambda e: e.tensor_sub(sg_lo, C["SGC"], sg_hi), r=[cN] + gwb.b(), w=gwb.b())

    ring_rb = lambda a, b: ring.b(a * 256, b * 256) + \
        ring.b(4096 + a * 256, 4096 + b * 256)
    ringA = ring_rb(0, 8)
    ringB = ring_rb(8, 16)

    def ring_fill(n):
        for k0 in range(0, 16, 4):
            def fill(e, sem, n=n, k0=k0):
                for ki in range(k0, k0 + 4):
                    e.dma_start(out=ring_h(ki),
                                in_=XNH_H[ki * 128:(ki + 1) * 128, n * 512:(n + 1) * 512]) \
                        .then_inc(sem, 16)
                    e.dma_start(out=ring_l(ki),
                                in_=XNH_L[ki * 128:(ki + 1) * 128, n * 512:(n + 1) * 512]) \
                        .then_inc(sem, 16)
            pg.emit("sync", fill, r=xnh_b, w=ring_rb(k0, k0 + 4), dma="xr", ndma=8)

    for n in range(2):
        ring_fill(n)
        bk = mmbank()
        def rlog(e, bk=bk):
            last = None
            for ki in range(16):
                e.matmul(PS[bk][0:8, 0:512], GWH(ki), ring_h(ki),
                         start=(ki == 0), stop=False)
                e.matmul(PS[bk][0:8, 0:512], GWH(ki), ring_l(ki), start=False, stop=False)
                last = e.matmul(PS[bk][0:8, 0:512], GWL(ki), ring_h(ki),
                                start=False, stop=(ki == 15))
            return last
        pg.emit(ET, rlog, r=ring.b() + gwb.b(), w=[ps_b[bk]])
        pg.emit(EV, lambda e, n=n, bk=bk:
                e.tensor_copy(lfm.a(n * 512, (n + 1) * 512, 0, 8), PS[bk][0:8, 0:512]),
                r=[ps_b[bk]], w=lfm.b(n * 512, (n + 1) * 512))
        bk = mmbank()
        def slog(e, bk=bk):
            last = None
            for ki in range(16):
                e.matmul(PS[bk][0:1, 0:512], SGH(ki), ring_h(ki),
                         start=(ki == 0), stop=False)
                e.matmul(PS[bk][0:1, 0:512], SGH(ki), ring_l(ki), start=False, stop=False)
                last = e.matmul(PS[bk][0:1, 0:512], SGL(ki), ring_h(ki),
                                start=False, stop=(ki == 15))
            return last
        pg.emit(ET, slog, r=ring.b() + gwb.b(), w=[ps_b[bk]])
        pg.emit(EV, lambda e, n=n, bk=bk:
                e.tensor_copy(wfm.a(n * 512, (n + 1) * 512, 0, 1), PS[bk][0:1, 0:512]),
                r=[ps_b[bk]], w=wfm.b(n * 512, (n + 1) * 512))
    pg.emit(ES, lambda e: e.activation(wfm.a(0, 1024, 0, 1), wfm.a(0, 1024, 0, 1),
                                       AF.Sigmoid),
            r=wfm.b(), w=wfm.b())
    for n in range(2):
        bk = mmbank()
        pg.emit(ET, lambda e, n=n, bk=bk:
                e.matmul(PS[bk][:, 0:512], C["ONES1"][0:1, 0:128],
                         wfm.a(n * 512, (n + 1) * 512, 0, 1), start=True, stop=True),
                r=wfm.b() + [cN], w=[ps_b[bk]])
        pg.emit(EV, lambda e, n=n, bk=bk:
                e.tensor_copy(grep.a(n * 512, (n + 1) * 512), PS[bk][:, 0:512]),
                r=[ps_b[bk]], w=grep.b())
    for tc in range(8):
        bk = trbank()
        pg.emit(ET, lambda e, tc=tc, bk=bk:
                e.transpose(PS[bk][0:128, 0:8], lfm.a(tc * 128, (tc + 1) * 128, 0, 8),
                            C["IDN"][0:8, 0:8]),
                r=lfm.b(tc * 128, (tc + 1) * 128) + [cN], w=[ps_b[bk]])
        pg.emit(EV, lambda e, bk=bk: e.tensor_copy(ltm.a(0, 8), PS[bk][0:128, 0:8]),
                r=[ps_b[bk]], w=ltm.b())
        c1, c2, c3, c4, c5, c6 = [cc.a(0, 1) for cc in cols]
        cb = [cc.b() for cc in cols]
        pg.emit(EV, lambda e, c1=c1: e.reduce_max(c1, ltm.a(0, 8), axis=AX.X),
                r=ltm.b(), w=cb[0])
        pg.emit(EV, lambda e, c1=c1, c2=c2: e.tensor_scalar_mul(c2, c1, -1.0),
                r=cb[0], w=cb[1])
        pg.emit(ES, lambda e, c2=c2: e.activation(ltm.a(0, 8), ltm.a(0, 8), AF.Exp, bias=c2),
                r=ltm.b() + cb[1], w=ltm.b())
        pg.emit(EV, lambda e, c3=c3: e.reduce_sum(c3, ltm.a(0, 8), axis=AX.X),
                r=ltm.b(), w=cb[2])
        pg.emit(EV, lambda e, c3=c3: e.reciprocal(c3, c3), r=cb[2], w=cb[2])
        pg.emit(EV, lambda e, c3=c3: e.tensor_scalar_mul(ltm.a(0, 8), ltm.a(0, 8), c3),
                r=ltm.b() + cb[2], w=ltm.b())
        pg.emit(EV, lambda e, c1=c1: e.reduce_max(c1, ltm.a(0, 8), axis=AX.X),
                r=ltm.b(), w=cb[0])
        pg.emit(EV, lambda e, c1=c1: e.tensor_scalar(p8a.a(0, 8), ltm.a(0, 8), c1, None,
                                                     ALU.is_equal),
                r=ltm.b() + cb[0], w=p8a.b())
        pg.emit(EV, lambda e, c4=c4: e.reduce_sum(c4, p8a.a(0, 8), axis=AX.X),
                r=p8a.b(), w=cb[3])
        pg.emit(EV, lambda e: e.scalar_tensor_tensor(p8a.a(0, 8), p8a.a(0, 8), -1e30,
                                                     ltm.a(0, 8), ALU.mult, ALU.add),
                r=p8a.b() + ltm.b(), w=p8a.b())
        pg.emit(EV, lambda e, c5=c5: e.reduce_max(c5, p8a.a(0, 8), axis=AX.X),
                r=p8a.b(), w=cb[4])
        pg.emit(EV, lambda e, c4=c4: e.tensor_scalar(c4, c4, 1.5, None, ALU.is_ge),
                r=cb[3], w=cb[3])
        pg.emit(EV, lambda e, c1=c1, c5=c5, c6=c6: e.tensor_sub(c6, c1, c5),
                r=cb[0] + cb[4], w=cb[5])
        pg.emit(EV, lambda e, c4=c4, c5=c5, c6=c6:
                e.scalar_tensor_tensor(c5, c4, c6, c5, ALU.mult, ALU.add),
                r=cb[3] + cb[5] + cb[4], w=cb[4])
        pg.emit(EV, lambda e, c5=c5: e.tensor_scalar(p8a.a(0, 8), ltm.a(0, 8), c5, None,
                                                     ALU.is_ge),
                r=ltm.b() + cb[4], w=p8a.b())
        pg.emit(EV, lambda e: e.tensor_mul(p8a.a(0, 8), p8a.a(0, 8), ltm.a(0, 8)),
                r=p8a.b() + ltm.b(), w=p8a.b())
        bk = trbank()
        pg.emit(ET, lambda e, bk=bk: e.transpose(PS[bk][0:8, 0:128], p8a.a(0, 8), C["IDN"]),
                r=p8a.b() + [cN], w=[ps_b[bk]])
        pg.emit(EV, lambda e, tc=tc, bk=bk:
                e.tensor_copy(lfm.a(tc * 128, (tc + 1) * 128, 0, 8), PS[bk][0:8, 0:128]),
                r=[ps_b[bk]], w=lfm.b(tc * 128, (tc + 1) * 128))
    ring_fill(0)


    def ffn_pass(fc1_blk, fc2_blk, proj_blk, prep_region, nfi, tag,
                 fill0_done=False, next_fill0=None):
        for n in range(2):
            if n == 1 or not fill0_done:
                ring_fill(n)
            for fi in range(nfi):
                bk1 = mmbank()
                ffn_group3(fc1_blk(fi), 16, ring_h, ring_l, (ringA, ringB), bk1,
                           qbufs=ring_rb if fi == 0 else None)
                pg.emit(ES, lambda e, bk1=bk1: e.activation(h1s.a(0, 512), PS[bk1], AF.Silu),
                        r=[ps_b[bk1]], w=h1s.b(0, 512))
                bk2 = mmbank()
                ffn_group3(fc2_blk(fi), 16, ring_h, ring_l, (ringA, ringB), bk2)
                pg.emit(EV, lambda e, bk2=bk2:
                        e.tensor_mul(h1s.a(0, 512), h1s.a(0, 512), PS[bk2]),
                        r=h1s.b(0, 512) + [ps_b[bk2]], w=h1s.b(0, 512))
                pg.emit(EV, lambda e, n=n:
                        e.tensor_mul(h1s.a(0, 512), h1s.a(0, 512),
                                     prep_region.a(n * 512, (n + 1) * 512)),
                        r=h1s.b(0, 512) + prep_region.b(), w=h1s.b(0, 512))
                hb = (hbuf.b(fi * 512 + n * 256, fi * 512 + (n + 1) * 256)
                      + hbuf.b(5632 + fi * 512 + n * 256, 5632 + fi * 512 + (n + 1) * 256))
                pg.emit(ES, lambda e, fi=fi, n=n:
                        e.activation(H_h(fi, n), h1s.a(0, 512), AF.Copy),
                        r=h1s.b(0, 512), w=hb)
                pg.emit(EV, lambda e, fi=fi, n=n:
                        e.tensor_sub(H_l(fi, n), h1s.a(0, 512), H_h(fi, n)),
                        r=h1s.b(0, 512) + hb, w=hb)
        if next_fill0 is not None:
            next_fill0()
        for ci in range(16):
            parts = wload_parts(proj_blk(ci), nfi)
            for n in range(2):
                bk = mmbank()
                emit_parts(parts, lambda fi, n=n: H_h(fi, n),
                           lambda fi, n=n: H_l(fi, n), hbuf.b(), bk)
                pg.emit(EV, lambda e, ci=ci, n=n, bk=bk:
                        e.tensor_add(ACC[:, ci * 1024 + n * 512:ci * 1024 + (n + 1) * 512],
                                     ACC[:, ci * 1024 + n * 512:ci * 1024 + (n + 1) * 512],
                                     PS[bk]),
                        r=[acc_b[ci], ps_b[bk]], w=[acc_b[ci]])

    for ex in range(8):
        for n in range(2):
            bk = mmbank()
            sele = bcast(C["IDN"][0:8, ex:ex + 1], 128)
            pg.emit(ET, lambda e, n=n, bk=bk, sele=sele:
                    e.matmul(PS[bk][:, 0:512], sele, lfm.a(n * 512, (n + 1) * 512, 0, 8),
                             start=True, stop=True),
                    r=lfm.b() + [cN], w=[ps_b[bk]])
            pg.emit(EV, lambda e, n=n, bk=bk:
                    e.tensor_copy(wrep.a(n * 512, (n + 1) * 512), PS[bk][:, 0:512]),
                    r=[ps_b[bk]], w=wrep.b())
        ffn_pass(lambda fi, ex=ex: F1B[ex, fi], lambda fi, ex=ex: F2B[ex, fi],
                 lambda ci, ex=ex: PJB[ex, ci], wrep, 11, f"e{ex}",
                 fill0_done=True, next_fill0=lambda: ring_fill(0))

    for p in range(4):
        ffn_pass(lambda fi, p=p: SF1B[11 * p + fi], lambda fi, p=p: SF2B[11 * p + fi],
                 lambda ci, p=p: SPB[ci][:, p * FE:(p + 1) * FE], grep, 11, f"s{p}",
                 fill0_done=True, next_fill0=(lambda: ring_fill(0)) if p < 3 else None)

    out_b = pg.buf()

    def out_stores(e, sem):
        for ci in range(16):
            e.dma_start(out=OUT[ci * 128:(ci + 1) * 128, :],
                        in_=ACC[:, ci * 1024:(ci + 1) * 1024]).then_inc(sem, 16)
    pg.emit("gpsimd", out_stores, r=acc_b, w=[out_b], dma="o", ndma=16)
    pg.emit("gpsimd", lambda e: None, r=[out_b])

    counts = pg.finalize()
    semkeys = sorted({op.sem for op in pg.ops if op.sem}) + \
              [("e", ET), ("e", EV), ("e", ES), ("e", "gpsimd"), ("e", "sync")]
    import contextlib
    with contextlib.ExitStack() as st:
        sems = {}
        for i, k in enumerate(semkeys):
            nm = k if isinstance(k, str) else f"es_{k[1]}"
            sems[k] = st.enter_context(nc.semaphore(f"sm_{nm}_{i}"))
        blk = st.enter_context(nc.Block())
        marks = {}
        blk.sync(lambda e: pg.stream(e, "sync", sems, marks))
        blk.gpsimd(lambda e: pg.stream(e, "gpsimd", sems, marks))
        blk.tensor(lambda e: pg.stream(e, ET, sems, marks))
        blk.vector(lambda e: pg.stream(e, EV, sems, marks))
        blk.scalar(lambda e: pg.stream(e, ES, sems, marks))
    return nc



def _blockify(W):
    K, M = W.shape
    Kc, Mc = K // 128, M // 128
    return np.ascontiguousarray(
        W.reshape(Kc, 128, Mc, 128).transpose(2, 1, 0, 3).reshape(Mc, 128, K))


def kernel(**inputs):
    f = lambda k: np.asarray(inputs[k], dtype=np.float32)
    hs = f("L0_attn_hidden_states")
    W_in = f("L0_attn_W_in"); b_in = f("L0_attn_b_in")
    conv_w = f("L0_attn_conv_w"); conv_b = f("L0_attn_conv_b")
    A_log = f("L0_attn_A_log"); D_param = f("L0_attn_D_param")
    dt_bias = f("L0_attn_dt_bias"); norm_w = f("L0_attn_norm_w")
    W_out = f("L0_attn_W_out"); b_out = f("L0_attn_b_out")
    mup = f("L0_attn_mup_vector")
    gate_w = f("L0_ffn_gate_w")
    efc1 = f("L0_ffn_expert_fc1"); efc2 = f("L0_ffn_expert_fc2")
    eproj = f("L0_ffn_expert_proj")
    sfc1 = f("L0_ffn_shared_fc1"); sfc2 = f("L0_ffn_shared_fc2")
    sproj = f("L0_ffn_shared_proj"); sgw = f("L0_ffn_shared_gate_w")

    Wp = np.zeros((D, NPC * 128), dtype=np.float32)
    Wp[:, :4640] = W_in
    WINB = _blockify(Wp)
    F1B = np.stack([_blockify(np.ascontiguousarray(efc1[e].T)) for e in range(8)])
    F2B = np.stack([_blockify(np.ascontiguousarray(efc2[e].T)) for e in range(8)])
    PJB = np.stack([_blockify(np.ascontiguousarray(eproj[e].T)) for e in range(8)])
    SF1B = _blockify(np.ascontiguousarray(sfc1.T))
    SF2B = _blockify(np.ascontiguousarray(sfc2.T))
    SPB = _blockify(np.ascontiguousarray(sproj.T))
    WOB = _blockify(W_out)
    GWC = np.ascontiguousarray(
        gate_w.T.reshape(16, 128, 8).transpose(1, 0, 2).reshape(128, 128))
    SGC = np.ascontiguousarray(sgw.reshape(16, 128).T)
    vpad = lambda v, n: np.ascontiguousarray(
        np.pad(v, (0, n * 128 - v.shape[0])).reshape(n, 128).T)
    BINC = vpad(b_in, NPC); MUPC = vpad(mup, NPC)
    CWC = np.ascontiguousarray(conv_w.reshape(20, 128, 4).transpose(1, 0, 2).reshape(128, 80))
    CBC = vpad(conv_b, 20); NWC = vpad(norm_w, 16); BOC = vpad(b_out, 16)
    ALC = A_log.reshape(32, 1).copy()
    DTBC = dt_bias.reshape(32, 1).copy()
    DROW = D_param.reshape(1, 32).copy()
    IDN = np.eye(128, dtype=np.float32)
    jj, tt = np.meshgrid(np.arange(128), np.arange(128), indexing="ij")
    MASKC = np.where(jj <= tt, 0.0, -1e30).astype(np.float32)
    CS8 = np.zeros((128, 64), dtype=np.float32)
    for g in range(8):
        CS8[:, 8 * g + g] = 1.0
    CS2 = np.zeros((128, 4), dtype=np.float32)
    CS2[:, 0] = 1.0
    CS2[:, 3] = 1.0
    ONES1 = np.ones((1, 512), dtype=np.float32)
    Z32 = np.zeros((32, 256), dtype=np.float32)

    shared = dict(WINB=WINB, F1B=F1B, F2B=F2B, PJB=PJB, SF1B=SF1B, SF2B=SF2B,
                  SPB=SPB, WOB=WOB, GWC=GWC, SGC=SGC, BINC=BINC, MUPC=MUPC,
                  CWC=CWC, CBC=CBC, NWC=NWC, BOC=BOC, ALC=ALC, DTBC=DTBC,
                  DROW=DROW, IDN=IDN, MASKC=MASKC, CS8=CS8, CS2=CS2,
                  ONES1=ONES1, Z32=Z32)

    in_maps = []
    for c in range(NCORE):
        b, k = c // 2, c % 2
        m = dict(shared)
        m["XM"] = np.ascontiguousarray(hs[b, 1024 * k:1024 * (k + 1), :].T)
        m["XP"] = np.ascontiguousarray(hs[b, 0:1024, :].T)
        m["MSK"] = np.full((128, 1), float(k), dtype=np.float32)
        in_maps.append(m)

    nc = bass.Bass("TRN2", target_bir_lowering=False)
    build_program(nc)
    res = run_bass_kernel_spmd(nc, in_maps, core_ids=list(range(NCORE)))

    out = np.empty((Bsz, Lseq, D), dtype=np.float32)
    for c in range(NCORE):
        b, k = c // 2, c % 2
        out[b, 1024 * k:1024 * (k + 1), :] = np.asarray(res.results[c]["OUT"]).T
    return out


if __name__ == "__main__":
    import reference
    ins = reference.setup_inputs()
    ins = {k: np.asarray(v) for k, v in ins.items()}
    got = kernel(**ins)
    want = np.asarray(reference.reference(**ins))
    err = np.linalg.norm(got - want) / np.linalg.norm(want)
    print("rel_err:", err)
